# version 01
# baseline (speedup 1.0000x reference)
import os
import numpy as np
import concourse.bass as bass
import concourse.mybir as mybir
from concourse.bass_utils import run_bass_kernel_spmd

F32 = mybir.dt.float32
BF16 = mybir.dt.bfloat16
AF = mybir.ActivationFunctionType
ALU = mybir.AluOpType
AX = mybir.AxisListType

P = 128
B, L, D, DI, NST = 2, 8192, 1024, 2048, 16
NCH = 8
Q = L // NCH
DSL = DI // 4
TSL = L // 4
PQ = 256
NPC = TSL // PQ
KD = D // P
RG = [[0, 1, 2, 3], [4, 5, 6, 7]]


class _Op:
    __slots__ = ("eng", "fn", "deps", "sem", "inc", "count_after")


class Sched:
    def __init__(self, nc, semnames):
        self.nc = nc
        self.items = []
        self.counts = {n: 0 for n in semnames}
        self.sems = {n: nc._sched_sems[n] for n in semnames}

    def op(self, eng, fn, inc=None, wait=(), force_wait=False):
        o = _Op()
        o.eng, o.fn = eng, fn
        o.deps = []
        for w in wait:
            if w is None:
                continue
            need = force_wait or w.eng != eng or w.sem in ("ld", "st", "cc")
            o.deps.append((w.sem, w.count_after, need))
        if os.environ.get("KSERIAL", "1") != "0":
            o.deps = [(s, c, True) for s, c in self.counts.items() if c > 0]
        o.sem, o.inc = None, 0
        if inc is not None:
            s, n = inc
            self.counts[s] += n
            o.sem, o.inc, o.count_after = s, n, self.counts[s]
        self.items.append(o)
        return o

    def emit(self, block):
        waited = {e: {} for e in ("sync", "scalar", "tensor", "vector", "gpsimd")}

        def mk(engname):
            def run(eng):
                w8 = waited[engname]
                for o in self.items:
                    if o.eng != engname:
                        continue
                    for (s, cnt, needed) in o.deps:
                        if needed and w8.get(s, 0) < cnt:
                            eng.wait_ge(self.sems[s], cnt)
                            w8[s] = cnt
                    ins = o.fn(eng)
                    if o.sem is not None:
                        ins.then_inc(self.sems[o.sem], o.inc)
            return run

        block.sync(mk("sync"))
        block.scalar(mk("scalar"))
        block.tensor(mk("tensor"))
        block.vector(mk("vector"))
        block.gpsimd(mk("gpsimd"))


def build():
    nc = bass.Bass("TRN2", target_bir_lowering=False, detect_race_conditions=False)
    E = {}

    def inp(name, shape):
        E[name] = nc.dram_tensor(name, list(shape), F32, kind="ExternalInput")
        return E[name]

    for nm, shp in [
            ("xT", [D, L]), ("wxT", [D, DSL]), ("wzT", [D, DI]),
            ("cw_st", [P, 16]), ("cb_st", [P, 4]), ("xpT_st", [P, 132]),
            ("dtw_st", [P, 4]), ("dtb_st", [P, 4]), ("Alog_st", [P, 64]),
            ("D_st", [P, 4]), ("opT", [DI, D]), ("lng_st", [P, 8]),
            ("lnb_st", [P, 8]), ("w1t", [D, 512]), ("w1b", [D, 512]),
            ("b1_st", [P, 4]), ("w2", [512, 64]), ("b2_st", [64, 1]),
            ("tau", [1, 1]), ("wihT", [D, 3072]), ("whhT", [D, 3072]),
            ("bih_st", [P, 24]), ("bhh_st", [P, 24]), ("wq", [D, 256]),
            ("bq_st", [P, 2]), ("wk", [D, 256]), ("bk_st", [P, 2]),
            ("wv", [D, D]), ("bv_row", [1, D]), ("fflng", [1, D]),
            ("fflnb", [1, D]), ("wvp", [D, D]), ("bvp_st", [P, 8]),
            ("wo", [D, D]), ("bo_st", [P, 8]), ("s0T", [D, 64]),
            ("xTo", [D, TSL])]:
        inp(nm, shp)
    E["sown"] = nc.dram_tensor("sown", [1, 1], mybir.dt.int32, kind="ExternalInput")

    out_h = nc.dram_tensor("out", [TSL, D], F32, kind="ExternalOutput")
    cc_ar_in = nc.dram_tensor("cc_ar_in", [NCH * 33, Q], F32)
    cc_ar_out = nc.dram_tensor("cc_ar_out", [NCH * 33, Q], F32)
    cc_ag_in = nc.dram_tensor("cc_ag_in", [NCH * DSL, Q], F32)
    cc_ag_out = nc.dram_tensor("cc_ag_out", [NCH * DI, Q], F32)
    x2_scr = nc.dram_tensor("x2_scr", [D, TSL], F32)
    cc_sl_in = nc.dram_tensor("cc_sl_in", [64, 1056], F32)
    cc_sl_out = nc.dram_tensor("cc_sl_out", [64, 1056], F32)
    KDBG = bool(os.environ.get("KDBG"))
    if KDBG:
        dbg_ar = nc.dram_tensor("dbg_ar", [NCH * 33, Q], F32, kind="ExternalOutput")
        dbg_ag = nc.dram_tensor("dbg_ag", [NCH * DI, Q], F32, kind="ExternalOutput")
        dbg_x2 = nc.dram_tensor("dbg_x2", [D, TSL], F32, kind="ExternalOutput")
        dbg_sl = nc.dram_tensor("dbg_sl", [64, 1056], F32, kind="ExternalOutput")

    SB = {}

    def sbp(name, shape, dt=F32):
        SB[name] = nc.alloc_sbuf_tensor(name, list(shape), dt)
        return SB[name]

    sbp("ident", [P, P])
    sbp("identb", [P, P], BF16)
    sbp("onesr", [1, P])
    sbp("onec", [P, 1])
    sbp("sel16", [16, 16 * P])
    sbp("fc", [P, 8])
    FC_EPS, FC_NHALF, FC_SC, FC_TENTH, FC_E8 = 0, 1, 2, 3, 4
    sbp("sdiag", [P, 16 * P], BF16)
    sbp("scw", [P, 16])
    sbp("scb", [P, 4])
    sbp("sxp", [P, 132])
    sbp("sdtw", [P, 4])
    sbp("sdtb", [P, 4])
    sbp("sA", [P, 64])
    sbp("sD", [P, 4])
    sbp("slng", [P, 8])
    sbp("slnb", [P, 8])
    sbp("scarry", [P, 64])
    sbp("salfb", [P, 16 * 64], BF16)
    sbp("swx", [P, 8 * 512], BF16)
    sbp("sw1t", [P, 8 * 512], BF16)
    sbp("sw2b", [P, 4 * 64], BF16)
    sbp("sbt1", [P, 4])
    sbp("sitau", [64, 2])
    sbp("sb2", [64, 1])
    sbp("sbihh", [P, 24])
    sbp("sbhh", [P, 24])
    sbp("sbq", [P, 2])
    sbp("sbk", [P, 2])
    sbp("sbvp", [P, 8])
    sbp("sbo", [P, 8])
    sbp("ssmean", [P, 8])
    sbp("sstg", [P, 3072])
    sbp("ssl", [64, 1056])
    SB["sown"] = nc.alloc_sbuf_tensor("sown_sb", [1, 1], mybir.dt.int32)

    RB = 150 * 1024
    r0, _ = nc.bump_sbuf(RB)

    def mkregion():
        ptr = [r0]

        def f(name, shape, dt=F32):
            nbytes = int(np.prod(shape[1:])) * (2 if dt == BF16 else 4)
            off = (ptr[0] + 31) // 32 * 32
            ptr[0] = off + nbytes
            assert ptr[0] <= r0 + RB, f"{name}: reuse overflow {ptr[0] - r0}"
            SB[name] = nc.alloc_sbuf_tensor_at(name, list(shape), dt, offset=off)
            return SB[name]
        return f

    sb1 = mkregion()
    sb1("sxT", [P, 2 * Q])
    sb1("sxq", [P, 2 * Q])
    sb1("sxnb", [P, 8 * Q], BF16)
    sb1("sxin", [P, 4 * (Q + 4)], BF16)
    sb1("sxc", [P, 4 * Q])
    sb1("sdt", [P, 4 * Q])
    sb1("sys", [P, 4 * Q])
    sb1("sBbc", [P, Q])
    sb1("sCbc", [P, Q])
    sb1("sdA", [P, Q])
    sb1("sdBx", [P, Q])
    sb1("sh", [P, Q])
    sb1("stmp", [P, Q])
    sb1("sx0r", [1, Q])
    sb1("sBr", [16, Q])
    sb1("sCr", [16, Q])
    sb1("sxde", [33, Q])
    sb1("sstat", [1, 2 * Q])
    sb1("svar", [1, Q])
    sb1("ssinv", [1, Q])

    sb2 = mkregion()
    sb2("swzT", [P, 8 * DI], BF16)
    sb2("sopT", [P, 16 * D], BF16)
    sb2("swp", [P, 16 * PQ])
    sb2("ssz", [P, 16 * PQ])
    sb2("sytb", [P, 16 * PQ], BF16)
    sb2("soat", [P, 8 * PQ])
    sb2("sxo", [P, 8 * PQ])
    sb2("sxn8", [P, 8 * PQ])
    sb2("sxnb2", [P, 8 * PQ], BF16)
    sb2("sxn2b", [P, 8 * PQ], BF16)
    sb2("sxn2t", [P, 2 * 1056], BF16)
    sb2("sh1b", [P, 4 * PQ], BF16)
    sb2("srt", [64, PQ])
    sb2("srtt", [P, 2 * 64])
    sb2("st8", [P, 8])
    sb2("stm1", [P, 1])
    sb2("sal", [P, 64])
    sb2("sals", [P, 1])
    sb2("sq2", [P, 2 * PQ])
    sb2("sstat2", [1, 2 * PQ])
    sb2("svar2", [1, PQ])
    sb2("ssinv2", [1, PQ])

    sb3 = mkregion()
    sb3("swkb", [P, 3072], BF16)
    sb3("sw3", [P, 8 * 1024], BF16)
    sb3("ssiT", [P, 8 * 64], BF16)
    sb3("ss0f", [P, 8 * 64])
    sb3("ss0b", [P, 8 * 64], BF16)
    sb3("sgr", [P, 8 * 64])
    sb3("sgz", [P, 8 * 64])
    sb3("sgn", [P, 8 * 64])
    sb3("sgt", [P, 8 * 64])
    sb3("ssnew", [P, 8 * 64])
    sb3("ssnb", [P, 8 * 64], BF16)
    sb3("sqkT", [P, 4 * 64], BF16)
    sb3("satm", [64, 64])
    sb3("satb", [64, 64], BF16)
    sb3("satT", [64, 64], BF16)
    sb3("svtok", [64, D])
    sb3("svtkb", [64, D], BF16)
    sb3("sy3", [64, D])
    sb3("sy3q", [64, D])
    sb3("sy3Tb", [P, 8 * 64], BF16)
    sb3("ssvb", [64, 8 * P], BF16)
    sb3("salfT", [64, 16 * P], BF16)
    sb3("ssc3", [64, 8])
    sb3("sbvb", [64, D])
    sb3("sgbb", [64, 2 * D])
    sb3("sx2p", [P, 8 * PQ])
    sb3("soutb", [P, 8 * PQ], BF16)
    sb3("sfin", [P, 8 * PQ])
    sb3("sfint", [P, D])

    PA = nc.alloc_psum_tensor("PA", [P, 1024], F32)
    PB = nc.alloc_psum_tensor("PB", [P, 1024], F32)
    PC = nc.alloc_psum_tensor("PC", [P, 1024], F32)
    PD = nc.alloc_psum_tensor("PD", [P, 1024], F32)

    semnames = ["ld", "tA", "vA", "sA", "gA", "st", "cc"]
    nc._sched_sems = {}
    nc._sem_guards = []
    for n_ in semnames:
        g = nc.semaphore(n_)
        nc._sched_sems[n_] = g.__enter__()
        nc._sem_guards.append(g)

    S = Sched(nc, semnames)
    T = SB

    def dma(q, dst, src, wait=(), sem="ld"):
        return S.op(q, lambda e, d=dst, s=src: e.dma_start(out=d, in_=s),
                    inc=(sem, 16), wait=wait)

    def vop(fn, wait=(), force=False):
        return S.op("vector", fn, inc=("vA", 1), wait=wait, force_wait=force)

    def sop(fn, wait=(), force=False):
        return S.op("scalar", fn, inc=("sA", 1), wait=wait, force_wait=force)

    def top(fn, wait=(), force=False):
        return S.op("tensor", fn, inc=("tA", 1), wait=wait, force_wait=force)

    def gop(fn, wait=(), force=False):
        return S.op("gpsimd", fn, inc=("gA", 1), wait=wait, force_wait=force)

    SL = {}
    for nm, dst in [("cw_st", "scw"), ("cb_st", "scb"), ("xpT_st", "sxp"),
                    ("dtw_st", "sdtw"), ("dtb_st", "sdtb"), ("Alog_st", "sA"),
                    ("D_st", "sD"), ("lng_st", "slng"), ("lnb_st", "slnb"),
                    ("b2_st", "sb2"), ("bq_st", "sbq"), ("bk_st", "sbk"),
                    ("bvp_st", "sbvp"), ("bo_st", "sbo"), ("bih_st", "sbihh"),
                    ("bhh_st", "sbhh"), ("b1_st", "sbt1")]:
        SL[dst] = dma("sync", T[dst][:], E[nm].ap())
    ld_tau = dma("sync", T["fc"][0:1, 6:7], E["tau"].ap())

    g0 = gop(lambda e: e.memset(T["ident"][:], 0.0))
    g1 = gop(lambda e: e.affine_select(out=T["ident"][:], in_=T["ident"][:],
                                       compare_op=ALU.not_equal, fill=1.0, base=0,
                                       pattern=[[-1, P]], channel_multiplier=1),
             wait=[g0], force=True)
    gop(lambda e: e.memset(T["onesr"][:], 1.0))
    gop(lambda e: e.memset(T["onec"][:], 1.0))
    gs0 = gop(lambda e: e.memset(T["sel16"][:], 0.0))
    gop(lambda e: e.affine_select(
        out=T["sel16"][:].rearrange("c (b j) -> c b j", b=16),
        in_=T["sel16"][:].rearrange("c (b j) -> c b j", b=16),
        compare_op=ALU.not_equal, fill=1.0, base=0,
        pattern=[[-1, 16], [0, P]], channel_multiplier=1),
        wait=[gs0], force=True)
    for col, val in [(FC_EPS, 1e-5), (FC_NHALF, -0.5), (FC_SC, 256.0 ** -0.5),
                     (FC_TENTH, 0.1), (FC_E8, 1e-8)]:
        gop(lambda e, cc=col, v=val: e.memset(T["fc"][:, cc:cc + 1], v))
    g_all = S.op("gpsimd", lambda e: e.engine_nop(), inc=("gA", 1))

    v_idb = vop(lambda e: e.tensor_copy(T["identb"][:], T["ident"][:]), wait=[g1])
    s_negA = sop(lambda e: e.activation(out=T["sA"][:], in_=T["sA"][:], func=AF.Exp),
                 wait=[SL["sA"], g_all])
    v_negA = vop(lambda e: e.tensor_scalar_mul(T["sA"][:], T["sA"][:], -1.0),
                 wait=[s_negA])
    diag_ops = []
    for j in range(16):
        diag_ops.append(vop(lambda e, j=j: e.tensor_scalar_mul(
            T["sdiag"][:, j * P:(j + 1) * P], T["ident"][:], T["scw"][:, j:j + 1]),
            wait=[g1, SL["scw"]]))
    v_bih = vop(lambda e: e.tensor_add(T["sbihh"][:, 0:16], T["sbihh"][:, 0:16],
                                       T["sbhh"][:, 0:16]), wait=[SL["sbihh"], SL["sbhh"]])
    s_ab = sop(lambda e: e.activation(out=T["fc"][0:1, 7:8], in_=T["fc"][0:1, 6:7],
                                      func=AF.Abs), wait=[ld_tau, g_all])
    v_ab = vop(lambda e: e.tensor_add(T["fc"][0:1, 7:8], T["fc"][0:1, 7:8],
                                      T["fc"][0:1, FC_TENTH:FC_TENTH + 1]), wait=[s_ab, g_all])
    v_rc = vop(lambda e: e.reciprocal(T["fc"][0:1, 7:8], T["fc"][0:1, 7:8]),
               wait=[v_ab], force=True)
    t_it = top(lambda e: e.matmul(PA[0:64, 0:1], T["onesr"][0:1, 0:64],
                                  T["fc"][0:1, 7:8], start=True, stop=True),
               wait=[v_rc, g_all])
    v_it = vop(lambda e: e.tensor_copy(T["sitau"][:, 0:1], PA[0:64, 0:1]), wait=[t_it])
    v_it2 = vop(lambda e: e.tensor_mul(T["sitau"][:, 1:2], T["sb2"][:],
                                       T["sitau"][:, 0:1]),
                wait=[v_it, SL["sb2"]], force=True)

    def stream_cast(dram_ap, dst_name, kt, cols, prev=None, dst_coloff=0):
        ops = []
        lastop = prev
        for k in range(kt):
            ldk = dma("sync", T["sstg"][:, 0:cols], dram_ap[k * P:(k + 1) * P, :],
                      wait=[lastop] if lastop else [])
            ck = vop(lambda e, k=k: e.tensor_copy(
                T[dst_name][:, dst_coloff + k * cols:dst_coloff + (k + 1) * cols],
                T["sstg"][:, 0:cols]), wait=[ldk])
            ops.append(ck)
            lastop = ck
        return ops

    wx_ops = stream_cast(E["wxT"].ap(), "swx", 8, 512)
    w1_ops = stream_cast(E["w1t"].ap(), "sw1t", 8, 512, prev=wx_ops[-1])
    w2_ops = stream_cast(E["w2"].ap(), "sw2b", 4, 64, prev=w1_ops[-1])

    prevc = w2_ops[-1]
    sm_ops = []
    for k in range(KD):
        ldk = dma("sync", T["sstg"][:, 0:64], E["s0T"].ap()[k * P:(k + 1) * P, :],
                  wait=[prevc])
        vr = vop(lambda e, k=k: e.reduce_sum(T["ssmean"][:, k:k + 1],
                                             T["sstg"][:, 0:64], axis=AX.X), wait=[ldk])
        sm_ops.append(vr)
        prevc = vr
    v_sm = vop(lambda e: e.tensor_scalar_mul(T["ssmean"][:], T["ssmean"][:], 1.0 / 64.0),
               wait=[prevc], force=True)
    c1_last = []
    prevc = v_sm
    for k in range(KD):
        ldk = dma("sync", T["sstg"][:, 0:512], E["w1b"].ap()[k * P:(k + 1) * P, :],
                  wait=[prevc])
        for mt in range(4):
            mm = top(lambda e, k=k, m=mt: e.matmul(
                PB[0:P, m:m + 1], T["sstg"][:, m * 128:(m + 1) * 128],
                T["ssmean"][:, k:k + 1], start=(k == 0), stop=(k == KD - 1)),
                wait=[ldk, v_sm])
            if k == KD - 1:
                c1_last.append(mm)
        prevc = mm
    v_bt1 = vop(lambda e: e.tensor_add(T["sbt1"][:], T["sbt1"][:], PB[0:P, 0:4]),
                wait=c1_last + [SL["sbt1"]])

    prev_xn_last = None
    prev_sq = None
    prev_scan_end = None
    prev_dt_last = None
    prev_wds = []
    ag_ops = []
    for c in range(NCH):
        C0 = c * Q
        mu_mms, sq_mms, tile_lds = [], [], []
        for k in range(KD):
            half = (k % 2) * Q
            wl = [wx_ops[-1], v_bt1]
            if len(mu_mms) >= 2:
                wl += [mu_mms[-2], sq_mms[-2]]
            if k < 2 and prev_xn_last is not None:
                wl.append(prev_xn_last)
            ldk = dma("sync", T["sxT"][:, half:half + Q],
                      E["xT"].ap()[k * P:(k + 1) * P, C0:C0 + Q], wait=wl)
            tile_lds.append(ldk)
            sqk = sop(lambda e, h=half: e.activation(
                out=T["sxq"][:, h:h + Q], in_=T["sxT"][:, h:h + Q], func=AF.Square),
                wait=[ldk] + ([prev_sq] if prev_sq else []), force=True)
            prev_sq = sqk
            for hf in range(2):
                wl2 = [ldk, g_all]
                if c > 0 and k == 0:
                    wl2 += [prev_scan_end, prev_dt_last]
                mu_mms.append(top(lambda e, h=half, f=hf, kk=k: e.matmul(
                    PA[0:1, f * 512:(f + 1) * 512], T["onec"][:],
                    T["sxT"][:, h + f * 512:h + (f + 1) * 512],
                    start=(kk == 0), stop=(kk == KD - 1)), wait=wl2))
                sq_mms.append(top(lambda e, h=half, f=hf, kk=k: e.matmul(
                    PB[0:1, f * 512:(f + 1) * 512], T["onec"][:],
                    T["sxq"][:, h + f * 512:h + (f + 1) * 512],
                    start=(kk == 0), stop=(kk == KD - 1)), wait=[sqk]))
        v_mu = vop(lambda e: e.tensor_scalar_mul(T["sstat"][0:1, 0:Q], PA[0:1, 0:Q],
                                                 1.0 / D), wait=mu_mms[-2:])
        v_e2 = vop(lambda e: e.tensor_scalar_mul(T["sstat"][0:1, Q:2 * Q], PB[0:1, 0:Q],
                                                 1.0 / D), wait=sq_mms[-2:])
        v_va = vop(lambda e: e.tensor_mul(T["svar"][0:1, :], T["sstat"][0:1, 0:Q],
                                          T["sstat"][0:1, 0:Q]), wait=[v_mu])
        v_vb = vop(lambda e: e.tensor_sub(T["svar"][0:1, :], T["sstat"][0:1, Q:2 * Q],
                                          T["svar"][0:1, :]), wait=[v_va, v_e2], force=True)
        s_ln = sop(lambda e: e.activation(out=T["svar"][0:1, :], in_=T["svar"][0:1, :],
                                          func=AF.Ln, bias=T["fc"][0:1, FC_EPS:FC_EPS + 1]),
                   wait=[v_vb])
        s_si = sop(lambda e: e.activation(out=T["ssinv"][0:1, :], in_=T["svar"][0:1, :],
                                          func=AF.Exp,
                                          scale=T["fc"][0:1, FC_NHALF:FC_NHALF + 1]),
                   wait=[s_ln], force=True)
        bc_mu, bc_si = [], []
        for hf in range(2):
            bc_mu.append(top(lambda e, f=hf: e.matmul(
                PC[0:P, f * 512:(f + 1) * 512], T["onesr"][:],
                T["sstat"][0:1, f * 512:(f + 1) * 512], start=True, stop=True),
                wait=[v_mu, v_vb, prev_scan_end]))
            bc_si.append(top(lambda e, f=hf: e.matmul(
                PD[0:P, f * 512:(f + 1) * 512], T["onesr"][:],
                T["ssinv"][0:1, f * 512:(f + 1) * 512], start=True, stop=True),
                wait=[s_si]))
        xn_ops = []
        for k in range(KD):
            half = (k % 2) * Q
            wl = [mu_mms[-1], sq_mms[-1]]
            if len(xn_ops) >= 2:
                wl.append(xn_ops[-2])
            ldk = dma("sync", T["sxT"][:, half:half + Q],
                      E["xT"].ap()[k * P:(k + 1) * P, C0:C0 + Q], wait=wl)
            v1 = vop(lambda e, h=half: e.tensor_sub(
                T["sxT"][:, h:h + Q], T["sxT"][:, h:h + Q], PC[0:P, 0:Q]),
                wait=[ldk] + bc_mu)
            v2 = vop(lambda e, h=half, k=k: e.tensor_tensor(
                out=T["sxnb"][:, k * Q:(k + 1) * Q], in0=T["sxT"][:, h:h + Q],
                in1=PD[0:P, 0:Q], op=ALU.mult), wait=[v1] + bc_si, force=True)
            xn_ops.append(v2)
        prev_xn_last = xn_ops[-1]
        halo_ops = []
        for dt_ in range(4):
            base = dt_ * (Q + 4)
            if c == 0:
                halo_ops.append(vop(lambda e, b=base: e.memset(T["sxin"][:, b:b + 4], 0.0)))
            else:
                halo_ops.append(vop(lambda e, b=base: e.tensor_copy(
                    T["sxin"][:, b:b + 4], T["sxin"][:, b + Q:b + Q + 4])))
        xi_evac = []
        for mt in range(4):
            pm = [PA, PB][mt % 2]
            mml = []
            for hf in range(2):
                for k in range(KD):
                    wl = [xn_ops[k], wx_ops[-1]]
                    if mt < 2:
                        wl.append([v_mu, v_e2][mt])
                    if len(xi_evac) >= 2:
                        wl.append(xi_evac[-2])
                    mm = top(lambda e, m=mt, f=hf, k=k, p=pm: e.matmul(
                        p[0:P, f * 512:(f + 1) * 512],
                        T["swx"][:, k * 512 + m * 128:k * 512 + (m + 1) * 128],
                        T["sxnb"][:, k * Q + f * 512:k * Q + (f + 1) * 512],
                        start=(k == 0), stop=(k == KD - 1)), wait=wl)
                mml.append(mm)
            ev = vop(lambda e, m=mt, p=pm: e.tensor_copy(
                T["sxin"][:, m * (Q + 4) + 4:(m + 1) * (Q + 4)], p[0:P, 0:Q]),
                wait=mml + halo_ops[mt:mt + 1])
            xi_evac.append(ev)
        xc_ops = []
        for dt_ in range(4):
            pm = [PC, PD][dt_ % 2]
            base = dt_ * (Q + 4)
            mml = []
            for hf in range(2):
                for k_ in range(4):
                    wl = [xi_evac[dt_], v_idb]
                    if dt_ < 2:
                        wl.append(xn_ops[-1])
                    if len(xc_ops) >= 2:
                        wl.append(xc_ops[-2])
                    mm = top(lambda e, d=dt_, f=hf, k=k_, p=pm, b=base: e.matmul(
                        p[0:P, f * 512:(f + 1) * 512],
                        T["sdiag"][:, (d * 4 + k) * P:(d * 4 + k + 1) * P],
                        T["sxin"][:, b + 1 + k + f * 512:b + 1 + k + f * 512 + 512],
                        start=(k == 0), stop=(k == 3)), wait=wl)
                mml.append(mm)
            sv = sop(lambda e, d=dt_, p=pm: e.activation(
                out=T["sxc"][:, d * Q:(d + 1) * Q], in_=p[0:P, 0:Q], func=AF.Silu,
                bias=T["scb"][:, d:d + 1]), wait=mml + [SL["scb"]])
            xc_ops.append(sv)
        xp_mm = []
        for hf in range(2):
            for dt_ in range(4):
                mm = top(lambda e, d=dt_, f=hf: e.matmul(
                    PA[0:33, f * 512:(f + 1) * 512],
                    T["sxp"][:, d * 33:(d + 1) * 33],
                    T["sxc"][:, d * Q + f * 512:d * Q + (f + 1) * 512],
                    start=(d == 0), stop=(d == 3)),
                    wait=[xc_ops[dt_], SL["sxp"], xi_evac[0], xi_evac[2], v_mu])
            xp_mm.append(mm)
        v_xd = vop(lambda e: e.tensor_copy(T["sxde"][:], PA[0:33, 0:Q]), wait=xp_mm)
        d_xd = dma("sync", cc_ar_in.ap()[c * 33:(c + 1) * 33, :], T["sxde"][:],
                   wait=[v_xd], sem="st")
        g_ar = S.op("gpsimd", lambda e, cc_=c: e.collective_compute(
            "AllReduce", ALU.add, replica_groups=RG,
            ins=[cc_ar_in.ap()[cc_ * 33:(cc_ + 1) * 33, :]],
            outs=[cc_ar_out.ap()[cc_ * 33:(cc_ + 1) * 33, :]]),
            inc=("cc", 1), wait=[d_xd])
        d_x0 = dma("sync", T["sx0r"][:], cc_ar_out.ap()[c * 33:c * 33 + 1, :], wait=[g_ar])
        d_Br = dma("sync", T["sBr"][:], cc_ar_out.ap()[c * 33 + 1:c * 33 + 17, :], wait=[g_ar])
        d_Cr = dma("sync", T["sCr"][:], cc_ar_out.ap()[c * 33 + 17:c * 33 + 33, :], wait=[g_ar])
        t_x0 = []
        for hf in range(2):
            t_x0.append(top(lambda e, f=hf: e.matmul(
                PB[0:P, f * 512:(f + 1) * 512], T["onesr"][:],
                T["sx0r"][0:1, f * 512:(f + 1) * 512], start=True, stop=True),
                wait=[d_x0, v_e2, xi_evac[3]]))
        dt_ops = []
        for dt_ in range(4):
            s1 = sop(lambda e, d=dt_: e.activation(
                out=T["sdt"][:, d * Q:(d + 1) * Q], in_=PB[0:P, 0:Q], func=AF.Exp,
                scale=T["sdtw"][:, d:d + 1], bias=T["sdtb"][:, d:d + 1]),
                wait=t_x0 + [SL["sdtw"], SL["sdtb"]])
            s2 = sop(lambda e, d=dt_: e.activation(
                out=T["sdt"][:, d * Q:(d + 1) * Q], in_=T["sdt"][:, d * Q:(d + 1) * Q],
                func=AF.Ln, bias=1.0), wait=[s1], force=True)
            dt_ops.append(s2)
        prev_dt_last = dt_ops[-1]
        ys_init, udt_ops = [], []
        for dt_ in range(4):
            wl = [xc_ops[dt_], xp_mm[-1], SL["sD"]]
            if c > 0:
                wl += prev_wds[dt_:dt_ + 1]
            yi = vop(lambda e, d=dt_: e.tensor_scalar_mul(
                T["sys"][:, d * Q:(d + 1) * Q], T["sxc"][:, d * Q:(d + 1) * Q],
                T["sD"][:, d:d + 1]), wait=wl)
            ys_init.append(yi)
            ud = vop(lambda e, d=dt_: e.tensor_tensor(
                out=T["sxc"][:, d * Q:(d + 1) * Q], in0=T["sxc"][:, d * Q:(d + 1) * Q],
                in1=T["sdt"][:, d * Q:(d + 1) * Q], op=ALU.mult),
                wait=[yi, dt_ops[dt_]], force=True)
            udt_ops.append(ud)
        last_accum = [None] * 4
        for n_ in range(16):
            wbc = [d_Br, v_xd]
            if n_ == 0:
                wbc += [v_mu]
            tb1 = top(lambda e, n=n_: e.matmul(
                PA[0:P, 0:512], T["sel16"][:, n * P:(n + 1) * P],
                T["sBr"][:, 0:512], start=True, stop=True), wait=wbc)
            tb2 = top(lambda e, n=n_: e.matmul(
                PA[0:P, 512:1024], T["sel16"][:, n * P:(n + 1) * P],
                T["sBr"][:, 512:1024], start=True, stop=True), wait=[d_Br])
            tc1 = top(lambda e, n=n_: e.matmul(
                PC[0:P, 0:512], T["sel16"][:, n * P:(n + 1) * P],
                T["sCr"][:, 0:512], start=True, stop=True),
                wait=[d_Cr, xc_ops[2], xc_ops[3]])
            tc2 = top(lambda e, n=n_: e.matmul(
                PC[0:P, 512:1024], T["sel16"][:, n * P:(n + 1) * P],
                T["sCr"][:, 512:1024], start=True, stop=True), wait=[d_Cr])
            vb = vop(lambda e: e.tensor_copy(T["sBbc"][:], PA[0:P, 0:Q]), wait=[tb1, tb2])
            vc = vop(lambda e: e.tensor_copy(T["sCbc"][:], PC[0:P, 0:Q]), wait=[tc1, tc2])
            for dt_ in range(4):
                cidx = dt_ * 16 + n_
                sda = sop(lambda e, d=dt_, n=n_: e.activation(
                    out=T["sdA"][:], in_=T["sdt"][:, d * Q:(d + 1) * Q], func=AF.Exp,
                    scale=T["sA"][:, d * 16 + n:d * 16 + n + 1]),
                    wait=[dt_ops[dt_], v_negA])
                vdx = vop(lambda e, d=dt_: e.tensor_tensor(
                    out=T["sdBx"][:], in0=T["sxc"][:, d * Q:(d + 1) * Q],
                    in1=T["sBbc"][:], op=ALU.mult), wait=[udt_ops[dt_], vb], force=True)
                init = 0.0 if c == 0 else T["scarry"][:, cidx:cidx + 1]
                vsc = vop(lambda e, i=init: e.tensor_tensor_scan(
                    out=T["sh"][:], data0=T["sdA"][:], data1=T["sdBx"][:],
                    initial=i, op0=ALU.mult, op1=ALU.add),
                    wait=[sda, vdx], force=True)
                vcp = vop(lambda e, ci=cidx: e.tensor_copy(
                    T["scarry"][:, ci:ci + 1], T["sh"][:, Q - 1:Q]),
                    wait=[vsc], force=True)
                vtm = vop(lambda e: e.tensor_mul(T["stmp"][:], T["sh"][:], T["sCbc"][:]),
                          wait=[vsc, vc], force=True)
                vac = vop(lambda e, d=dt_: e.tensor_add(
                    T["sys"][:, d * Q:(d + 1) * Q], T["sys"][:, d * Q:(d + 1) * Q],
                    T["stmp"][:]), wait=[vtm, ys_init[dt_]], force=True)
                last_accum[dt_] = vac
        prev_scan_end = last_accum[3]
        wds = []
        for dt_ in range(4):
            wds.append(dma("sync",
                           cc_ag_in.ap()[c * DSL + dt_ * P:c * DSL + (dt_ + 1) * P, :],
                           T["sys"][:, dt_ * Q:(dt_ + 1) * Q],
                           wait=[last_accum[dt_]], sem="st"))
        prev_wds = wds
        g_ag = S.op("gpsimd", lambda e, cc_=c: e.collective_compute(
            "AllGather", ALU.bypass, replica_groups=RG,
            ins=[cc_ag_in.ap()[cc_ * DSL:(cc_ + 1) * DSL, :]],
            outs=[cc_ag_out.ap()[cc_ * DI:(cc_ + 1) * DI, :]]),
            inc=("cc", 1), wait=wds)
        ag_ops.append(g_ag)

    _fix_sda_chain(S)

    p1_done = ag_ops[-1]

    p1v = prev_scan_end
    p1w = prev_wds[-1]
    p1s = prev_dt_last

    def stream_cast2(dram_ap, dst_name, kt, cols, prev, coloff=0, guard=()):
        ops = []
        lastop = prev
        for k in range(kt):
            ldk = dma("sync", T["sstg"][:, 0:cols], dram_ap[k * P:(k + 1) * P, :],
                      wait=[lastop])
            ck = vop(lambda e, k=k, c=cols, co=coloff: e.tensor_copy(
                T[dst_name][:, co + k * c:co + (k + 1) * c], T["sstg"][:, 0:c]),
                wait=[ldk] + (list(guard) if k == 0 else []))
            ops.append(ck)
            lastop = ck
        return ops

    guard2 = [p1v, p1w, p1s]
    wz_ops = stream_cast2(E["wzT"].ap(), "swzT", 8, DI, prev=prev_wds[-1], guard=guard2)
    op_ops = stream_cast2(E["opT"].ap(), "sopT", 16, D, prev=wz_ops[-1], guard=guard2)
    v_slz = vop(lambda e: e.memset(T["ssl"][:], 0.0), wait=[p1v])

    d_own = S.op("gpsimd", lambda e: e.dma_start(out=T["sown"][:], in_=E["sown"].ap()),
                 inc=("ld", 16))
    v_own = S.op("gpsimd",
                 lambda e: setattr(nc, "_own_val", e.value_load(T["sown"][0:1, 0:1])),
                 inc=None, wait=[d_own])

    slot_mm_all = []
    prev_piece = {}
    for p in range(NPC):
        pc = p * PQ
        wl = [ag_ops[-1], op_ops[-1]]
        if p > 0:
            wl.append(prev_piece["ytb"])
        tc = pc % Q
        cadd = pc // Q
        d_w1 = S.op("gpsimd", lambda e, t0=tc, ca=cadd: e.dma_start(
            out=T["swp"][:].rearrange("p (b t) -> p b t", b=16),
            in_=cc_ag_out.ap().rearrange("(c b p) t -> p b c t", p=P, b=16)
                [:, :, bass.ds(nc._own_val + ca, 1), t0:t0 + PQ].squeeze(2)),
            inc=("ld", 16), wait=wl)
        d_w = [d_w1] * 16
        d_x = []
        for k in range(KD):
            wl = [wz_ops[-1]]
            if p > 0:
                wl.append(prev_piece["x2d"])
            d_x.append(dma("sync", T["sxo"][:, k * PQ:(k + 1) * PQ],
                           E["xTo"].ap()[k * P:(k + 1) * P, pc:pc + PQ], wait=wl))

        def piece_ln(src_name, dst_f32, dst_b16, guard_v, guard_s, tag):
            mus, sqs = [], []
            prev_sq2 = guard_s
            for k in range(KD):
                half = (k % 2) * PQ
                sq = sop(lambda e, k=k, h=half, s=src_name: e.activation(
                    out=T["sq2"][:, h:h + PQ], in_=T[s][:, k * PQ:(k + 1) * PQ],
                    func=AF.Square), wait=guard_v + ([prev_sq2] if prev_sq2 else []),
                    force=True)
                prev_sq2 = sq
                mus.append(top(lambda e, k=k, s=src_name: e.matmul(
                    PA[0:1, 0:PQ], T["onec"][:], T[s][:, k * PQ:(k + 1) * PQ],
                    start=(k == 0), stop=(k == KD - 1)), wait=guard_v))
                sqs.append(top(lambda e, h=half, kk=k: e.matmul(
                    PB[0:1, 0:PQ], T["onec"][:], T["sq2"][:, h:h + PQ],
                    start=(kk == 0), stop=(kk == KD - 1)), wait=[sq]))
            vmu = vop(lambda e: e.tensor_scalar_mul(T["sstat2"][0:1, 0:PQ],
                                                    PA[0:1, 0:PQ], 1.0 / D),
                      wait=mus[-1:])
            ve2 = vop(lambda e: e.tensor_scalar_mul(T["sstat2"][0:1, PQ:2 * PQ],
                                                    PB[0:1, 0:PQ], 1.0 / D),
                      wait=sqs[-1:])
            vva = vop(lambda e: e.tensor_mul(T["svar2"][0:1, :], T["sstat2"][0:1, 0:PQ],
                                             T["sstat2"][0:1, 0:PQ]), wait=[vmu])
            vvb = vop(lambda e: e.tensor_sub(T["svar2"][0:1, :],
                                             T["sstat2"][0:1, PQ:2 * PQ],
                                             T["svar2"][0:1, :]),
                      wait=[vva, ve2], force=True)
            sln = sop(lambda e: e.activation(out=T["svar2"][0:1, :], in_=T["svar2"][0:1, :],
                                             func=AF.Ln,
                                             bias=T["fc"][0:1, FC_EPS:FC_EPS + 1]),
                      wait=[vvb])
            ssi = sop(lambda e: e.activation(out=T["ssinv2"][0:1, :], in_=T["svar2"][0:1, :],
                                             func=AF.Exp,
                                             scale=T["fc"][0:1, FC_NHALF:FC_NHALF + 1]),
                      wait=[sln], force=True)
            bmu = top(lambda e: e.matmul(PA[0:P, 0:PQ], T["onesr"][:],
                                         T["sstat2"][0:1, 0:PQ], start=True, stop=True),
                      wait=[vmu, vvb])
            bsi = top(lambda e: e.matmul(PB[0:P, 0:PQ], T["onesr"][:],
                                         T["ssinv2"][0:1, 0:PQ], start=True, stop=True),
                      wait=[ssi, ve2])
            outs = []
            for k in range(KD):
                v1 = vop(lambda e, k=k, s=src_name, d=dst_f32: e.tensor_sub(
                    T[d][:, k * PQ:(k + 1) * PQ], T[s][:, k * PQ:(k + 1) * PQ],
                    PA[0:P, 0:PQ]), wait=[bmu])
                if dst_b16 is None:
                    v2 = vop(lambda e, k=k, d=dst_f32: e.tensor_tensor(
                        out=T[d][:, k * PQ:(k + 1) * PQ],
                        in0=T[d][:, k * PQ:(k + 1) * PQ], in1=PB[0:P, 0:PQ],
                        op=ALU.mult), wait=[v1, bsi], force=True)
                else:
                    v2 = vop(lambda e, k=k, d=dst_f32: e.tensor_tensor(
                        out=T[d][:, k * PQ:(k + 1) * PQ],
                        in0=T[d][:, k * PQ:(k + 1) * PQ], in1=PB[0:P, 0:PQ],
                        op=ALU.mult), wait=[v1, bsi], force=True)
                    v2 = vop(lambda e, k=k, d=dst_f32, b=dst_b16: e.tensor_copy(
                        T[b][:, k * PQ:(k + 1) * PQ], T[d][:, k * PQ:(k + 1) * PQ]),
                        wait=[v2], force=True)
                outs.append(v2)
            return outs, (vmu, ve2, bmu, bsi, sqs[-1], mus[-1])

        guard_v = [dd for dd in d_x]
        if p > 0:
            guard_v += [prev_piece["vrt"], prev_piece["slot"]]
        gs = prev_piece.get("lastsq")
        xn_o, st1 = piece_ln("sxo", "sxn8", "sxnb2", guard_v, gs, "ln1")
        z_ops = []
        for mi in range(16):
            pm = [PC, PD][mi % 2]
            mml = []
            for k in range(KD):
                wl = [xn_o[k], wz_ops[-1]]
                if len(z_ops) >= 2:
                    wl.append(z_ops[-2])
                if mi < 2:
                    wl += [st1[2], st1[3]]
                mm = top(lambda e, m=mi, k=k, p=pm: e.matmul(
                    p[0:P, 0:PQ], T["swzT"][:, k * DI + m * 128:k * DI + (m + 1) * 128],
                    T["sxnb2"][:, k * PQ:(k + 1) * PQ],
                    start=(k == 0), stop=(k == KD - 1)), wait=wl)
                mml.append(mm)
            sz_ = sop(lambda e, m=mi, p=pm: e.activation(
                out=T["ssz"][:, m * PQ:(m + 1) * PQ], in_=p[0:P, 0:PQ], func=AF.Silu),
                wait=mml)
            z_ops.append(sz_)
        yt_ops = []
        for mi in range(16):
            yt_ops.append(vop(lambda e, m=mi: e.tensor_tensor(
                out=T["sytb"][:, m * PQ:(m + 1) * PQ],
                in0=T["swp"][:, m * PQ:(m + 1) * PQ],
                in1=T["ssz"][:, m * PQ:(m + 1) * PQ], op=ALU.mult),
                wait=[z_ops[mi], d_w[mi]]))
        ytb_done = yt_ops[-1]
        oat_ops = []
        for mi in range(KD):
            pm = [PA, PB][mi % 2]
            mml = []
            for k in range(16):
                wl = [yt_ops[k], op_ops[-1]]
                if mi < 2:
                    wl += [xn_o[-1], st1[0], st1[1]]
                if len(oat_ops) >= 2:
                    wl.append(oat_ops[-2])
                mm = top(lambda e, m=mi, k=k, p=pm: e.matmul(
                    p[0:P, 0:PQ], T["sopT"][:, k * D + m * 128:k * D + (m + 1) * 128],
                    T["sytb"][:, k * PQ:(k + 1) * PQ],
                    start=(k == 0), stop=(k == 15)), wait=wl)
                mml.append(mm)
            ev = vop(lambda e, m=mi, p=pm: e.tensor_copy(
                T["soat"][:, m * PQ:(m + 1) * PQ], p[0:P, 0:PQ]), wait=mml)
            oat_ops.append(ev)
        o_ops = []
        for k in range(KD):
            o_ops.append(vop(lambda e, k=k: e.tensor_add(
                T["soat"][:, k * PQ:(k + 1) * PQ], T["soat"][:, k * PQ:(k + 1) * PQ],
                T["sxn8"][:, k * PQ:(k + 1) * PQ]), wait=[oat_ops[k], xn_o[k]],
                force=True))
        at_o, st2 = piece_ln("soat", "soat", None, o_ops, st1[4], "lno")
        gb_ops = []
        for k in range(KD):
            gb_ops.append(vop(lambda e, k=k: e.tensor_scalar(
                out=T["soat"][:, k * PQ:(k + 1) * PQ],
                in0=T["soat"][:, k * PQ:(k + 1) * PQ],
                scalar1=T["slng"][:, k:k + 1], scalar2=T["slnb"][:, k:k + 1],
                op0=ALU.mult, op1=ALU.add), wait=[at_o[k], SL["slng"], SL["slnb"]],
                force=True))
        x2_ops = []
        for k in range(KD):
            x2_ops.append(vop(lambda e, k=k: e.tensor_add(
                T["soat"][:, k * PQ:(k + 1) * PQ], T["soat"][:, k * PQ:(k + 1) * PQ],
                T["sxo"][:, k * PQ:(k + 1) * PQ]), wait=[gb_ops[k]], force=True))
        d_x2 = []
        for k in range(KD):
            d_x2.append(dma("sync", x2_scr.ap()[k * P:(k + 1) * P, pc:pc + PQ],
                            T["soat"][:, k * PQ:(k + 1) * PQ], wait=[x2_ops[k]],
                            sem="st"))
        xn2_o, st3 = piece_ln("soat", "sxn8", "sxn2b", x2_ops, st2[4], "ln2")
        tt_tr = {0: [], 1: []}
        for tt in range(2):
            pm = [PC, PD][tt]
            trl = []
            for k in range(KD):
                wl = [xn2_o[k], v_idb]
                if k == 0 and tt == 0:
                    wl += z_ops[-2:]
                tr = top(lambda e, k=k, t=tt, p=pm: e.transpose(
                    p[0:P, k * 64:(k + 1) * 64].bitcast(BF16),
                    T["sxn2b"][:, k * PQ + t * 128:k * PQ + t * 128 + 128],
                    T["identb"][:]), wait=wl)
                trl.append(tr)
            ev = vop(lambda e, t=tt, p=pm: e.tensor_copy(
                T["sxn2t"][:, t * 1056:t * 1056 + 1024],
                p[0:P, 0:512].bitcast(BF16)), wait=trl)
            vo1 = vop(lambda e, t=tt: e.memset(T["sxn2t"][:, t * 1056 + 1024:t * 1056 + 1025], 1.0),
                      wait=[ev], force=True)
            vo2 = vop(lambda e, t=tt: e.memset(T["sxn2t"][:, t * 1056 + 1025:t * 1056 + 1056], 0.0),
                      wait=[vo1], force=True)
            tt_tr[tt] = [ev, vo1, vo2]
        h1_ops = []
        for mt in range(4):
            pm = [PC, PD][mt % 2]
            mml = []
            for k in range(KD):
                wl = [xn2_o[k], w1_ops[-1], tt_tr[0][0], tt_tr[1][0]]
                if len(h1_ops) >= 2:
                    wl.append(h1_ops[-2])
                mm = top(lambda e, m=mt, k=k, p=pm: e.matmul(
                    p[0:P, 0:PQ], T["sw1t"][:, k * 512 + m * 128:k * 512 + (m + 1) * 128],
                    T["sxn2b"][:, k * PQ:(k + 1) * PQ],
                    start=(k == 0), stop=(k == KD - 1)), wait=wl)
                mml.append(mm)
            sg = sop(lambda e, m=mt, p=pm: e.activation(
                out=T["sh1b"][:, m * PQ:(m + 1) * PQ], in_=p[0:P, 0:PQ], func=AF.Gelu,
                bias=T["sbt1"][:, m:m + 1]), wait=mml + [v_bt1])
            h1_ops.append(sg)
        rt_mm = []
        for k4 in range(4):
            rt_mm.append(top(lambda e, k=k4: e.matmul(
                PA[0:64, 0:PQ], T["sw2b"][:, k * 64:(k + 1) * 64],
                T["sh1b"][:, k * PQ:(k + 1) * PQ],
                start=(k == 0), stop=(k == 3)),
                wait=[h1_ops[k4], w2_ops[-1], st3[0], st3[1]]))
        v_rt = vop(lambda e: e.tensor_scalar(
            out=T["srt"][:], in0=PA[0:64, 0:PQ], scalar1=T["sitau"][:, 0:1],
            scalar2=T["sitau"][:, 1:2], op0=ALU.mult, op1=ALU.add),
            wait=rt_mm + [v_it2])
        rtt_tr = []
        for tt in range(2):
            tr = top(lambda e, t=tt: e.transpose(
                PB[0:P, t * 64:(t + 1) * 64], T["srt"][0:64, t * 128:(t + 1) * 128],
                T["ident"][0:64, 0:64]), wait=[v_rt, st3[2], st3[3]])
            rtt_tr.append(tr)
        v_rtt = vop(lambda e: e.tensor_copy(T["srtt"][:], PB[0:P, 0:128]), wait=rtt_tr)
        for tt in range(2):
            gt = p * 2 + tt
            vm8 = vop(lambda e, t=tt: e.max(out=T["st8"][:],
                                            in_=T["srtt"][:, t * 64:(t + 1) * 64]),
                      wait=[v_rtt])
            vms = vop(lambda e, t=tt: e.tensor_scalar(
                out=T["sal"][:], in0=T["srtt"][:, t * 64:(t + 1) * 64],
                scalar1=T["st8"][:, 7:8], scalar2=None, op0=ALU.is_ge),
                wait=[vm8], force=True)
            vmn = vop(lambda e: e.tensor_scalar_mul(T["stm1"][:], T["st8"][:, 0:1], -1.0),
                      wait=[vm8], force=True)
            sex = sop(lambda e, t=tt: e.activation(
                out=T["sq2"][:, 0:64], in_=T["srtt"][:, t * 64:(t + 1) * 64],
                func=AF.Exp, bias=T["stm1"][:]), wait=[vmn, vms])
            vem = vop(lambda e: e.tensor_mul(T["sal"][:], T["sal"][:], T["sq2"][:, 0:64]),
                      wait=[sex], force=True)
            vsm = vop(lambda e: e.reduce_sum(T["sals"][:], T["sal"][:], axis=AX.X),
                      wait=[vem], force=True)
            vrc = vop(lambda e: e.reciprocal(T["sals"][:], T["sals"][:]),
                      wait=[vsm], force=True)
            val = vop(lambda e, g=gt: e.tensor_scalar(
                out=T["salfb"][:, g * 64:(g + 1) * 64], in0=T["sal"][:],
                scalar1=T["sals"][:], scalar2=None, op0=ALU.mult),
                wait=[vrc], force=True)
            sm1a = top(lambda e, g=gt, t=tt: e.matmul(
                PC[0:64, 0:512], T["salfb"][:, g * 64:(g + 1) * 64],
                T["sxn2t"][:, t * 1056:t * 1056 + 512], start=True, stop=True),
                wait=[val, tt_tr[tt][2], rtt_tr[0], rtt_tr[1], v_rt])
            sm1 = top(lambda e, g=gt, t=tt: e.matmul(
                PC[0:64, 512:1024], T["salfb"][:, g * 64:(g + 1) * 64],
                T["sxn2t"][:, t * 1056 + 512:t * 1056 + 1024], start=True, stop=True),
                wait=[sm1a])
            sm2 = top(lambda e, g=gt, t=tt: e.matmul(
                PD[0:64, 0:32], T["salfb"][:, g * 64:(g + 1) * 64],
                T["sxn2t"][:, t * 1056 + 1024:t * 1056 + 1056], start=True, stop=True),
                wait=[val])
            va1 = vop(lambda e: e.tensor_add(T["ssl"][:, 0:1024], T["ssl"][:, 0:1024],
                                             PC[0:64, 0:1024]), wait=[sm1, sm1a, v_slz],
                      force=True)
            va2 = vop(lambda e: e.tensor_add(T["ssl"][:, 1024:1056], T["ssl"][:, 1024:1056],
                                             PD[0:64, 0:32]), wait=[sm2, va1], force=True)
            slot_mm_all.append(va2)
        prev_piece = {"ytb": ytb_done, "x2d": d_x2[-1], "lastsq": st3[4],
                      "vrt": v_rt, "slot": slot_mm_all[-1]}

    d_sli = dma("sync", cc_sl_in.ap()[:, 0:1056], T["ssl"][:], wait=[slot_mm_all[-1]],
                sem="st")
    g_sl = S.op("gpsimd", lambda e: e.collective_compute(
        "AllReduce", ALU.add, replica_groups=RG,
        ins=[cc_sl_in.ap()[:, 0:1056]], outs=[cc_sl_out.ap()[:, 0:1056]]),
        inc=("cc", 1), wait=[d_sli])

    p2end = [slot_mm_all[-1], prev_piece["x2d"]]
    d_sl = dma("sync", T["ssl"][:], cc_sl_out.ap()[:, 0:1056], wait=[g_sl])
    s0_ops = []
    for k in range(KD):
        wl = [g_sl] + (p2end if k == 0 else [])
        s0_ops.append(dma("sync", T["ss0f"][:, k * 64:(k + 1) * 64],
                          E["s0T"].ap()[k * P:(k + 1) * P, :], wait=wl))
    v_s0b = vop(lambda e: e.tensor_copy(T["ss0b"][:], T["ss0f"][:]),
                wait=s0_ops + p2end)
    d_bv = dma("sync", T["sstg"][0:1, 0:D], E["bv_row"].ap(), wait=[s0_ops[-1]])
    d_gg = dma("sync", T["sstg"][0:1, D:2 * D], E["fflng"].ap(), wait=[d_bv])
    d_bb = dma("sync", T["sstg"][0:1, 2 * D:3 * D], E["fflnb"].ap(), wait=[d_gg])
    t_bv = top(lambda e: e.matmul(PA[0:64, 0:512], T["onesr"][0:1, 0:64],
                                  T["sstg"][0:1, 0:512], start=True, stop=True),
               wait=[d_bv] + p2end)
    t_bv2 = top(lambda e: e.matmul(PA[0:64, 512:1024], T["onesr"][0:1, 0:64],
                                   T["sstg"][0:1, 512:1024], start=True, stop=True),
                wait=[d_bv])
    v_bv = vop(lambda e: e.tensor_copy(T["sbvb"][:], PA[0:64, 0:1024]),
               wait=[t_bv, t_bv2] + p2end)
    t_gg = top(lambda e: e.matmul(PB[0:64, 0:512], T["onesr"][0:1, 0:64],
                                  T["sstg"][0:1, D:D + 512], start=True, stop=True),
               wait=[d_gg])
    t_gg2 = top(lambda e: e.matmul(PB[0:64, 512:1024], T["onesr"][0:1, 0:64],
                                   T["sstg"][0:1, D + 512:2 * D], start=True, stop=True),
                wait=[d_gg])
    v_gg = vop(lambda e: e.tensor_copy(T["sgbb"][:, 0:1024], PB[0:64, 0:1024]),
               wait=[t_gg, t_gg2], force=True)
    t_bb = top(lambda e: e.matmul(PC[0:64, 0:512], T["onesr"][0:1, 0:64],
                                  T["sstg"][0:1, 2 * D:2 * D + 512], start=True, stop=True),
               wait=[d_bb])
    t_bb2 = top(lambda e: e.matmul(PC[0:64, 512:1024], T["onesr"][0:1, 0:64],
                                   T["sstg"][0:1, 2 * D + 512:3 * D], start=True, stop=True),
                wait=[d_bb])
    v_bb = vop(lambda e: e.tensor_copy(T["sgbb"][:, 1024:2048], PC[0:64, 0:1024]),
               wait=[t_bb, t_bb2], force=True)
    v_dn = vop(lambda e: e.tensor_scalar(
        out=T["ssc3"][:, 0:1], in0=T["ssl"][:, 1024:1025],
        scalar1=T["fc"][0:64, FC_E8:FC_E8 + 1], scalar2=None, op0=ALU.add),
        wait=[d_sl] + p2end)
    v_dr = vop(lambda e: e.reciprocal(T["ssc3"][:, 0:1], T["ssc3"][:, 0:1]),
               wait=[v_dn], force=True)
    v_si = vop(lambda e: e.tensor_scalar(
        out=T["ssl"][:, 0:1024], in0=T["ssl"][:, 0:1024],
        scalar1=T["ssc3"][:, 0:1], scalar2=None, op0=ALU.mult),
        wait=[v_dr], force=True)
    si_tr, si_ev = [], []
    for k in range(KD):
        tr = top(lambda e, k=k: e.transpose(
            PD[0:P, 0:64], T["ssl"][0:64, k * 128:(k + 1) * 128], T["ident"][0:64, 0:64]),
            wait=[v_si, v_bb] + ([si_ev[-1]] if si_ev else []))
        ev = vop(lambda e, k=k: e.tensor_copy(T["ssiT"][:, k * 64:(k + 1) * 64],
                                              PD[0:P, 0:64]), wait=[tr])
        si_tr.append(tr)
        si_ev.append(ev)
    gate_mm = []
    prevw = si_ev[-1]
    for src, rhs_name, part in [("wihT", "ssiT", 0), ("whhT", "ss0b", 1)]:
        for k in range(KD):
            ldk = dma("sync", T["sstg"][:, 0:3072], E[src].ap()[k * P:(k + 1) * P, :],
                      wait=[prevw])
            ck = vop(lambda e: e.tensor_copy(T["swkb"][:], T["sstg"][:, 0:3072]),
                     wait=[ldk, v_gg])
            prevw = ck
            for mc in range(24):
                if mc < 16:
                    dst = lambda e, m=mc: PA[0:P, m * 64:(m + 1) * 64]
                    first = (part == 0 and k == 0)
                    last = (part == 1 and k == KD - 1)
                else:
                    dst = lambda e, m=mc, pt=part: PB[0:P, (pt * 8 + m - 16) * 64:(pt * 8 + m - 15) * 64]
                    first = (k == 0)
                    last = (k == KD - 1)
                mm = top(lambda e, m=mc, k=k, r=rhs_name, d=dst, f=first, l=last: e.matmul(
                    d(e), T["swkb"][:, m * 128:(m + 1) * 128],
                    T[r][:, k * 64:(k + 1) * 64], start=f, stop=l),
                    wait=[ck, v_s0b, si_ev[k] if part == 0 else v_s0b, v_bv])
                gate_mm.append(mm)
    gate_r = []
    for mc in range(8):
        gate_r.append(sop(lambda e, m=mc: e.activation(
            out=T["sgr"][:, m * 64:(m + 1) * 64], in_=PA[0:P, m * 64:(m + 1) * 64],
            func=AF.Sigmoid, bias=T["sbihh"][:, m:m + 1]), wait=gate_mm[-24:],
            force=True))
    gate_z = []
    for mc in range(8):
        gate_z.append(sop(lambda e, m=mc: e.activation(
            out=T["sgz"][:, m * 64:(m + 1) * 64], in_=PA[0:P, (8 + m) * 64:(9 + m) * 64],
            func=AF.Sigmoid, bias=T["sbihh"][:, 8 + m:9 + m]), wait=gate_mm[-24:],
            force=True))
    v_hn = []
    for mc in range(8):
        a1 = vop(lambda e, m=mc: e.tensor_scalar(
            out=T["sgt"][:, m * 64:(m + 1) * 64], in0=PB[0:P, (8 + m) * 64:(9 + m) * 64],
            scalar1=T["sbhh"][:, 16 + m:17 + m], scalar2=None, op0=ALU.add),
            wait=gate_mm[-24:] + [SL["sbhh"]])
        a2 = vop(lambda e, m=mc: e.tensor_mul(
            T["sgt"][:, m * 64:(m + 1) * 64], T["sgt"][:, m * 64:(m + 1) * 64],
            T["sgr"][:, m * 64:(m + 1) * 64]), wait=[a1, gate_r[mc]], force=True)
        a3 = vop(lambda e, m=mc: e.tensor_scalar(
            out=T["sgn"][:, m * 64:(m + 1) * 64], in0=PB[0:P, m * 64:(m + 1) * 64],
            scalar1=T["sbihh"][:, 16 + m:17 + m], scalar2=None, op0=ALU.add),
            wait=gate_mm[-24:] + [SL["sbihh"]])
        a4 = vop(lambda e, m=mc: e.tensor_add(
            T["sgn"][:, m * 64:(m + 1) * 64], T["sgn"][:, m * 64:(m + 1) * 64],
            T["sgt"][:, m * 64:(m + 1) * 64]), wait=[a2, a3], force=True)
        v_hn.append(a4)
    s_n = sop(lambda e: e.activation(out=T["sgn"][:], in_=T["sgn"][:], func=AF.Tanh),
              wait=v_hn)
    v_sn1 = vop(lambda e: e.tensor_sub(T["sgt"][:], T["ss0f"][:], T["sgn"][:]),
                wait=[s_n, v_s0b])
    v_sn2 = vop(lambda e: e.tensor_mul(T["sgt"][:], T["sgt"][:], T["sgz"][:]),
                wait=[v_sn1] + gate_z, force=True)
    v_sn3 = vop(lambda e: e.tensor_add(T["ssnew"][:], T["sgn"][:], T["sgt"][:]),
                wait=[v_sn2], force=True)
    v_snb = vop(lambda e: e.tensor_copy(T["ssnb"][:], T["ssnew"][:]), wait=[v_sn3],
                force=True)
    wq_ops = stream_cast2(E["wq"].ap(), "sw3", 8, 256, prev=prevw, coloff=0,
                          guard=p2end)
    wk_ops = stream_cast2(E["wk"].ap(), "sw3", 8, 256, prev=wq_ops[-1], coloff=2048)
    qk_ev = []
    for which, co, bias, outoff in [(0, 0, "sbq", 0), (1, 2048, "sbk", 128)]:
        for m2 in range(2):
            mml = []
            for k in range(KD):
                mm = top(lambda e, k=k, m=m2, c=co: e.matmul(
                    PC[0:P, 0:64], T["sw3"][:, c + k * 256 + m * 128:c + k * 256 + (m + 1) * 128],
                    T["ssnb"][:, k * 64:(k + 1) * 64], start=(k == 0), stop=(k == KD - 1)),
                    wait=[v_snb, (wq_ops if which == 0 else wk_ops)[-1],
                          gate_r[-1], gate_z[-1]] + ([qk_ev[-1]] if qk_ev else []))
                mml.append(mm)
            ev = vop(lambda e, m=m2, b=bias, oo=outoff: e.tensor_scalar(
                out=T["sqkT"][:, (oo // 64 + m) * 64:(oo // 64 + m + 1) * 64],
                in0=PC[0:P, 0:64], scalar1=T[b][:, m:m + 1], scalar2=None,
                op0=ALU.add), wait=mml + [SL[bias]])
            qk_ev.append(ev)
    sc_mm = []
    for k2 in range(2):
        sc_mm.append(top(lambda e, k=k2: e.matmul(
            PD[0:64, 0:64], T["sqkT"][:, k * 64:(k + 1) * 64],
            T["sqkT"][:, (2 + k) * 64:(3 + k) * 64], start=(k == 0), stop=(k == 1)),
            wait=[qk_ev[1], qk_ev[3], si_ev[-1]]))
    v_mx = vop(lambda e: e.reduce_max(T["ssc3"][:, 1:2], PD[0:64, 0:64], axis=AX.X),
               wait=sc_mm)
    v_mxs = vop(lambda e: e.tensor_scalar_mul(T["ssc3"][:, 2:3], T["ssc3"][:, 1:2],
                                              -(256.0 ** -0.5)), wait=[v_mx], force=True)
    s_se = sop(lambda e: e.activation(out=T["satm"][:], in_=PD[0:64, 0:64], func=AF.Exp,
                                      scale=T["fc"][0:64, FC_SC:FC_SC + 1],
                                      bias=T["ssc3"][:, 2:3]), wait=[v_mxs])
    v_ss = vop(lambda e: e.reduce_sum(T["ssc3"][:, 3:4], T["satm"][:], axis=AX.X),
               wait=[s_se])
    v_sr = vop(lambda e: e.reciprocal(T["ssc3"][:, 3:4], T["ssc3"][:, 3:4]),
               wait=[v_ss], force=True)
    v_at = vop(lambda e: e.tensor_scalar(
        out=T["satb"][:], in0=T["satm"][:], scalar1=T["ssc3"][:, 3:4], scalar2=None,
        op0=ALU.mult), wait=[v_sr], force=True)
    t_atT = top(lambda e: e.transpose(PD[0:64, 64:96].bitcast(BF16), T["satb"][:],
                                      T["identb"][0:64, 0:64]), wait=[v_at, v_mx, s_se])
    v_atT = vop(lambda e: e.tensor_copy(T["satT"][:], PD[0:64, 64:96].bitcast(BF16)),
                wait=[t_atT])
    wv_ops = stream_cast2(E["wv"].ap(), "sw3", 8, 1024, prev=wk_ops[-1],
                          guard=[qk_ev[-1]])
    v_mm = []
    for hf in range(2):
        for k in range(KD):
            mm = top(lambda e, k=k, f=hf: e.matmul(
                PA[0:64, f * 512:(f + 1) * 512],
                T["ssnb"][:, k * 64:(k + 1) * 64],
                T["sw3"][:, k * 1024 + f * 512:k * 1024 + (f + 1) * 512],
                start=(k == 0), stop=(k == KD - 1)),
                wait=[wv_ops[k], v_snb, v_bv, gate_r[-1], gate_z[-1]])
        v_mm.append(mm)
    v_vt = vop(lambda e: e.tensor_add(T["svtok"][:], PA[0:64, 0:1024], T["sbvb"][:]),
               wait=v_mm + [v_bv])
    v_vtb = vop(lambda e: e.tensor_copy(T["svtkb"][:], T["svtok"][:]), wait=[v_vt],
                force=True)
    t_scma = top(lambda e: e.matmul(PB[0:64, 0:512], T["satT"][:], T["svtkb"][:, 0:512],
                                    start=True, stop=True),
                 wait=[v_atT, v_vtb, v_gg])
    t_scm = top(lambda e: e.matmul(PB[0:64, 512:1024], T["satT"][:],
                                   T["svtkb"][:, 512:1024], start=True, stop=True),
                wait=[t_scma])
    snT_tr = []
    for k in range(KD):
        tr = top(lambda e, k=k: e.transpose(
            PC[0:64, k * 128:(k + 1) * 128], T["ssnew"][:, k * 64:(k + 1) * 64],
            T["ident"][:]), wait=[v_sn3, v_bb, s_se])
        snT_tr.append(tr)
    v_y3a = vop(lambda e: e.tensor_copy(T["sy3"][:], PB[0:64, 0:1024]),
                wait=snT_tr + [t_scm])
    v_y3 = vop(lambda e: e.tensor_add(T["sy3"][:], T["sy3"][:], PC[0:64, 0:1024]),
               wait=[v_y3a], force=True)
    v_m3 = vop(lambda e: e.reduce_sum(T["ssc3"][:, 4:5], T["sy3"][:], axis=AX.X),
               wait=[v_y3], force=True)
    v_m3b = vop(lambda e: e.tensor_scalar_mul(T["ssc3"][:, 4:5], T["ssc3"][:, 4:5],
                                              1.0 / D), wait=[v_m3], force=True)
    s_q3 = sop(lambda e: e.activation(out=T["sy3q"][:], in_=T["sy3"][:], func=AF.Square),
               wait=[v_y3])
    v_e3 = vop(lambda e: e.reduce_sum(T["ssc3"][:, 5:6], T["sy3q"][:], axis=AX.X),
               wait=[s_q3])
    v_e3b = vop(lambda e: e.tensor_scalar_mul(T["ssc3"][:, 5:6], T["ssc3"][:, 5:6],
                                              1.0 / D), wait=[v_e3], force=True)
    v_v3 = vop(lambda e: e.tensor_mul(T["ssc3"][:, 6:7], T["ssc3"][:, 4:5],
                                      T["ssc3"][:, 4:5]), wait=[v_m3b], force=True)
    v_v3b = vop(lambda e: e.tensor_sub(T["ssc3"][:, 6:7], T["ssc3"][:, 5:6],
                                       T["ssc3"][:, 6:7]), wait=[v_v3, v_e3b], force=True)
    s_l3 = sop(lambda e: e.activation(out=T["ssc3"][:, 6:7], in_=T["ssc3"][:, 6:7],
                                      func=AF.Ln, bias=T["fc"][0:64, FC_EPS:FC_EPS + 1]),
               wait=[v_v3b])
    s_i3 = sop(lambda e: e.activation(out=T["ssc3"][:, 7:8], in_=T["ssc3"][:, 6:7],
                                      func=AF.Exp,
                                      scale=T["fc"][0:64, FC_NHALF:FC_NHALF + 1]),
               wait=[s_l3], force=True)
    v_yn = vop(lambda e: e.tensor_scalar(
        out=T["sy3"][:], in0=T["sy3"][:], scalar1=T["ssc3"][:, 4:5],
        scalar2=T["ssc3"][:, 7:8], op0=ALU.subtract, op1=ALU.mult),
        wait=[s_i3], force=True)
    v_yg = vop(lambda e: e.tensor_mul(T["sy3"][:], T["sy3"][:], T["sgbb"][:, 0:1024]),
               wait=[v_yn, v_gg], force=True)
    v_yb = vop(lambda e: e.tensor_add(T["sy3"][:], T["sy3"][:], T["sgbb"][:, 1024:2048]),
               wait=[v_yg, v_bb], force=True)
    y3T_tr, y3T_ev = [], []
    for k in range(KD):
        tr = top(lambda e, k=k: e.transpose(
            PD[0:P, 0:64], T["sy3"][0:64, k * 128:(k + 1) * 128], T["ident"][0:64, 0:64]),
            wait=[v_yb, v_atT] + ([y3T_ev[-1]] if y3T_ev else []))
        ev = vop(lambda e, k=k: e.tensor_copy(T["sy3Tb"][:, k * 64:(k + 1) * 64],
                                              PD[0:P, 0:64]), wait=[tr])
        y3T_tr.append(tr)
        y3T_ev.append(ev)
    wvp_ops = stream_cast2(E["wvp"].ap(), "sw3", 8, 1024, prev=wv_ops[-1],
                           guard=[v_mm[-1]])
    sv_tok = []
    for mt in range(KD):
        mml = []
        for k in range(KD):
            mm = top(lambda e, k=k, m=mt: e.matmul(
                PA[0:P, 0:64], T["sw3"][:, k * 1024 + m * 128:k * 1024 + (m + 1) * 128],
                T["sy3Tb"][:, k * 64:(k + 1) * 64], start=(k == 0), stop=(k == KD - 1)),
                wait=[wvp_ops[k], y3T_ev[k], v_vt] + ([sv_tok[-1]] if sv_tok else []))
            mml.append(mm)
        ev = vop(lambda e, m=mt: e.tensor_scalar(
            out=T["sgt"][:, 0:64], in0=PA[0:P, 0:64], scalar1=T["sbvp"][:, m:m + 1],
            scalar2=None, op0=ALU.add), wait=mml + [SL["sbvp"]])
        tr = top(lambda e, m=mt: e.transpose(PB[0:64, 0:128], T["sgt"][0:P, 0:64],
                                             T["ident"][:]), wait=[ev, v_y3])
        ev2 = vop(lambda e, m=mt: e.tensor_copy(T["ssvb"][:, m * 128:(m + 1) * 128],
                                                PB[0:64, 0:128]), wait=[tr])
        sv_tok.append(ev2)
    al_tr = []
    for gt in range(16):
        tr = top(lambda e, g=gt: e.transpose(
            PC[0:64, 0:64].bitcast(BF16), T["salfb"][:, g * 64:(g + 1) * 64],
            T["identb"][:]),
            wait=[snT_tr[-1], sv_tok[-1] if gt == 0 else al_tr[-1], v_yb])
        ev = vop(lambda e, g=gt: e.tensor_copy(T["salfT"][:, g * 128:(g + 1) * 128],
                                               PC[0:64, 0:64].bitcast(BF16)), wait=[tr])
        al_tr.append(ev)
    wo_ops = stream_cast2(E["wo"].ap(), "sw3", 8, 1024, prev=wvp_ops[-1],
                          guard=[sv_tok[-1]])
    prev_fin = None
    prev_outd = None
    prev_outev = None
    for p in range(NPC):
        pc = p * PQ
        d_x2p = []
        for k in range(KD):
            wl = [wo_ops[-1]]
            if prev_fin is not None:
                wl.append(prev_fin)
            d_x2p.append(dma("sync", T["sx2p"][:, k * PQ:(k + 1) * PQ],
                             x2_scr.ap()[k * P:(k + 1) * P, pc:pc + PQ], wait=wl))
        sc_ev = []
        for dc in range(KD):
            pm = [PA, PB][dc % 2]
            mml = []
            for tt in range(2):
                gt = p * 2 + tt
                mm = top(lambda e, d=dc, g=gt, t=tt, pmm=pm: e.matmul(
                    pmm[0:P, t * 128:(t + 1) * 128],
                    T["ssvb"][0:64, d * 128:(d + 1) * 128],
                    T["salfT"][0:64, g * 128:(g + 1) * 128], start=True, stop=True),
                    wait=[sv_tok[dc], al_tr[gt], v_yb] +
                         ([sc_ev[-2]] if len(sc_ev) >= 2 else []) +
                         ([prev_fin, prev_outev] if (p > 0 and dc == 0) else []))
                mml.append(mm)
            ev = vop(lambda e, d=dc, pmm=pm: e.tensor_copy(
                T["soutb"][:, d * PQ:(d + 1) * PQ], pmm[0:P, 0:PQ]), wait=mml)
            sc_ev.append(ev)
        fin_ops = []
        for mt in range(KD):
            pm = [PC, PD][mt % 2]
            mml = []
            for k in range(KD):
                mm = top(lambda e, k=k, m=mt, pmm=pm: e.matmul(
                    pmm[0:P, 0:PQ], T["sw3"][:, k * 1024 + m * 128:k * 1024 + (m + 1) * 128],
                    T["soutb"][:, k * PQ:(k + 1) * PQ], start=(k == 0), stop=(k == KD - 1)),
                    wait=[wo_ops[k], sc_ev[k]] +
                         ([fin_ops[-2]] if len(fin_ops) >= 2 else []) +
                         ([al_tr[-1]] if (p == 0 and mt == 0) else []))
                mml.append(mm)
            f1 = vop(lambda e, m=mt, pmm=pm: e.tensor_add(
                T["sfin"][:, m * PQ:(m + 1) * PQ], pmm[0:P, 0:PQ],
                T["sx2p"][:, m * PQ:(m + 1) * PQ]), wait=mml + [d_x2p[mt]])
            f2 = vop(lambda e, m=mt: e.tensor_scalar(
                out=T["sfin"][:, m * PQ:(m + 1) * PQ],
                in0=T["sfin"][:, m * PQ:(m + 1) * PQ],
                scalar1=T["sbo"][:, m:m + 1], scalar2=None, op0=ALU.add),
                wait=[f1, SL["sbo"]], force=True)
            fin_ops.append(f2)
        outd = []
        for tt in range(2):
            pm = [PA, PB][tt]
            trl = []
            for k in range(KD):
                tr = top(lambda e, k=k, t=tt, pmm=pm: e.transpose(
                    pmm[0:P, k * 128:(k + 1) * 128],
                    T["sfin"][:, k * PQ + t * 128:k * PQ + t * 128 + 128],
                    T["ident"][:]), wait=[fin_ops[k]] +
                    ([sc_ev[(k + 1) % KD]] if True else []))
                trl.append(tr)
            ev = vop(lambda e, pmm=pm: e.tensor_copy(T["sfint"][:], pmm[0:P, 0:1024]),
                     wait=trl + ([prev_outd] if prev_outd else []), force=True)
            od = dma("sync", out_h.ap()[pc + tt * 128:pc + (tt + 1) * 128, :],
                     T["sfint"][:], wait=[ev], sem="st")
            outd.append(od)
            prev_outd = od
            prev_outev = ev
        prev_fin = fin_ops[-1]

    if KDBG:
        dd1 = dma("sync", dbg_ar.ap(), cc_ar_out.ap(), wait=[prev_outd], sem="st")
        dd2 = dma("sync", dbg_ag.ap(), cc_ag_out.ap(), wait=[dd1], sem="st")
        dd3 = dma("sync", dbg_x2.ap(), x2_scr.ap(), wait=[dd2], sem="st")
        dd4 = dma("sync", dbg_sl.ap(), cc_sl_out.ap(), wait=[dd3], sem="st")
        prev_outd = dd4
    S.op("gpsimd", lambda e: e.engine_nop(), inc=("gA", 1), wait=[prev_outd])



    with nc.Block() as block:
        S.emit(block)
    for g in nc._sem_guards:
        g.__exit__(None, None, None)
    return nc


def _fix_sda_chain(S):
    va = 0
    for o in S.items:
        if o.eng == "scalar":
            o.deps.append(("vA", va, True))
        if o.sem == "vA":
            va = o.count_after


def _stage_core(I, core):
    b, g = core // 4, core % 4
    st = {}
    x = np.ascontiguousarray(I["L0_attn_x"][b].T)
    st["xT"] = x.astype(np.float32)
    WT = np.ascontiguousarray(I["L0_attn_in_proj_w"].T)
    st["wxT"] = np.ascontiguousarray(WT[:, g * DSL:(g + 1) * DSL])
    st["wzT"] = np.ascontiguousarray(WT[:, DI:2 * DI])
    dsl = slice(g * DSL, (g + 1) * DSL)
    cw = I["L0_attn_conv_w"][dsl, 0, :]
    st["cw_st"] = np.ascontiguousarray(cw.reshape(4, P, 4).transpose(1, 0, 2).reshape(P, 16))
    st["cb_st"] = np.ascontiguousarray(I["L0_attn_conv_b"][dsl].reshape(4, P).T)
    xp = I["L0_attn_x_proj_w"][:, dsl]
    st["xpT_st"] = np.ascontiguousarray(
        xp.T.reshape(4, P, 33).transpose(1, 0, 2).reshape(P, 132))
    st["dtw_st"] = np.ascontiguousarray(I["L0_attn_dt_proj_w"][dsl, 0].reshape(4, P).T)
    st["dtb_st"] = np.ascontiguousarray(I["L0_attn_dt_proj_b"][dsl].reshape(4, P).T)
    al = I["L0_attn_A_log"][dsl]
    st["Alog_st"] = np.ascontiguousarray(
        al.reshape(4, P, 16).transpose(1, 0, 2).reshape(P, 64))
    st["D_st"] = np.ascontiguousarray(I["L0_attn_D"][dsl].reshape(4, P).T)
    st["opT"] = np.ascontiguousarray(I["L0_attn_out_proj_w"].T)
    st["lng_st"] = np.ascontiguousarray(I["L0_attn_ln_gamma"].reshape(KD, P).T)
    st["lnb_st"] = np.ascontiguousarray(I["L0_attn_ln_beta"].reshape(KD, P).T)
    st["w1t"] = np.ascontiguousarray(I["L0_ffn_w1"][0:D])
    st["w1b"] = np.ascontiguousarray(I["L0_ffn_w1"][D:2 * D])
    st["b1_st"] = np.ascontiguousarray(I["L0_ffn_b1"].reshape(4, P).T)
    st["w2"] = np.ascontiguousarray(I["L0_ffn_w2"])
    st["b2_st"] = np.ascontiguousarray(I["L0_ffn_b2"].reshape(64, 1))
    st["tau"] = np.ascontiguousarray(I["L0_ffn_tau"].reshape(1, 1))
    st["wihT"] = np.ascontiguousarray(I["L0_ffn_Wih"].T)
    st["whhT"] = np.ascontiguousarray(I["L0_ffn_Whh"].T)
    st["bih_st"] = np.ascontiguousarray(I["L0_ffn_bih"].reshape(24, P).T)
    st["bhh_st"] = np.ascontiguousarray(I["L0_ffn_bhh"].reshape(24, P).T)
    st["wq"] = np.ascontiguousarray(I["L0_ffn_wq"])
    st["bq_st"] = np.ascontiguousarray(I["L0_ffn_bq"].reshape(2, P).T)
    st["wk"] = np.ascontiguousarray(I["L0_ffn_wk"])
    st["bk_st"] = np.ascontiguousarray(I["L0_ffn_bk"].reshape(2, P).T)
    st["wv"] = np.ascontiguousarray(I["L0_ffn_wv"])
    st["bv_row"] = np.ascontiguousarray(I["L0_ffn_bv"].reshape(1, D))
    st["fflng"] = np.ascontiguousarray(I["L0_ffn_ln_g"].reshape(1, D))
    st["fflnb"] = np.ascontiguousarray(I["L0_ffn_ln_b"].reshape(1, D))
    st["wvp"] = np.ascontiguousarray(I["L0_ffn_wvp"])
    st["bvp_st"] = np.ascontiguousarray(I["L0_ffn_bvp"].reshape(8, P).T)
    st["wo"] = np.ascontiguousarray(I["L0_ffn_wo"])
    st["bo_st"] = np.ascontiguousarray(I["L0_ffn_bo"].reshape(8, P).T)
    st["s0T"] = np.ascontiguousarray(I["L0_ffn_slot_init"].T)
    st["xTo"] = np.ascontiguousarray(st["xT"][:, g * TSL:(g + 1) * TSL])
    out = {k: np.asarray(v, dtype=np.float32) for k, v in st.items()}
    out["sown"] = np.array([[2 * g]], dtype=np.int32)
    return out


def kernel(**inputs):
    nc = build()
    in_maps = [_stage_core(inputs, c) for c in range(8)]
    res = run_bass_kernel_spmd(nc, in_maps, core_ids=list(range(8)))
    out = np.empty((B, L, D), np.float32)
    for c in range(8):
        b, g = c // 4, c % 4
        out[b, g * TSL:(g + 1) * TSL, :] = res.results[c]["out"]
    return out


# version 06
# speedup vs baseline: 1.1418x; 1.1418x over previous
import os
import numpy as np
import concourse.bass as bass
import concourse.mybir as mybir
from concourse.bass_utils import run_bass_kernel_spmd

F32 = mybir.dt.float32
BF16 = mybir.dt.bfloat16
AF = mybir.ActivationFunctionType
ALU = mybir.AluOpType
AX = mybir.AxisListType

P = 128
B, L, D, DI, NST = 2, 8192, 1024, 2048, 16
NCH = 8
Q = L // NCH
DSL = DI // 4
TSL = L // 4
PQ = 256
NPC = TSL // PQ
KD = D // P
RG = [[0, 1, 2, 3], [4, 5, 6, 7]]


class _Op:
    __slots__ = ("eng", "fn", "deps", "sem", "inc", "count_after")


class Sched:
    def __init__(self, nc, semnames):
        self.nc = nc
        self.items = []
        self.counts = {n: 0 for n in semnames}
        self.sems = {n: nc._sched_sems[n] for n in semnames}

    def op(self, eng, fn, inc=None, wait=(), force_wait=False):
        o = _Op()
        o.eng, o.fn = eng, fn
        o.deps = []
        for w in wait:
            if w is None:
                continue
            need = force_wait or w.eng != eng or w.sem in ("ld", "st", "cc")
            o.deps.append((w.sem, w.count_after, need))
        lvl = int(os.environ.get("KSYNC", "14"))
        if lvl == 2:
            o.deps = [(s, c, True) for s, c in self.counts.items() if c > 0]
        elif lvl == 14:
            if eng in ("sync", "gpsimd"):
                o.deps = [(s, c, True) for s, c in self.counts.items() if c > 0]
            elif eng == "tensor":
                o.deps += [(s, self.counts[s], True) for s in ("vA", "sA")
                           if self.counts[s] > 0]
            else:
                o.deps += [("tA", self.counts["tA"], True)] if self.counts["tA"] > 0 else []
        elif lvl == 13:
            if eng in ("sync", "gpsimd"):
                o.deps += [(s, self.counts[s], True) for s in ("vA", "sA", "tA", "gA")
                           if self.counts[s] > 0]
            elif eng == "tensor":
                o.deps += [(s, self.counts[s], True) for s in ("vA", "sA")
                           if self.counts[s] > 0]
            else:
                o.deps += [("tA", self.counts["tA"], True)] if self.counts["tA"] > 0 else []
        elif lvl == 3:
            if eng in ("sync", "gpsimd"):
                o.deps = [(s, c, True) for s, c in self.counts.items() if c > 0]
            else:
                o.deps += [(s, self.counts[s], True) for s in ("ld", "st", "cc")
                           if self.counts[s] > 0]
                if eng == "tensor":
                    o.deps += [(s, self.counts[s], True) for s in ("vA", "sA")
                               if self.counts[s] > 0]
                else:
                    o.deps += [("tA", self.counts["tA"], True)] if self.counts["tA"] > 0 else []
        elif lvl == 1:
            if eng == "tensor":
                o.deps += [(s, self.counts[s], True) for s in ("vA", "sA")
                           if self.counts[s] > 0]
            elif eng in ("vector", "scalar"):
                o.deps += [("tA", self.counts["tA"], True)] if self.counts["tA"] > 0 else []
        o.sem, o.inc = None, 0
        if inc is not None:
            s, n = inc
            self.counts[s] += n
            o.sem, o.inc, o.count_after = s, n, self.counts[s]
        self.items.append(o)
        return o

    def emit(self, block):
        waited = {e: {} for e in ("sync", "scalar", "tensor", "vector", "gpsimd")}

        def mk(engname):
            def run(eng):
                w8 = waited[engname]
                for o in self.items:
                    if o.eng != engname:
                        continue
                    for (s, cnt, needed) in o.deps:
                        if needed and w8.get(s, 0) < cnt:
                            eng.wait_ge(self.sems[s], cnt)
                            w8[s] = cnt
                    ins = o.fn(eng)
                    if o.sem is not None:
                        ins.then_inc(self.sems[o.sem], o.inc)
            return run

        block.sync(mk("sync"))
        block.scalar(mk("scalar"))
        block.tensor(mk("tensor"))
        block.vector(mk("vector"))
        block.gpsimd(mk("gpsimd"))


def build():
    nc = bass.Bass("TRN2", target_bir_lowering=False, detect_race_conditions=False)
    E = {}

    def inp(name, shape):
        E[name] = nc.dram_tensor(name, list(shape), F32, kind="ExternalInput")
        return E[name]

    for nm, shp in [
            ("xT", [D, L]), ("wxT", [D, DSL]), ("wzT", [D, DI]),
            ("cw_st", [P, 16]), ("cb_st", [P, 4]), ("xpT_st", [P, 132]),
            ("dtw_st", [P, 4]), ("dtb_st", [P, 4]), ("Alog_st", [P, 64]),
            ("D_st", [P, 4]), ("opT", [DI, D]), ("lng_st", [P, 8]),
            ("lnb_st", [P, 8]), ("w1t", [D, 512]), ("w1b", [D, 512]),
            ("b1_st", [P, 4]), ("w2", [512, 64]), ("b2_st", [64, 1]),
            ("tau", [1, 1]), ("wihT", [D, 3072]), ("whhT", [D, 3072]),
            ("bih_st", [P, 24]), ("bhh_st", [P, 24]), ("wq", [D, 256]),
            ("bq_st", [P, 2]), ("wk", [D, 256]), ("bk_st", [P, 2]),
            ("wv", [D, D]), ("bv_row", [1, D]), ("fflng", [1, D]),
            ("fflnb", [1, D]), ("wvp", [D, D]), ("bvp_st", [P, 8]),
            ("wo", [D, D]), ("bo_st", [P, 8]), ("s0T", [D, 64]),
            ("xTo", [D, TSL])]:
        inp(nm, shp)
    E["sown"] = nc.dram_tensor("sown", [1, 1], mybir.dt.int32, kind="ExternalInput")

    out_h = nc.dram_tensor("out", [TSL, D], F32, kind="ExternalOutput")
    cc_ar_in = nc.dram_tensor("cc_ar_in", [NCH * 33, Q], F32)
    cc_ar_out = nc.dram_tensor("cc_ar_out", [NCH * 33, Q], F32)
    cc_ag_in = nc.dram_tensor("cc_ag_in", [NCH * DSL, Q], F32)
    cc_ag_out = nc.dram_tensor("cc_ag_out", [NCH * DI, Q], F32)
    x2_scr = nc.dram_tensor("x2_scr", [D, TSL], F32)
    cc_sl_in = nc.dram_tensor("cc_sl_in", [64, 1056], F32)
    cc_sl_out = nc.dram_tensor("cc_sl_out", [64, 1056], F32)
    KDBG = bool(os.environ.get("KDBG"))
    if KDBG:
        dbg_p2 = nc.dram_tensor("dbg_p2", [P, 4 * 8 * PQ], F32, kind="ExternalOutput")
        dbg_p3 = nc.dram_tensor("dbg_p3", [P, 3 * 16 * PQ], F32, kind="ExternalOutput")
        dbg_ar = nc.dram_tensor("dbg_ar", [NCH * 33, Q], F32, kind="ExternalOutput")
        dbg_ag = nc.dram_tensor("dbg_ag", [NCH * DI, Q], F32, kind="ExternalOutput")
        dbg_x2 = nc.dram_tensor("dbg_x2", [D, TSL], F32, kind="ExternalOutput")
        dbg_sl = nc.dram_tensor("dbg_sl", [64, 1056], F32, kind="ExternalOutput")

    SB = {}

    def sbp(name, shape, dt=F32):
        SB[name] = nc.alloc_sbuf_tensor(name, list(shape), dt)
        return SB[name]

    sbp("ident", [P, P])
    sbp("identb", [P, P], BF16)
    sbp("onesr", [1, P])
    sbp("onec", [P, 1])
    sbp("sel16", [16, 16 * P])
    sbp("fc", [P, 8])
    FC_EPS, FC_NHALF, FC_SC, FC_TENTH, FC_E8 = 0, 1, 2, 3, 4
    sbp("sdiag", [P, 16 * P], BF16)
    sbp("scw", [P, 16])
    sbp("scb", [P, 4])
    sbp("sxp", [P, 132])
    sbp("sdtw", [P, 4])
    sbp("sdtb", [P, 4])
    sbp("sA", [P, 64])
    sbp("sD", [P, 4])
    sbp("slng", [P, 8])
    sbp("slnb", [P, 8])
    sbp("scarry", [P, 64])
    sbp("salfb", [P, 16 * 64], BF16)
    sbp("swx", [P, 8 * 512], BF16)
    sbp("sw1t", [P, 8 * 512], BF16)
    sbp("sw2b", [P, 4 * 64], BF16)
    sbp("sbt1", [P, 4])
    sbp("sitau", [64, 2])
    sbp("sb2", [64, 1])
    sbp("sbihh", [P, 24])
    sbp("sbhh", [P, 24])
    sbp("sbq", [P, 2])
    sbp("sbk", [P, 2])
    sbp("sbvp", [P, 8])
    sbp("sbo", [P, 8])
    sbp("ssmean", [P, 8])
    sbp("sstg", [P, 3072])
    sbp("ssl", [64, 1056])
    SB["sown"] = nc.alloc_sbuf_tensor("sown_sb", [1, 1], mybir.dt.int32)

    RB = 150 * 1024
    r0, _ = nc.bump_sbuf(RB)

    def mkregion():
        ptr = [r0]

        def f(name, shape, dt=F32):
            nbytes = int(np.prod(shape[1:])) * (2 if dt == BF16 else 4)
            off = (ptr[0] + 31) // 32 * 32
            ptr[0] = off + nbytes
            assert ptr[0] <= r0 + RB, f"{name}: reuse overflow {ptr[0] - r0}"
            SB[name] = nc.alloc_sbuf_tensor_at(name, list(shape), dt, offset=off)
            return SB[name]
        return f

    sb1 = mkregion()
    sb1("sxT", [P, 2 * Q])
    sb1("sxq", [P, 2 * Q])
    sb1("sxnb", [P, 8 * Q], BF16)
    sb1("sxin", [P, 4 * (Q + 4)], BF16)
    sb1("sxc", [P, 4 * Q])
    sb1("sdt", [P, 4 * Q])
    sb1("sys", [P, 4 * Q])
    sb1("sBbc", [P, Q])
    sb1("sCbc", [P, Q])
    sb1("sdA", [P, Q])
    sb1("sdBx", [P, Q])
    sb1("sh", [P, Q])
    sb1("stmp", [P, Q])
    sb1("sx0r", [1, Q])
    sb1("sBr", [16, Q])
    sb1("sCr", [16, Q])
    sb1("sxde", [33, Q])
    sb1("sstat", [1, 2 * Q])
    sb1("svar", [1, Q])
    sb1("ssinv", [1, Q])

    sb2 = mkregion()
    sb2("swzT", [P, 8 * DI], BF16)
    sb2("sopT", [P, 16 * D], BF16)
    sb2("swp", [P, 16 * PQ])
    sb2("ssz", [P, 16 * PQ])
    sb2("sytb", [P, 16 * PQ], BF16)
    sb2("soat", [P, 8 * PQ])
    sb2("sxo", [P, 8 * PQ])
    sb2("sxn8", [P, 8 * PQ])
    sb2("sxnb2", [P, 8 * PQ], BF16)
    sb2("sxn2b", [P, 8 * PQ], BF16)
    sb2("sxn2t", [P, 2 * 1056], BF16)
    sb2("sh1b", [P, 4 * PQ], BF16)
    sb2("srt", [64, PQ])
    sb2("srtt", [P, 2 * 64])
    sb2("st8", [P, 8])
    sb2("stm1", [P, 1])
    sb2("sal", [P, 64])
    sb2("sals", [P, 1])
    sb2("sq2", [P, 2 * PQ])
    sb2("sstat2", [1, 2 * PQ])
    sb2("svar2", [1, PQ])
    sb2("ssinv2", [1, PQ])

    sb3 = mkregion()
    sb3("swkb", [P, 3072], BF16)
    sb3("sw3", [P, 8 * 1024], BF16)
    sb3("ssiT", [P, 8 * 64], BF16)
    sb3("ss0f", [P, 8 * 64])
    sb3("ss0b", [P, 8 * 64], BF16)
    sb3("sgr", [P, 8 * 64])
    sb3("sgz", [P, 8 * 64])
    sb3("sgn", [P, 8 * 64])
    sb3("sgt", [P, 8 * 64])
    sb3("ssnew", [P, 8 * 64])
    sb3("ssnb", [P, 8 * 64], BF16)
    sb3("sqkT", [P, 4 * 64], BF16)
    sb3("satm", [64, 64])
    sb3("satb", [64, 64], BF16)
    sb3("satT", [64, 64], BF16)
    sb3("svtok", [64, D])
    sb3("svtkb", [64, D], BF16)
    sb3("sy3", [64, D])
    sb3("sy3q", [64, D])
    sb3("sy3Tb", [P, 8 * 64], BF16)
    sb3("ssvb", [64, 8 * P], BF16)
    sb3("salfT", [64, 16 * P], BF16)
    sb3("ssc3", [64, 8])
    sb3("sbvb", [64, D])
    sb3("sgbb", [64, 2 * D])
    sb3("sx2p", [P, 8 * PQ])
    sb3("soutb", [P, 8 * PQ], BF16)
    sb3("sfin", [P, 8 * PQ])
    sb3("sfint", [P, D])

    PA = nc.alloc_psum_tensor("PA", [P, 1024], F32)
    PB = nc.alloc_psum_tensor("PB", [P, 1024], F32)
    PC = nc.alloc_psum_tensor("PC", [P, 1024], F32)
    PD = nc.alloc_psum_tensor("PD", [P, 1024], F32)

    semnames = ["ld", "tA", "vA", "sA", "gA", "st", "cc"]
    nc._sched_sems = {}
    nc._sem_guards = []
    for n_ in semnames:
        g = nc.semaphore(n_)
        nc._sched_sems[n_] = g.__enter__()
        nc._sem_guards.append(g)

    S = Sched(nc, semnames)
    T = SB

    def dma(q, dst, src, wait=(), sem="ld"):
        return S.op(q, lambda e, d=dst, s=src: e.dma_start(out=d, in_=s),
                    inc=(sem, 16), wait=wait)

    def vop(fn, wait=(), force=False):
        return S.op("vector", fn, inc=("vA", 1), wait=wait, force_wait=force)

    def sop(fn, wait=(), force=False):
        return S.op("scalar", fn, inc=("sA", 1), wait=wait, force_wait=force)

    def top(fn, wait=(), force=False):
        return S.op("tensor", fn, inc=("tA", 1), wait=wait, force_wait=force)

    def gop(fn, wait=(), force=False):
        return S.op("gpsimd", fn, inc=("gA", 1), wait=wait, force_wait=force)

    SL = {}
    for nm, dst in [("cw_st", "scw"), ("cb_st", "scb"), ("xpT_st", "sxp"),
                    ("dtw_st", "sdtw"), ("dtb_st", "sdtb"), ("Alog_st", "sA"),
                    ("D_st", "sD"), ("lng_st", "slng"), ("lnb_st", "slnb"),
                    ("b2_st", "sb2"), ("bq_st", "sbq"), ("bk_st", "sbk"),
                    ("bvp_st", "sbvp"), ("bo_st", "sbo"), ("bih_st", "sbihh"),
                    ("bhh_st", "sbhh"), ("b1_st", "sbt1")]:
        SL[dst] = dma("sync", T[dst][:], E[nm].ap())
    ld_tau = dma("sync", T["fc"][0:1, 6:7], E["tau"].ap())

    g0 = gop(lambda e: e.memset(T["ident"][:], 0.0))
    g1 = gop(lambda e: e.affine_select(out=T["ident"][:], in_=T["ident"][:],
                                       compare_op=ALU.not_equal, fill=1.0, base=0,
                                       pattern=[[-1, P]], channel_multiplier=1),
             wait=[g0], force=True)
    gop(lambda e: e.memset(T["onesr"][:], 1.0))
    gop(lambda e: e.memset(T["onec"][:], 1.0))
    gs0 = gop(lambda e: e.memset(T["sel16"][:], 0.0))
    gop(lambda e: e.affine_select(
        out=T["sel16"][:].rearrange("c (b j) -> c b j", b=16),
        in_=T["sel16"][:].rearrange("c (b j) -> c b j", b=16),
        compare_op=ALU.not_equal, fill=1.0, base=0,
        pattern=[[-1, 16], [0, P]], channel_multiplier=1),
        wait=[gs0], force=True)
    for col, val in [(FC_EPS, 1e-5), (FC_NHALF, -0.5), (FC_SC, 256.0 ** -0.5),
                     (FC_TENTH, 0.1), (FC_E8, 1e-8)]:
        gop(lambda e, cc=col, v=val: e.memset(T["fc"][:, cc:cc + 1], v))
    g_all = S.op("gpsimd", lambda e: e.engine_nop(), inc=("gA", 1))

    v_idb = vop(lambda e: e.tensor_copy(T["identb"][:], T["ident"][:]), wait=[g1])
    s_negA = sop(lambda e: e.activation(out=T["sA"][:], in_=T["sA"][:], func=AF.Exp),
                 wait=[SL["sA"], g_all])
    v_negA = vop(lambda e: e.tensor_scalar_mul(T["sA"][:], T["sA"][:], -1.0),
                 wait=[s_negA])
    diag_ops = []
    for j in range(16):
        diag_ops.append(vop(lambda e, j=j: e.tensor_scalar_mul(
            T["sdiag"][:, j * P:(j + 1) * P], T["ident"][:], T["scw"][:, j:j + 1]),
            wait=[g1, SL["scw"]]))
    v_bih = vop(lambda e: e.tensor_add(T["sbihh"][:, 0:16], T["sbihh"][:, 0:16],
                                       T["sbhh"][:, 0:16]), wait=[SL["sbihh"], SL["sbhh"]])
    s_ab = sop(lambda e: e.activation(out=T["fc"][0:1, 7:8], in_=T["fc"][0:1, 6:7],
                                      func=AF.Abs), wait=[ld_tau, g_all])
    v_ab = vop(lambda e: e.tensor_add(T["fc"][0:1, 7:8], T["fc"][0:1, 7:8],
                                      T["fc"][0:1, FC_TENTH:FC_TENTH + 1]), wait=[s_ab, g_all])
    v_rc = vop(lambda e: e.reciprocal(T["fc"][0:1, 7:8], T["fc"][0:1, 7:8]),
               wait=[v_ab], force=True)
    t_it = top(lambda e: e.matmul(PA[0:64, 0:1], T["onesr"][0:1, 0:64],
                                  T["fc"][0:1, 7:8], start=True, stop=True),
               wait=[v_rc, g_all])
    v_it = vop(lambda e: e.tensor_copy(T["sitau"][:, 0:1], PA[0:64, 0:1]), wait=[t_it])
    v_it2 = vop(lambda e: e.tensor_mul(T["sitau"][:, 1:2], T["sb2"][:],
                                       T["sitau"][:, 0:1]),
                wait=[v_it, SL["sb2"]], force=True)

    def stream_cast(dram_ap, dst_name, kt, cols, prev=None, dst_coloff=0):
        ops = []
        lastop = prev
        for k in range(kt):
            ldk = dma("sync", T["sstg"][:, 0:cols], dram_ap[k * P:(k + 1) * P, :],
                      wait=[lastop] if lastop else [])
            ck = vop(lambda e, k=k: e.tensor_copy(
                T[dst_name][:, dst_coloff + k * cols:dst_coloff + (k + 1) * cols],
                T["sstg"][:, 0:cols]), wait=[ldk])
            ops.append(ck)
            lastop = ck
        return ops

    wx_ops = stream_cast(E["wxT"].ap(), "swx", 8, 512)
    w1_ops = stream_cast(E["w1t"].ap(), "sw1t", 8, 512, prev=wx_ops[-1])
    w2_ops = stream_cast(E["w2"].ap(), "sw2b", 4, 64, prev=w1_ops[-1])

    prevc = w2_ops[-1]
    sm_ops = []
    for k in range(KD):
        ldk = dma("sync", T["sstg"][:, 0:64], E["s0T"].ap()[k * P:(k + 1) * P, :],
                  wait=[prevc])
        vr = vop(lambda e, k=k: e.reduce_sum(T["ssmean"][:, k:k + 1],
                                             T["sstg"][:, 0:64], axis=AX.X), wait=[ldk])
        sm_ops.append(vr)
        prevc = vr
    v_sm = vop(lambda e: e.tensor_scalar_mul(T["ssmean"][:], T["ssmean"][:], 1.0 / 64.0),
               wait=[prevc], force=True)
    c1_last = []
    prevc = v_sm
    for k in range(KD):
        ldk = dma("sync", T["sstg"][:, 0:512], E["w1b"].ap()[k * P:(k + 1) * P, :],
                  wait=[prevc])
        for mt in range(4):
            mm = top(lambda e, k=k, m=mt: e.matmul(
                PB[0:P, m:m + 1], T["sstg"][:, m * 128:(m + 1) * 128],
                T["ssmean"][:, k:k + 1], start=(k == 0), stop=(k == KD - 1)),
                wait=[ldk, v_sm])
            if k == KD - 1:
                c1_last.append(mm)
        prevc = mm
    v_bt1 = vop(lambda e: e.tensor_add(T["sbt1"][:], T["sbt1"][:], PB[0:P, 0:4]),
                wait=c1_last + [SL["sbt1"]])

    prev_xn_last = None
    prev_sq = None
    prev_scan_end = None
    prev_dt_last = None
    prev_wds = []
    ag_ops = []
    for c in range(NCH):
        C0 = c * Q
        mu_mms, sq_mms, tile_lds = [], [], []
        for k in range(KD):
            half = (k % 2) * Q
            wl = [wx_ops[-1], v_bt1]
            if len(mu_mms) >= 2:
                wl += [mu_mms[-2], sq_mms[-2]]
            if k < 2 and prev_xn_last is not None:
                wl.append(prev_xn_last)
            ldk = dma("sync", T["sxT"][:, half:half + Q],
                      E["xT"].ap()[k * P:(k + 1) * P, C0:C0 + Q], wait=wl)
            tile_lds.append(ldk)
            sqk = sop(lambda e, h=half: e.activation(
                out=T["sxq"][:, h:h + Q], in_=T["sxT"][:, h:h + Q], func=AF.Square),
                wait=[ldk] + ([prev_sq] if prev_sq else []), force=True)
            prev_sq = sqk
            for hf in range(2):
                wl2 = [ldk, g_all]
                if c > 0 and k == 0:
                    wl2 += [prev_scan_end, prev_dt_last]
                mu_mms.append(top(lambda e, h=half, f=hf, kk=k: e.matmul(
                    PA[0:1, f * 512:(f + 1) * 512], T["onec"][:],
                    T["sxT"][:, h + f * 512:h + (f + 1) * 512],
                    start=(kk == 0), stop=(kk == KD - 1)), wait=wl2))
                sq_mms.append(top(lambda e, h=half, f=hf, kk=k: e.matmul(
                    PB[0:1, f * 512:(f + 1) * 512], T["onec"][:],
                    T["sxq"][:, h + f * 512:h + (f + 1) * 512],
                    start=(kk == 0), stop=(kk == KD - 1)), wait=[sqk]))
        v_mu = vop(lambda e: e.tensor_scalar_mul(T["sstat"][0:1, 0:Q], PA[0:1, 0:Q],
                                                 1.0 / D), wait=mu_mms[-2:])
        v_e2 = vop(lambda e: e.tensor_scalar_mul(T["sstat"][0:1, Q:2 * Q], PB[0:1, 0:Q],
                                                 1.0 / D), wait=sq_mms[-2:])
        v_va = vop(lambda e: e.tensor_mul(T["svar"][0:1, :], T["sstat"][0:1, 0:Q],
                                          T["sstat"][0:1, 0:Q]), wait=[v_mu])
        v_vb = vop(lambda e: e.tensor_sub(T["svar"][0:1, :], T["sstat"][0:1, Q:2 * Q],
                                          T["svar"][0:1, :]), wait=[v_va, v_e2], force=True)
        s_ln = sop(lambda e: e.activation(out=T["svar"][0:1, :], in_=T["svar"][0:1, :],
                                          func=AF.Ln, bias=T["fc"][0:1, FC_EPS:FC_EPS + 1]),
                   wait=[v_vb])
        s_si = sop(lambda e: e.activation(out=T["ssinv"][0:1, :], in_=T["svar"][0:1, :],
                                          func=AF.Exp,
                                          scale=T["fc"][0:1, FC_NHALF:FC_NHALF + 1]),
                   wait=[s_ln], force=True)
        bc_mu, bc_si = [], []
        for hf in range(2):
            bc_mu.append(top(lambda e, f=hf: e.matmul(
                PC[0:P, f * 512:(f + 1) * 512], T["onesr"][:],
                T["sstat"][0:1, f * 512:(f + 1) * 512], start=True, stop=True),
                wait=[v_mu, v_vb, prev_scan_end]))
            bc_si.append(top(lambda e, f=hf: e.matmul(
                PD[0:P, f * 512:(f + 1) * 512], T["onesr"][:],
                T["ssinv"][0:1, f * 512:(f + 1) * 512], start=True, stop=True),
                wait=[s_si]))
        xn_ops = []
        for k in range(KD):
            half = (k % 2) * Q
            wl = [mu_mms[-1], sq_mms[-1]]
            if len(xn_ops) >= 2:
                wl.append(xn_ops[-2])
            ldk = dma("sync", T["sxT"][:, half:half + Q],
                      E["xT"].ap()[k * P:(k + 1) * P, C0:C0 + Q], wait=wl)
            v1 = vop(lambda e, h=half: e.tensor_sub(
                T["sxT"][:, h:h + Q], T["sxT"][:, h:h + Q], PC[0:P, 0:Q]),
                wait=[ldk] + bc_mu)
            v2 = vop(lambda e, h=half, k=k: e.tensor_tensor(
                out=T["sxnb"][:, k * Q:(k + 1) * Q], in0=T["sxT"][:, h:h + Q],
                in1=PD[0:P, 0:Q], op=ALU.mult), wait=[v1] + bc_si, force=True)
            xn_ops.append(v2)
        prev_xn_last = xn_ops[-1]
        halo_ops = []
        for dt_ in range(4):
            base = dt_ * (Q + 4)
            if c == 0:
                halo_ops.append(vop(lambda e, b=base: e.memset(T["sxin"][:, b:b + 4], 0.0)))
            else:
                halo_ops.append(vop(lambda e, b=base: e.tensor_copy(
                    T["sxin"][:, b:b + 4], T["sxin"][:, b + Q:b + Q + 4])))
        xi_evac = []
        for mt in range(4):
            pm = [PA, PB][mt % 2]
            mml = []
            for hf in range(2):
                for k in range(KD):
                    wl = [xn_ops[k], wx_ops[-1]]
                    if mt < 2:
                        wl.append([v_mu, v_e2][mt])
                    if len(xi_evac) >= 2:
                        wl.append(xi_evac[-2])
                    mm = top(lambda e, m=mt, f=hf, k=k, p=pm: e.matmul(
                        p[0:P, f * 512:(f + 1) * 512],
                        T["swx"][:, k * 512 + m * 128:k * 512 + (m + 1) * 128],
                        T["sxnb"][:, k * Q + f * 512:k * Q + (f + 1) * 512],
                        start=(k == 0), stop=(k == KD - 1)), wait=wl)
                mml.append(mm)
            ev = vop(lambda e, m=mt, p=pm: e.tensor_copy(
                T["sxin"][:, m * (Q + 4) + 4:(m + 1) * (Q + 4)], p[0:P, 0:Q]),
                wait=mml + halo_ops[mt:mt + 1])
            xi_evac.append(ev)
        xc_ops = []
        for dt_ in range(4):
            pm = [PC, PD][dt_ % 2]
            base = dt_ * (Q + 4)
            mml = []
            for hf in range(2):
                for k_ in range(4):
                    wl = [xi_evac[dt_], v_idb]
                    if dt_ < 2:
                        wl.append(xn_ops[-1])
                    if len(xc_ops) >= 2:
                        wl.append(xc_ops[-2])
                    mm = top(lambda e, d=dt_, f=hf, k=k_, p=pm, b=base: e.matmul(
                        p[0:P, f * 512:(f + 1) * 512],
                        T["sdiag"][:, (d * 4 + k) * P:(d * 4 + k + 1) * P],
                        T["sxin"][:, b + 1 + k + f * 512:b + 1 + k + f * 512 + 512],
                        start=(k == 0), stop=(k == 3)), wait=wl)
                mml.append(mm)
            sv = sop(lambda e, d=dt_, p=pm: e.activation(
                out=T["sxc"][:, d * Q:(d + 1) * Q], in_=p[0:P, 0:Q], func=AF.Silu,
                bias=T["scb"][:, d:d + 1]), wait=mml + [SL["scb"]])
            xc_ops.append(sv)
        xp_mm = []
        for hf in range(2):
            for dt_ in range(4):
                mm = top(lambda e, d=dt_, f=hf: e.matmul(
                    PA[0:33, f * 512:(f + 1) * 512],
                    T["sxp"][:, d * 33:(d + 1) * 33],
                    T["sxc"][:, d * Q + f * 512:d * Q + (f + 1) * 512],
                    start=(d == 0), stop=(d == 3)),
                    wait=[xc_ops[dt_], SL["sxp"], xi_evac[0], xi_evac[2], v_mu])
            xp_mm.append(mm)
        v_xd = vop(lambda e: e.tensor_copy(T["sxde"][:], PA[0:33, 0:Q]), wait=xp_mm)
        d_xd = dma("sync", cc_ar_in.ap()[c * 33:(c + 1) * 33, :], T["sxde"][:],
                   wait=[v_xd], sem="st")
        g_ar = S.op("gpsimd", lambda e, cc_=c: e.collective_compute(
            "AllReduce", ALU.add, replica_groups=RG,
            ins=[cc_ar_in.ap()[cc_ * 33:(cc_ + 1) * 33, :]],
            outs=[cc_ar_out.ap()[cc_ * 33:(cc_ + 1) * 33, :]]),
            inc=("cc", 1), wait=[d_xd])
        d_x0 = dma("sync", T["sx0r"][:], cc_ar_out.ap()[c * 33:c * 33 + 1, :],
                   wait=[g_ar, prev_scan_end])
        d_Br = dma("sync", T["sBr"][:], cc_ar_out.ap()[c * 33 + 1:c * 33 + 17, :],
                   wait=[g_ar, prev_scan_end])
        d_Cr = dma("sync", T["sCr"][:], cc_ar_out.ap()[c * 33 + 17:c * 33 + 33, :],
                   wait=[g_ar, prev_scan_end])
        t_x0 = []
        for hf in range(2):
            t_x0.append(top(lambda e, f=hf: e.matmul(
                PB[0:P, f * 512:(f + 1) * 512], T["onesr"][:],
                T["sx0r"][0:1, f * 512:(f + 1) * 512], start=True, stop=True),
                wait=[d_x0, v_e2, xi_evac[3]]))
        dt_ops = []
        for dt_ in range(4):
            s1 = sop(lambda e, d=dt_: e.activation(
                out=T["sdt"][:, d * Q:(d + 1) * Q], in_=PB[0:P, 0:Q], func=AF.Exp,
                scale=T["sdtw"][:, d:d + 1], bias=T["sdtb"][:, d:d + 1]),
                wait=t_x0 + [SL["sdtw"], SL["sdtb"]])
            s2 = sop(lambda e, d=dt_: e.activation(
                out=T["sdt"][:, d * Q:(d + 1) * Q], in_=T["sdt"][:, d * Q:(d + 1) * Q],
                func=AF.Ln, bias=1.0), wait=[s1], force=True)
            dt_ops.append(s2)
        prev_dt_last = dt_ops[-1]
        ys_init, udt_ops = [], []
        for dt_ in range(4):
            wl = [xc_ops[dt_], xp_mm[-1], SL["sD"]]
            if c > 0:
                wl += prev_wds[dt_:dt_ + 1]
            yi = vop(lambda e, d=dt_: e.tensor_scalar_mul(
                T["sys"][:, d * Q:(d + 1) * Q], T["sxc"][:, d * Q:(d + 1) * Q],
                T["sD"][:, d:d + 1]), wait=wl)
            ys_init.append(yi)
            ud = vop(lambda e, d=dt_: e.tensor_tensor(
                out=T["sxc"][:, d * Q:(d + 1) * Q], in0=T["sxc"][:, d * Q:(d + 1) * Q],
                in1=T["sdt"][:, d * Q:(d + 1) * Q], op=ALU.mult),
                wait=[yi, dt_ops[dt_]], force=True)
            udt_ops.append(ud)
        last_accum = [None] * 4
        for n_ in range(16):
            wbc = [d_Br, v_xd]
            if n_ == 0:
                wbc += [v_mu]
            tb1 = top(lambda e, n=n_: e.matmul(
                PA[0:P, 0:512], T["sel16"][:, n * P:(n + 1) * P],
                T["sBr"][:, 0:512], start=True, stop=True), wait=wbc)
            tb2 = top(lambda e, n=n_: e.matmul(
                PA[0:P, 512:1024], T["sel16"][:, n * P:(n + 1) * P],
                T["sBr"][:, 512:1024], start=True, stop=True), wait=[d_Br])
            tc1 = top(lambda e, n=n_: e.matmul(
                PC[0:P, 0:512], T["sel16"][:, n * P:(n + 1) * P],
                T["sCr"][:, 0:512], start=True, stop=True),
                wait=[d_Cr, xc_ops[2], xc_ops[3]])
            tc2 = top(lambda e, n=n_: e.matmul(
                PC[0:P, 512:1024], T["sel16"][:, n * P:(n + 1) * P],
                T["sCr"][:, 512:1024], start=True, stop=True), wait=[d_Cr])
            vb = vop(lambda e: e.tensor_copy(T["sBbc"][:], PA[0:P, 0:Q]), wait=[tb1, tb2])
            vc = vop(lambda e: e.tensor_copy(T["sCbc"][:], PC[0:P, 0:Q]), wait=[tc1, tc2])
            for dt_ in range(4):
                cidx = dt_ * 16 + n_
                sda = sop(lambda e, d=dt_, n=n_: e.activation(
                    out=T["sdA"][:], in_=T["sdt"][:, d * Q:(d + 1) * Q], func=AF.Exp,
                    scale=T["sA"][:, d * 16 + n:d * 16 + n + 1]),
                    wait=[dt_ops[dt_], v_negA])
                vdx = vop(lambda e, d=dt_: e.tensor_tensor(
                    out=T["sdBx"][:], in0=T["sxc"][:, d * Q:(d + 1) * Q],
                    in1=T["sBbc"][:], op=ALU.mult), wait=[udt_ops[dt_], vb], force=True)
                init = 0.0 if c == 0 else T["scarry"][:, cidx:cidx + 1]
                vsc = vop(lambda e, i=init: e.tensor_tensor_scan(
                    out=T["sh"][:], data0=T["sdA"][:], data1=T["sdBx"][:],
                    initial=i, op0=ALU.mult, op1=ALU.add),
                    wait=[sda, vdx], force=True)
                vcp = vop(lambda e, ci=cidx: e.tensor_copy(
                    T["scarry"][:, ci:ci + 1], T["sh"][:, Q - 1:Q]),
                    wait=[vsc], force=True)
                vtm = vop(lambda e: e.tensor_mul(T["stmp"][:], T["sh"][:], T["sCbc"][:]),
                          wait=[vsc, vc], force=True)
                vac = vop(lambda e, d=dt_: e.tensor_add(
                    T["sys"][:, d * Q:(d + 1) * Q], T["sys"][:, d * Q:(d + 1) * Q],
                    T["stmp"][:]), wait=[vtm, ys_init[dt_]], force=True)
                last_accum[dt_] = vac
        prev_scan_end = last_accum[3]
        wds = []
        for dt_ in range(4):
            wds.append(dma("sync",
                           cc_ag_in.ap()[c * DSL + dt_ * P:c * DSL + (dt_ + 1) * P, :],
                           T["sys"][:, dt_ * Q:(dt_ + 1) * Q],
                           wait=[last_accum[dt_]], sem="st"))
        prev_wds = wds
        g_ag = S.op("gpsimd", lambda e, cc_=c: e.collective_compute(
            "AllGather", ALU.bypass, replica_groups=RG,
            ins=[cc_ag_in.ap()[cc_ * DSL:(cc_ + 1) * DSL, :]],
            outs=[cc_ag_out.ap()[cc_ * DI:(cc_ + 1) * DI, :]]),
            inc=("cc", 1), wait=wds)
        ag_ops.append(g_ag)

    _fix_sda_chain(S)

    p1_done = ag_ops[-1]

    p1v = prev_scan_end
    p1w = prev_wds[-1]
    p1s = prev_dt_last

    def stream_cast2(dram_ap, dst_name, kt, cols, prev, coloff=0, guard=()):
        ops = []
        lastop = prev
        for k in range(kt):
            ldk = dma("sync", T["sstg"][:, 0:cols], dram_ap[k * P:(k + 1) * P, :],
                      wait=[lastop])
            ck = vop(lambda e, k=k, c=cols, co=coloff: e.tensor_copy(
                T[dst_name][:, co + k * c:co + (k + 1) * c], T["sstg"][:, 0:c]),
                wait=[ldk] + (list(guard) if k == 0 else []))
            ops.append(ck)
            lastop = ck
        return ops

    guard2 = [p1v, p1w, p1s]
    wz_ops = stream_cast2(E["wzT"].ap(), "swzT", 8, DI, prev=prev_wds[-1], guard=guard2)
    op_ops = stream_cast2(E["opT"].ap(), "sopT", 16, D, prev=wz_ops[-1], guard=guard2)
    v_slz = vop(lambda e: e.memset(T["ssl"][:], 0.0), wait=[p1v])

    d_own = S.op("gpsimd", lambda e: e.dma_start(out=T["sown"][:], in_=E["sown"].ap()),
                 inc=("ld", 16))
    v_own = S.op("gpsimd",
                 lambda e: setattr(nc, "_own_val", e.value_load(T["sown"][0:1, 0:1])),
                 inc=None, wait=[d_own])

    slot_mm_all = []
    prev_piece = {}
    for p in range(NPC):
        pc = p * PQ
        wl = [ag_ops[-1], op_ops[-1]]
        if p > 0:
            wl.append(prev_piece["ytb"])
        tc = pc % Q
        cadd = pc // Q
        d_w1 = S.op("gpsimd", lambda e, t0=tc, ca=cadd: e.dma_start(
            out=T["swp"][:].rearrange("p (b t) -> p b t", b=16),
            in_=cc_ag_out.ap().rearrange("(c b p) t -> p b c t", p=P, b=16)
                [:, :, bass.ds(nc._own_val + ca, 1), t0:t0 + PQ].squeeze(2)),
            inc=("ld", 16), wait=wl)
        d_w = [d_w1] * 16
        d_x = []
        for k in range(KD):
            wl = [wz_ops[-1]]
            if p > 0:
                wl.append(prev_piece["x2d"])
            d_x.append(dma("sync", T["sxo"][:, k * PQ:(k + 1) * PQ],
                           E["xTo"].ap()[k * P:(k + 1) * P, pc:pc + PQ], wait=wl))

        if KDBG and p == 0:
            for mi in range(16):
                dma("sync", dbg_p3.ap()[:, mi * PQ:(mi + 1) * PQ],
                    T["swp"][:, mi * PQ:(mi + 1) * PQ], wait=[d_w[0]], sem="st")

        def piece_ln(src_name, dst_f32, dst_b16, guard_v, guard_s, tag):
            mus, sqs = [], []
            prev_sq2 = guard_s
            for k in range(KD):
                half = (k % 2) * PQ
                sq = sop(lambda e, k=k, h=half, s=src_name: e.activation(
                    out=T["sq2"][:, h:h + PQ], in_=T[s][:, k * PQ:(k + 1) * PQ],
                    func=AF.Square), wait=guard_v + ([prev_sq2] if prev_sq2 else []),
                    force=True)
                prev_sq2 = sq
                mus.append(top(lambda e, k=k, s=src_name: e.matmul(
                    PA[0:1, 0:PQ], T["onec"][:], T[s][:, k * PQ:(k + 1) * PQ],
                    start=(k == 0), stop=(k == KD - 1)), wait=guard_v))
                sqs.append(top(lambda e, h=half, kk=k: e.matmul(
                    PB[0:1, 0:PQ], T["onec"][:], T["sq2"][:, h:h + PQ],
                    start=(kk == 0), stop=(kk == KD - 1)), wait=[sq]))
            vmu = vop(lambda e: e.tensor_scalar_mul(T["sstat2"][0:1, 0:PQ],
                                                    PA[0:1, 0:PQ], 1.0 / D),
                      wait=mus[-1:])
            ve2 = vop(lambda e: e.tensor_scalar_mul(T["sstat2"][0:1, PQ:2 * PQ],
                                                    PB[0:1, 0:PQ], 1.0 / D),
                      wait=sqs[-1:])
            vva = vop(lambda e: e.tensor_mul(T["svar2"][0:1, :], T["sstat2"][0:1, 0:PQ],
                                             T["sstat2"][0:1, 0:PQ]), wait=[vmu])
            vvb = vop(lambda e: e.tensor_sub(T["svar2"][0:1, :],
                                             T["sstat2"][0:1, PQ:2 * PQ],
                                             T["svar2"][0:1, :]),
                      wait=[vva, ve2], force=True)
            sln = sop(lambda e: e.activation(out=T["svar2"][0:1, :], in_=T["svar2"][0:1, :],
                                             func=AF.Ln,
                                             bias=T["fc"][0:1, FC_EPS:FC_EPS + 1]),
                      wait=[vvb])
            ssi = sop(lambda e: e.activation(out=T["ssinv2"][0:1, :], in_=T["svar2"][0:1, :],
                                             func=AF.Exp,
                                             scale=T["fc"][0:1, FC_NHALF:FC_NHALF + 1]),
                      wait=[sln], force=True)
            bmu = top(lambda e: e.matmul(PA[0:P, 0:PQ], T["onesr"][:],
                                         T["sstat2"][0:1, 0:PQ], start=True, stop=True),
                      wait=[vmu, vvb])
            bsi = top(lambda e: e.matmul(PB[0:P, 0:PQ], T["onesr"][:],
                                         T["ssinv2"][0:1, 0:PQ], start=True, stop=True),
                      wait=[ssi, ve2])
            outs = []
            for k in range(KD):
                v1 = vop(lambda e, k=k, s=src_name, d=dst_f32: e.tensor_sub(
                    T[d][:, k * PQ:(k + 1) * PQ], T[s][:, k * PQ:(k + 1) * PQ],
                    PA[0:P, 0:PQ]), wait=[bmu])
                if dst_b16 is None:
                    v2 = vop(lambda e, k=k, d=dst_f32: e.tensor_tensor(
                        out=T[d][:, k * PQ:(k + 1) * PQ],
                        in0=T[d][:, k * PQ:(k + 1) * PQ], in1=PB[0:P, 0:PQ],
                        op=ALU.mult), wait=[v1, bsi], force=True)
                else:
                    v2 = vop(lambda e, k=k, d=dst_f32: e.tensor_tensor(
                        out=T[d][:, k * PQ:(k + 1) * PQ],
                        in0=T[d][:, k * PQ:(k + 1) * PQ], in1=PB[0:P, 0:PQ],
                        op=ALU.mult), wait=[v1, bsi], force=True)
                    v2 = vop(lambda e, k=k, d=dst_f32, b=dst_b16: e.tensor_copy(
                        T[b][:, k * PQ:(k + 1) * PQ], T[d][:, k * PQ:(k + 1) * PQ]),
                        wait=[v2], force=True)
                outs.append(v2)
            return outs, (vmu, ve2, bmu, bsi, sqs[-1], mus[-1])

        guard_v = [dd for dd in d_x]
        if p > 0:
            guard_v += [prev_piece["vrt"], prev_piece["slot"]]
        gs = prev_piece.get("lastsq")
        xn_o, st1 = piece_ln("sxo", "sxn8", "sxnb2", guard_v, gs, "ln1")
        z_ops = []
        for mi in range(16):
            pm = [PC, PD][mi % 2]
            mml = []
            for k in range(KD):
                wl = [xn_o[k], wz_ops[-1]]
                if len(z_ops) >= 2:
                    wl.append(z_ops[-2])
                if mi < 2:
                    wl += [st1[2], st1[3]]
                mm = top(lambda e, m=mi, k=k, p=pm: e.matmul(
                    p[0:P, 0:PQ], T["swzT"][:, k * DI + m * 128:k * DI + (m + 1) * 128],
                    T["sxnb2"][:, k * PQ:(k + 1) * PQ],
                    start=(k == 0), stop=(k == KD - 1)), wait=wl)
                mml.append(mm)
            sz_ = sop(lambda e, m=mi, p=pm: e.activation(
                out=T["ssz"][:, m * PQ:(m + 1) * PQ], in_=p[0:P, 0:PQ], func=AF.Silu),
                wait=mml)
            z_ops.append(sz_)
        if KDBG and p == 0:
            for mi in range(16):
                dma("sync", dbg_p3.ap()[:, 4096 + mi * PQ:4096 + (mi + 1) * PQ],
                    T["ssz"][:, mi * PQ:(mi + 1) * PQ], wait=[z_ops[mi]], sem="st")
        yt_ops = []
        for mi in range(16):
            yt_ops.append(vop(lambda e, m=mi: e.tensor_tensor(
                out=T["sytb"][:, m * PQ:(m + 1) * PQ],
                in0=T["swp"][:, m * PQ:(m + 1) * PQ],
                in1=T["ssz"][:, m * PQ:(m + 1) * PQ], op=ALU.mult),
                wait=[z_ops[mi], d_w[mi]]))
        ytb_done = yt_ops[-1]
        oat_ops = []
        for mi in range(KD):
            pm = [PA, PB][mi % 2]
            mml = []
            for k in range(16):
                wl = [yt_ops[k], op_ops[-1]]
                if mi < 2:
                    wl += [xn_o[-1], st1[0], st1[1]]
                if len(oat_ops) >= 2:
                    wl.append(oat_ops[-2])
                mm = top(lambda e, m=mi, k=k, p=pm: e.matmul(
                    p[0:P, 0:PQ], T["sopT"][:, k * D + m * 128:k * D + (m + 1) * 128],
                    T["sytb"][:, k * PQ:(k + 1) * PQ],
                    start=(k == 0), stop=(k == 15)), wait=wl)
                mml.append(mm)
            ev = vop(lambda e, m=mi, p=pm: e.tensor_copy(
                T["soat"][:, m * PQ:(m + 1) * PQ], p[0:P, 0:PQ]),
                wait=mml + ([prev_piece["x2d"]] if p > 0 else []))
            oat_ops.append(ev)
        if KDBG and p == 0:
            for k in range(KD):
                dma("sync", dbg_p2.ap()[:, 0 * 2048 + k * PQ:0 * 2048 + (k + 1) * PQ],
                    T["soat"][:, k * PQ:(k + 1) * PQ], wait=[oat_ops[k]], sem="st")
                dma("sync", dbg_p2.ap()[:, 3 * 2048 + k * PQ:3 * 2048 + (k + 1) * PQ],
                    T["sxn8"][:, k * PQ:(k + 1) * PQ], wait=[xn_o[k]], sem="st")
        o_ops = []
        for k in range(KD):
            o_ops.append(vop(lambda e, k=k: e.tensor_add(
                T["soat"][:, k * PQ:(k + 1) * PQ], T["soat"][:, k * PQ:(k + 1) * PQ],
                T["sxn8"][:, k * PQ:(k + 1) * PQ]), wait=[oat_ops[k], xn_o[k]],
                force=True))
        if KDBG and p == 0:
            for k in range(KD):
                dma("sync", dbg_p2.ap()[:, 1 * 2048 + k * PQ:1 * 2048 + (k + 1) * PQ],
                    T["soat"][:, k * PQ:(k + 1) * PQ], wait=[o_ops[k]], sem="st")
        at_o, st2 = piece_ln("soat", "soat", None, o_ops, st1[4], "lno")
        gb_ops = []
        for k in range(KD):
            gb_ops.append(vop(lambda e, k=k: e.tensor_scalar(
                out=T["soat"][:, k * PQ:(k + 1) * PQ],
                in0=T["soat"][:, k * PQ:(k + 1) * PQ],
                scalar1=T["slng"][:, k:k + 1], scalar2=T["slnb"][:, k:k + 1],
                op0=ALU.mult, op1=ALU.add), wait=[at_o[k], SL["slng"], SL["slnb"]],
                force=True))
        if KDBG and p == 0:
            for k in range(KD):
                dma("sync", dbg_p2.ap()[:, 2 * 2048 + k * PQ:2 * 2048 + (k + 1) * PQ],
                    T["soat"][:, k * PQ:(k + 1) * PQ], wait=[gb_ops[k]], sem="st")
        x2_ops = []
        for k in range(KD):
            x2_ops.append(vop(lambda e, k=k: e.tensor_add(
                T["soat"][:, k * PQ:(k + 1) * PQ], T["soat"][:, k * PQ:(k + 1) * PQ],
                T["sxo"][:, k * PQ:(k + 1) * PQ]), wait=[gb_ops[k]], force=True))
        d_x2 = []
        for k in range(KD):
            d_x2.append(dma("sync", x2_scr.ap()[k * P:(k + 1) * P, pc:pc + PQ],
                            T["soat"][:, k * PQ:(k + 1) * PQ], wait=[x2_ops[k]],
                            sem="st"))
        xn2_o, st3 = piece_ln("soat", "sxn8", "sxn2b", x2_ops, st2[4], "ln2")
        tt_tr = {0: [], 1: []}
        for tt in range(2):
            pm = [PC, PD][tt]
            trl = []
            for k in range(KD):
                wl = [xn2_o[k], v_idb]
                if k == 0 and tt == 0:
                    wl += z_ops[-2:]
                tr = top(lambda e, k=k, t=tt, p=pm: e.transpose(
                    p[0:P, k * 64:(k + 1) * 64].bitcast(BF16),
                    T["sxn2b"][:, k * PQ + t * 128:k * PQ + t * 128 + 128],
                    T["identb"][:]), wait=wl)
                trl.append(tr)
            ev = vop(lambda e, t=tt, p=pm: e.tensor_copy(
                T["sxn2t"][:, t * 1056:t * 1056 + 1024],
                p[0:P, 0:512].bitcast(BF16)), wait=trl)
            vo1 = vop(lambda e, t=tt: e.memset(T["sxn2t"][:, t * 1056 + 1024:t * 1056 + 1025], 1.0),
                      wait=[ev], force=True)
            vo2 = vop(lambda e, t=tt: e.memset(T["sxn2t"][:, t * 1056 + 1025:t * 1056 + 1056], 0.0),
                      wait=[vo1], force=True)
            tt_tr[tt] = [ev, vo1, vo2]
        h1_ops = []
        for mt in range(4):
            pm = [PC, PD][mt % 2]
            mml = []
            for k in range(KD):
                wl = [xn2_o[k], w1_ops[-1], tt_tr[0][0], tt_tr[1][0]]
                if len(h1_ops) >= 2:
                    wl.append(h1_ops[-2])
                mm = top(lambda e, m=mt, k=k, p=pm: e.matmul(
                    p[0:P, 0:PQ], T["sw1t"][:, k * 512 + m * 128:k * 512 + (m + 1) * 128],
                    T["sxn2b"][:, k * PQ:(k + 1) * PQ],
                    start=(k == 0), stop=(k == KD - 1)), wait=wl)
                mml.append(mm)
            sg = sop(lambda e, m=mt, p=pm: e.activation(
                out=T["sh1b"][:, m * PQ:(m + 1) * PQ], in_=p[0:P, 0:PQ], func=AF.Gelu,
                bias=T["sbt1"][:, m:m + 1]), wait=mml + [v_bt1])
            h1_ops.append(sg)
        rt_mm = []
        for k4 in range(4):
            rt_mm.append(top(lambda e, k=k4: e.matmul(
                PA[0:64, 0:PQ], T["sw2b"][:, k * 64:(k + 1) * 64],
                T["sh1b"][:, k * PQ:(k + 1) * PQ],
                start=(k == 0), stop=(k == 3)),
                wait=[h1_ops[k4], w2_ops[-1], st3[0], st3[1]]))
        v_rt = vop(lambda e: e.tensor_scalar(
            out=T["srt"][:], in0=PA[0:64, 0:PQ], scalar1=T["sitau"][:, 0:1],
            scalar2=T["sitau"][:, 1:2], op0=ALU.mult, op1=ALU.add),
            wait=rt_mm + [v_it2])
        rtt_tr = []
        for tt in range(2):
            tr = top(lambda e, t=tt: e.transpose(
                PB[0:P, t * 64:(t + 1) * 64], T["srt"][0:64, t * 128:(t + 1) * 128],
                T["ident"][0:64, 0:64]), wait=[v_rt, st3[2], st3[3]])
            rtt_tr.append(tr)
        v_rtt = vop(lambda e: e.tensor_copy(T["srtt"][:], PB[0:P, 0:128]), wait=rtt_tr)
        for tt in range(2):
            gt = p * 2 + tt
            vm8 = vop(lambda e, t=tt: e.max(out=T["st8"][:],
                                            in_=T["srtt"][:, t * 64:(t + 1) * 64]),
                      wait=[v_rtt])
            vms = vop(lambda e, t=tt: e.tensor_scalar(
                out=T["sal"][:], in0=T["srtt"][:, t * 64:(t + 1) * 64],
                scalar1=T["st8"][:, 7:8], scalar2=None, op0=ALU.is_ge),
                wait=[vm8], force=True)
            vmn = vop(lambda e: e.tensor_scalar_mul(T["stm1"][:], T["st8"][:, 0:1], -1.0),
                      wait=[vm8], force=True)
            sex = sop(lambda e, t=tt: e.activation(
                out=T["sq2"][:, 0:64], in_=T["srtt"][:, t * 64:(t + 1) * 64],
                func=AF.Exp, bias=T["stm1"][:]), wait=[vmn, vms])
            vem = vop(lambda e: e.tensor_mul(T["sal"][:], T["sal"][:], T["sq2"][:, 0:64]),
                      wait=[sex], force=True)
            vsm = vop(lambda e: e.reduce_sum(T["sals"][:], T["sal"][:], axis=AX.X),
                      wait=[vem], force=True)
            vrc = vop(lambda e: e.reciprocal(T["sals"][:], T["sals"][:]),
                      wait=[vsm], force=True)
            val = vop(lambda e, g=gt: e.tensor_scalar(
                out=T["salfb"][:, g * 64:(g + 1) * 64], in0=T["sal"][:],
                scalar1=T["sals"][:], scalar2=None, op0=ALU.mult),
                wait=[vrc], force=True)
            sm1a = top(lambda e, g=gt, t=tt: e.matmul(
                PC[0:64, 0:512], T["salfb"][:, g * 64:(g + 1) * 64],
                T["sxn2t"][:, t * 1056:t * 1056 + 512], start=True, stop=True),
                wait=[val, tt_tr[tt][2], rtt_tr[0], rtt_tr[1], v_rt])
            sm1 = top(lambda e, g=gt, t=tt: e.matmul(
                PC[0:64, 512:1024], T["salfb"][:, g * 64:(g + 1) * 64],
                T["sxn2t"][:, t * 1056 + 512:t * 1056 + 1024], start=True, stop=True),
                wait=[sm1a])
            sm2 = top(lambda e, g=gt, t=tt: e.matmul(
                PD[0:64, 0:32], T["salfb"][:, g * 64:(g + 1) * 64],
                T["sxn2t"][:, t * 1056 + 1024:t * 1056 + 1056], start=True, stop=True),
                wait=[val])
            va1 = vop(lambda e: e.tensor_add(T["ssl"][:, 0:1024], T["ssl"][:, 0:1024],
                                             PC[0:64, 0:1024]), wait=[sm1, sm1a, v_slz],
                      force=True)
            va2 = vop(lambda e: e.tensor_add(T["ssl"][:, 1024:1056], T["ssl"][:, 1024:1056],
                                             PD[0:64, 0:32]), wait=[sm2, va1], force=True)
            slot_mm_all.append(va2)
        prev_piece = {"ytb": ytb_done, "x2d": d_x2[-1], "lastsq": st3[4],
                      "vrt": v_rt, "slot": slot_mm_all[-1]}

    d_sli = dma("sync", cc_sl_in.ap()[:, 0:1056], T["ssl"][:], wait=[slot_mm_all[-1]],
                sem="st")
    g_sl = S.op("gpsimd", lambda e: e.collective_compute(
        "AllReduce", ALU.add, replica_groups=RG,
        ins=[cc_sl_in.ap()[:, 0:1056]], outs=[cc_sl_out.ap()[:, 0:1056]]),
        inc=("cc", 1), wait=[d_sli])

    p2end = [slot_mm_all[-1], prev_piece["x2d"]]
    d_sl = dma("sync", T["ssl"][:], cc_sl_out.ap()[:, 0:1056], wait=[g_sl])
    s0_ops = []
    for k in range(KD):
        wl = [g_sl] + (p2end if k == 0 else [])
        s0_ops.append(dma("sync", T["ss0f"][:, k * 64:(k + 1) * 64],
                          E["s0T"].ap()[k * P:(k + 1) * P, :], wait=wl))
    v_s0b = vop(lambda e: e.tensor_copy(T["ss0b"][:], T["ss0f"][:]),
                wait=s0_ops + p2end)
    d_bv = dma("sync", T["sstg"][0:1, 0:D], E["bv_row"].ap(), wait=[s0_ops[-1]])
    d_gg = dma("sync", T["sstg"][0:1, D:2 * D], E["fflng"].ap(), wait=[d_bv])
    d_bb = dma("sync", T["sstg"][0:1, 2 * D:3 * D], E["fflnb"].ap(), wait=[d_gg])
    t_bv = top(lambda e: e.matmul(PA[0:64, 0:512], T["onesr"][0:1, 0:64],
                                  T["sstg"][0:1, 0:512], start=True, stop=True),
               wait=[d_bv] + p2end)
    t_bv2 = top(lambda e: e.matmul(PA[0:64, 512:1024], T["onesr"][0:1, 0:64],
                                   T["sstg"][0:1, 512:1024], start=True, stop=True),
                wait=[d_bv])
    v_bv = vop(lambda e: e.tensor_copy(T["sbvb"][:], PA[0:64, 0:1024]),
               wait=[t_bv, t_bv2] + p2end)
    t_gg = top(lambda e: e.matmul(PB[0:64, 0:512], T["onesr"][0:1, 0:64],
                                  T["sstg"][0:1, D:D + 512], start=True, stop=True),
               wait=[d_gg])
    t_gg2 = top(lambda e: e.matmul(PB[0:64, 512:1024], T["onesr"][0:1, 0:64],
                                   T["sstg"][0:1, D + 512:2 * D], start=True, stop=True),
                wait=[d_gg])
    v_gg = vop(lambda e: e.tensor_copy(T["sgbb"][:, 0:1024], PB[0:64, 0:1024]),
               wait=[t_gg, t_gg2], force=True)
    t_bb = top(lambda e: e.matmul(PC[0:64, 0:512], T["onesr"][0:1, 0:64],
                                  T["sstg"][0:1, 2 * D:2 * D + 512], start=True, stop=True),
               wait=[d_bb])
    t_bb2 = top(lambda e: e.matmul(PC[0:64, 512:1024], T["onesr"][0:1, 0:64],
                                   T["sstg"][0:1, 2 * D + 512:3 * D], start=True, stop=True),
                wait=[d_bb])
    v_bb = vop(lambda e: e.tensor_copy(T["sgbb"][:, 1024:2048], PC[0:64, 0:1024]),
               wait=[t_bb, t_bb2], force=True)
    v_dn = vop(lambda e: e.tensor_scalar(
        out=T["ssc3"][:, 0:1], in0=T["ssl"][:, 1024:1025],
        scalar1=T["fc"][0:64, FC_E8:FC_E8 + 1], scalar2=None, op0=ALU.add),
        wait=[d_sl] + p2end)
    v_dr = vop(lambda e: e.reciprocal(T["ssc3"][:, 0:1], T["ssc3"][:, 0:1]),
               wait=[v_dn], force=True)
    v_si = vop(lambda e: e.tensor_scalar(
        out=T["ssl"][:, 0:1024], in0=T["ssl"][:, 0:1024],
        scalar1=T["ssc3"][:, 0:1], scalar2=None, op0=ALU.mult),
        wait=[v_dr], force=True)
    si_tr, si_ev = [], []
    for k in range(KD):
        tr = top(lambda e, k=k: e.transpose(
            PD[0:P, 0:64], T["ssl"][0:64, k * 128:(k + 1) * 128], T["ident"][0:64, 0:64]),
            wait=[v_si, v_bb] + ([si_ev[-1]] if si_ev else []))
        ev = vop(lambda e, k=k: e.tensor_copy(T["ssiT"][:, k * 64:(k + 1) * 64],
                                              PD[0:P, 0:64]), wait=[tr])
        si_tr.append(tr)
        si_ev.append(ev)
    gate_mm = []
    prevw = si_ev[-1]
    for src, rhs_name, part in [("wihT", "ssiT", 0), ("whhT", "ss0b", 1)]:
        for k in range(KD):
            ldk = dma("sync", T["sstg"][:, 0:3072], E[src].ap()[k * P:(k + 1) * P, :],
                      wait=[prevw, v_bv, v_gg, v_bb])
            ck = vop(lambda e: e.tensor_copy(T["swkb"][:], T["sstg"][:, 0:3072]),
                     wait=[ldk, v_gg])
            prevw = ck
            for mc in range(24):
                if mc < 16:
                    dst = lambda e, m=mc: PA[0:P, m * 64:(m + 1) * 64]
                    first = (part == 0 and k == 0)
                    last = (part == 1 and k == KD - 1)
                else:
                    dst = lambda e, m=mc, pt=part: PB[0:P, (pt * 8 + m - 16) * 64:(pt * 8 + m - 15) * 64]
                    first = (k == 0)
                    last = (k == KD - 1)
                mm = top(lambda e, m=mc, k=k, r=rhs_name, d=dst, f=first, l=last: e.matmul(
                    d(e), T["swkb"][:, m * 128:(m + 1) * 128],
                    T[r][:, k * 64:(k + 1) * 64], start=f, stop=l),
                    wait=[ck, v_s0b, si_ev[k] if part == 0 else v_s0b, v_bv])
                gate_mm.append(mm)
    gate_r = []
    for mc in range(8):
        gate_r.append(sop(lambda e, m=mc: e.activation(
            out=T["sgr"][:, m * 64:(m + 1) * 64], in_=PA[0:P, m * 64:(m + 1) * 64],
            func=AF.Sigmoid, bias=T["sbihh"][:, m:m + 1]), wait=gate_mm[-24:],
            force=True))
    gate_z = []
    for mc in range(8):
        gate_z.append(sop(lambda e, m=mc: e.activation(
            out=T["sgz"][:, m * 64:(m + 1) * 64], in_=PA[0:P, (8 + m) * 64:(9 + m) * 64],
            func=AF.Sigmoid, bias=T["sbihh"][:, 8 + m:9 + m]), wait=gate_mm[-24:],
            force=True))
    v_hn = []
    for mc in range(8):
        a1 = vop(lambda e, m=mc: e.tensor_scalar(
            out=T["sgt"][:, m * 64:(m + 1) * 64], in0=PB[0:P, (8 + m) * 64:(9 + m) * 64],
            scalar1=T["sbhh"][:, 16 + m:17 + m], scalar2=None, op0=ALU.add),
            wait=gate_mm[-24:] + [SL["sbhh"]])
        a2 = vop(lambda e, m=mc: e.tensor_mul(
            T["sgt"][:, m * 64:(m + 1) * 64], T["sgt"][:, m * 64:(m + 1) * 64],
            T["sgr"][:, m * 64:(m + 1) * 64]), wait=[a1, gate_r[mc]], force=True)
        a3 = vop(lambda e, m=mc: e.tensor_scalar(
            out=T["sgn"][:, m * 64:(m + 1) * 64], in0=PB[0:P, m * 64:(m + 1) * 64],
            scalar1=T["sbihh"][:, 16 + m:17 + m], scalar2=None, op0=ALU.add),
            wait=gate_mm[-24:] + [SL["sbihh"]])
        a4 = vop(lambda e, m=mc: e.tensor_add(
            T["sgn"][:, m * 64:(m + 1) * 64], T["sgn"][:, m * 64:(m + 1) * 64],
            T["sgt"][:, m * 64:(m + 1) * 64]), wait=[a2, a3], force=True)
        v_hn.append(a4)
    s_n = sop(lambda e: e.activation(out=T["sgn"][:], in_=T["sgn"][:], func=AF.Tanh),
              wait=v_hn)
    v_sn1 = vop(lambda e: e.tensor_sub(T["sgt"][:], T["ss0f"][:], T["sgn"][:]),
                wait=[s_n, v_s0b])
    v_sn2 = vop(lambda e: e.tensor_mul(T["sgt"][:], T["sgt"][:], T["sgz"][:]),
                wait=[v_sn1] + gate_z, force=True)
    v_sn3 = vop(lambda e: e.tensor_add(T["ssnew"][:], T["sgn"][:], T["sgt"][:]),
                wait=[v_sn2], force=True)
    v_snb = vop(lambda e: e.tensor_copy(T["ssnb"][:], T["ssnew"][:]), wait=[v_sn3],
                force=True)
    wq_ops = stream_cast2(E["wq"].ap(), "sw3", 8, 256, prev=prevw, coloff=0,
                          guard=p2end)
    wk_ops = stream_cast2(E["wk"].ap(), "sw3", 8, 256, prev=wq_ops[-1], coloff=2048)
    qk_ev = []
    for which, co, bias, outoff in [(0, 0, "sbq", 0), (1, 2048, "sbk", 128)]:
        for m2 in range(2):
            mml = []
            for k in range(KD):
                mm = top(lambda e, k=k, m=m2, c=co: e.matmul(
                    PC[0:P, 0:64], T["sw3"][:, c + k * 256 + m * 128:c + k * 256 + (m + 1) * 128],
                    T["ssnb"][:, k * 64:(k + 1) * 64], start=(k == 0), stop=(k == KD - 1)),
                    wait=[v_snb, (wq_ops if which == 0 else wk_ops)[-1],
                          gate_r[-1], gate_z[-1]] + ([qk_ev[-1]] if qk_ev else []))
                mml.append(mm)
            ev = vop(lambda e, m=m2, b=bias, oo=outoff: e.tensor_scalar(
                out=T["sqkT"][:, (oo // 64 + m) * 64:(oo // 64 + m + 1) * 64],
                in0=PC[0:P, 0:64], scalar1=T[b][:, m:m + 1], scalar2=None,
                op0=ALU.add), wait=mml + [SL[bias]])
            qk_ev.append(ev)
    sc_mm = []
    for k2 in range(2):
        sc_mm.append(top(lambda e, k=k2: e.matmul(
            PD[0:64, 0:64], T["sqkT"][:, k * 64:(k + 1) * 64],
            T["sqkT"][:, (2 + k) * 64:(3 + k) * 64], start=(k == 0), stop=(k == 1)),
            wait=[qk_ev[1], qk_ev[3], si_ev[-1]]))
    v_mx = vop(lambda e: e.reduce_max(T["ssc3"][:, 1:2], PD[0:64, 0:64], axis=AX.X),
               wait=sc_mm)
    v_mxs = vop(lambda e: e.tensor_scalar_mul(T["ssc3"][:, 2:3], T["ssc3"][:, 1:2],
                                              -(256.0 ** -0.5)), wait=[v_mx], force=True)
    s_se = sop(lambda e: e.activation(out=T["satm"][:], in_=PD[0:64, 0:64], func=AF.Exp,
                                      scale=T["fc"][0:64, FC_SC:FC_SC + 1],
                                      bias=T["ssc3"][:, 2:3]), wait=[v_mxs])
    v_ss = vop(lambda e: e.reduce_sum(T["ssc3"][:, 3:4], T["satm"][:], axis=AX.X),
               wait=[s_se])
    v_sr = vop(lambda e: e.reciprocal(T["ssc3"][:, 3:4], T["ssc3"][:, 3:4]),
               wait=[v_ss], force=True)
    v_at = vop(lambda e: e.tensor_scalar(
        out=T["satb"][:], in0=T["satm"][:], scalar1=T["ssc3"][:, 3:4], scalar2=None,
        op0=ALU.mult), wait=[v_sr], force=True)
    t_atT = top(lambda e: e.transpose(PD[0:64, 64:96].bitcast(BF16), T["satb"][:],
                                      T["identb"][0:64, 0:64]), wait=[v_at, v_mx, s_se])
    v_atT = vop(lambda e: e.tensor_copy(T["satT"][:], PD[0:64, 64:96].bitcast(BF16)),
                wait=[t_atT])
    wv_ops = stream_cast2(E["wv"].ap(), "sw3", 8, 1024, prev=wk_ops[-1],
                          guard=[qk_ev[-1]])
    v_mm = []
    for hf in range(2):
        for k in range(KD):
            mm = top(lambda e, k=k, f=hf: e.matmul(
                PA[0:64, f * 512:(f + 1) * 512],
                T["ssnb"][:, k * 64:(k + 1) * 64],
                T["sw3"][:, k * 1024 + f * 512:k * 1024 + (f + 1) * 512],
                start=(k == 0), stop=(k == KD - 1)),
                wait=[wv_ops[k], v_snb, v_bv, gate_r[-1], gate_z[-1]])
        v_mm.append(mm)
    v_vt = vop(lambda e: e.tensor_add(T["svtok"][:], PA[0:64, 0:1024], T["sbvb"][:]),
               wait=v_mm + [v_bv])
    v_vtb = vop(lambda e: e.tensor_copy(T["svtkb"][:], T["svtok"][:]), wait=[v_vt],
                force=True)
    t_scma = top(lambda e: e.matmul(PB[0:64, 0:512], T["satT"][:], T["svtkb"][:, 0:512],
                                    start=True, stop=True),
                 wait=[v_atT, v_vtb, v_gg])
    t_scm = top(lambda e: e.matmul(PB[0:64, 512:1024], T["satT"][:],
                                   T["svtkb"][:, 512:1024], start=True, stop=True),
                wait=[t_scma])
    snT_tr = []
    for k in range(KD):
        tr = top(lambda e, k=k: e.transpose(
            PC[0:64, k * 128:(k + 1) * 128], T["ssnew"][:, k * 64:(k + 1) * 64],
            T["ident"][:]), wait=[v_sn3, v_bb, s_se])
        snT_tr.append(tr)
    v_y3a = vop(lambda e: e.tensor_copy(T["sy3"][:], PB[0:64, 0:1024]),
                wait=snT_tr + [t_scm])
    v_y3 = vop(lambda e: e.tensor_add(T["sy3"][:], T["sy3"][:], PC[0:64, 0:1024]),
               wait=[v_y3a], force=True)
    v_m3 = vop(lambda e: e.reduce_sum(T["ssc3"][:, 4:5], T["sy3"][:], axis=AX.X),
               wait=[v_y3], force=True)
    v_m3b = vop(lambda e: e.tensor_scalar_mul(T["ssc3"][:, 4:5], T["ssc3"][:, 4:5],
                                              1.0 / D), wait=[v_m3], force=True)
    s_q3 = sop(lambda e: e.activation(out=T["sy3q"][:], in_=T["sy3"][:], func=AF.Square),
               wait=[v_y3])
    v_e3 = vop(lambda e: e.reduce_sum(T["ssc3"][:, 5:6], T["sy3q"][:], axis=AX.X),
               wait=[s_q3])
    v_e3b = vop(lambda e: e.tensor_scalar_mul(T["ssc3"][:, 5:6], T["ssc3"][:, 5:6],
                                              1.0 / D), wait=[v_e3], force=True)
    v_v3 = vop(lambda e: e.tensor_mul(T["ssc3"][:, 6:7], T["ssc3"][:, 4:5],
                                      T["ssc3"][:, 4:5]), wait=[v_m3b], force=True)
    v_v3b = vop(lambda e: e.tensor_sub(T["ssc3"][:, 6:7], T["ssc3"][:, 5:6],
                                       T["ssc3"][:, 6:7]), wait=[v_v3, v_e3b], force=True)
    s_l3 = sop(lambda e: e.activation(out=T["ssc3"][:, 6:7], in_=T["ssc3"][:, 6:7],
                                      func=AF.Ln, bias=T["fc"][0:64, FC_EPS:FC_EPS + 1]),
               wait=[v_v3b])
    s_i3 = sop(lambda e: e.activation(out=T["ssc3"][:, 7:8], in_=T["ssc3"][:, 6:7],
                                      func=AF.Exp,
                                      scale=T["fc"][0:64, FC_NHALF:FC_NHALF + 1]),
               wait=[s_l3], force=True)
    v_yn = vop(lambda e: e.tensor_scalar(
        out=T["sy3"][:], in0=T["sy3"][:], scalar1=T["ssc3"][:, 4:5],
        scalar2=T["ssc3"][:, 7:8], op0=ALU.subtract, op1=ALU.mult),
        wait=[s_i3], force=True)
    v_yg = vop(lambda e: e.tensor_mul(T["sy3"][:], T["sy3"][:], T["sgbb"][:, 0:1024]),
               wait=[v_yn, v_gg], force=True)
    v_yb = vop(lambda e: e.tensor_add(T["sy3"][:], T["sy3"][:], T["sgbb"][:, 1024:2048]),
               wait=[v_yg, v_bb], force=True)
    y3T_tr, y3T_ev = [], []
    for k in range(KD):
        tr = top(lambda e, k=k: e.transpose(
            PD[0:P, 0:64], T["sy3"][0:64, k * 128:(k + 1) * 128], T["ident"][0:64, 0:64]),
            wait=[v_yb, v_atT] + ([y3T_ev[-1]] if y3T_ev else []))
        ev = vop(lambda e, k=k: e.tensor_copy(T["sy3Tb"][:, k * 64:(k + 1) * 64],
                                              PD[0:P, 0:64]), wait=[tr])
        y3T_tr.append(tr)
        y3T_ev.append(ev)
    wvp_ops = stream_cast2(E["wvp"].ap(), "sw3", 8, 1024, prev=wv_ops[-1],
                           guard=[v_mm[-1]])
    sv_tok = []
    for mt in range(KD):
        mml = []
        for k in range(KD):
            mm = top(lambda e, k=k, m=mt: e.matmul(
                PA[0:P, 0:64], T["sw3"][:, k * 1024 + m * 128:k * 1024 + (m + 1) * 128],
                T["sy3Tb"][:, k * 64:(k + 1) * 64], start=(k == 0), stop=(k == KD - 1)),
                wait=[wvp_ops[k], y3T_ev[k], v_vt] + ([sv_tok[-1]] if sv_tok else []))
            mml.append(mm)
        ev = vop(lambda e, m=mt: e.tensor_scalar(
            out=T["sgt"][:, 0:64], in0=PA[0:P, 0:64], scalar1=T["sbvp"][:, m:m + 1],
            scalar2=None, op0=ALU.add), wait=mml + [SL["sbvp"]])
        tr = top(lambda e, m=mt: e.transpose(PB[0:64, 0:128], T["sgt"][0:P, 0:64],
                                             T["ident"][:]), wait=[ev, v_y3])
        ev2 = vop(lambda e, m=mt: e.tensor_copy(T["ssvb"][:, m * 128:(m + 1) * 128],
                                                PB[0:64, 0:128]), wait=[tr])
        sv_tok.append(ev2)
    al_tr = []
    for gt in range(16):
        tr = top(lambda e, g=gt: e.transpose(
            PC[0:64, 0:64].bitcast(BF16), T["salfb"][:, g * 64:(g + 1) * 64],
            T["identb"][:]),
            wait=[snT_tr[-1], sv_tok[-1] if gt == 0 else al_tr[-1], v_yb])
        ev = vop(lambda e, g=gt: e.tensor_copy(T["salfT"][:, g * 128:(g + 1) * 128],
                                               PC[0:64, 0:64].bitcast(BF16)), wait=[tr])
        al_tr.append(ev)
    wo_ops = stream_cast2(E["wo"].ap(), "sw3", 8, 1024, prev=wvp_ops[-1],
                          guard=[sv_tok[-1]])
    prev_fin = None
    prev_outd = None
    prev_outev = None
    for p in range(NPC):
        pc = p * PQ
        d_x2p = []
        for k in range(KD):
            wl = [wo_ops[-1]]
            if prev_fin is not None:
                wl.append(prev_fin)
            d_x2p.append(dma("sync", T["sx2p"][:, k * PQ:(k + 1) * PQ],
                             x2_scr.ap()[k * P:(k + 1) * P, pc:pc + PQ], wait=wl))
        sc_ev = []
        for dc in range(KD):
            pm = [PA, PB][dc % 2]
            mml = []
            for tt in range(2):
                gt = p * 2 + tt
                mm = top(lambda e, d=dc, g=gt, t=tt, pmm=pm: e.matmul(
                    pmm[0:P, t * 128:(t + 1) * 128],
                    T["ssvb"][0:64, d * 128:(d + 1) * 128],
                    T["salfT"][0:64, g * 128:(g + 1) * 128], start=True, stop=True),
                    wait=[sv_tok[dc], al_tr[gt], v_yb] +
                         ([sc_ev[-2]] if len(sc_ev) >= 2 else []) +
                         ([prev_fin, prev_outev] if (p > 0 and dc == 0) else []))
                mml.append(mm)
            ev = vop(lambda e, d=dc, pmm=pm: e.tensor_copy(
                T["soutb"][:, d * PQ:(d + 1) * PQ], pmm[0:P, 0:PQ]), wait=mml)
            sc_ev.append(ev)
        fin_ops = []
        for mt in range(KD):
            pm = [PC, PD][mt % 2]
            mml = []
            for k in range(KD):
                mm = top(lambda e, k=k, m=mt, pmm=pm: e.matmul(
                    pmm[0:P, 0:PQ], T["sw3"][:, k * 1024 + m * 128:k * 1024 + (m + 1) * 128],
                    T["soutb"][:, k * PQ:(k + 1) * PQ], start=(k == 0), stop=(k == KD - 1)),
                    wait=[wo_ops[k], sc_ev[k]] +
                         ([fin_ops[-2]] if len(fin_ops) >= 2 else []) +
                         ([al_tr[-1]] if (p == 0 and mt == 0) else []))
                mml.append(mm)
            f1 = vop(lambda e, m=mt, pmm=pm: e.tensor_add(
                T["sfin"][:, m * PQ:(m + 1) * PQ], pmm[0:P, 0:PQ],
                T["sx2p"][:, m * PQ:(m + 1) * PQ]), wait=mml + [d_x2p[mt]])
            f2 = vop(lambda e, m=mt: e.tensor_scalar(
                out=T["sfin"][:, m * PQ:(m + 1) * PQ],
                in0=T["sfin"][:, m * PQ:(m + 1) * PQ],
                scalar1=T["sbo"][:, m:m + 1], scalar2=None, op0=ALU.add),
                wait=[f1, SL["sbo"]], force=True)
            fin_ops.append(f2)
        outd = []
        for tt in range(2):
            pm = [PA, PB][tt]
            trl = []
            for k in range(KD):
                tr = top(lambda e, k=k, t=tt, pmm=pm: e.transpose(
                    pmm[0:P, k * 128:(k + 1) * 128],
                    T["sfin"][:, k * PQ + t * 128:k * PQ + t * 128 + 128],
                    T["ident"][:]), wait=[fin_ops[k]] +
                    ([sc_ev[(k + 1) % KD]] if True else []))
                trl.append(tr)
            ev = vop(lambda e, pmm=pm: e.tensor_copy(T["sfint"][:], pmm[0:P, 0:1024]),
                     wait=trl + ([prev_outd] if prev_outd else []), force=True)
            od = dma("sync", out_h.ap()[pc + tt * 128:pc + (tt + 1) * 128, :],
                     T["sfint"][:], wait=[ev], sem="st")
            outd.append(od)
            prev_outd = od
            prev_outev = ev
        prev_fin = fin_ops[-1]

    if KDBG:
        dd1 = dma("sync", dbg_ar.ap(), cc_ar_out.ap(), wait=[prev_outd], sem="st")
        dd2 = dma("sync", dbg_ag.ap(), cc_ag_out.ap(), wait=[dd1], sem="st")
        dd3 = dma("sync", dbg_x2.ap(), x2_scr.ap(), wait=[dd2], sem="st")
        dd4 = dma("sync", dbg_sl.ap(), cc_sl_out.ap(), wait=[dd3], sem="st")
        prev_outd = dd4
    S.op("gpsimd", lambda e: e.engine_nop(), inc=("gA", 1), wait=[prev_outd])



    with nc.Block() as block:
        S.emit(block)
    for g in nc._sem_guards:
        g.__exit__(None, None, None)
    return nc


def _fix_sda_chain(S):
    va = 0
    for o in S.items:
        if o.eng == "scalar":
            o.deps.append(("vA", va, True))
        if o.sem == "vA":
            va = o.count_after


def _stage_core(I, core):
    b, g = core // 4, core % 4
    st = {}
    x = np.ascontiguousarray(I["L0_attn_x"][b].T)
    st["xT"] = x.astype(np.float32)
    WT = np.ascontiguousarray(I["L0_attn_in_proj_w"].T)
    st["wxT"] = np.ascontiguousarray(WT[:, g * DSL:(g + 1) * DSL])
    st["wzT"] = np.ascontiguousarray(WT[:, DI:2 * DI])
    dsl = slice(g * DSL, (g + 1) * DSL)
    cw = I["L0_attn_conv_w"][dsl, 0, :]
    st["cw_st"] = np.ascontiguousarray(cw.reshape(4, P, 4).transpose(1, 0, 2).reshape(P, 16))
    st["cb_st"] = np.ascontiguousarray(I["L0_attn_conv_b"][dsl].reshape(4, P).T)
    xp = I["L0_attn_x_proj_w"][:, dsl]
    st["xpT_st"] = np.ascontiguousarray(
        xp.T.reshape(4, P, 33).transpose(1, 0, 2).reshape(P, 132))
    st["dtw_st"] = np.ascontiguousarray(I["L0_attn_dt_proj_w"][dsl, 0].reshape(4, P).T)
    st["dtb_st"] = np.ascontiguousarray(I["L0_attn_dt_proj_b"][dsl].reshape(4, P).T)
    al = I["L0_attn_A_log"][dsl]
    st["Alog_st"] = np.ascontiguousarray(
        al.reshape(4, P, 16).transpose(1, 0, 2).reshape(P, 64))
    st["D_st"] = np.ascontiguousarray(I["L0_attn_D"][dsl].reshape(4, P).T)
    st["opT"] = np.ascontiguousarray(I["L0_attn_out_proj_w"].T)
    st["lng_st"] = np.ascontiguousarray(I["L0_attn_ln_gamma"].reshape(KD, P).T)
    st["lnb_st"] = np.ascontiguousarray(I["L0_attn_ln_beta"].reshape(KD, P).T)
    st["w1t"] = np.ascontiguousarray(I["L0_ffn_w1"][0:D])
    st["w1b"] = np.ascontiguousarray(I["L0_ffn_w1"][D:2 * D])
    st["b1_st"] = np.ascontiguousarray(I["L0_ffn_b1"].reshape(4, P).T)
    st["w2"] = np.ascontiguousarray(I["L0_ffn_w2"])
    st["b2_st"] = np.ascontiguousarray(I["L0_ffn_b2"].reshape(64, 1))
    st["tau"] = np.ascontiguousarray(I["L0_ffn_tau"].reshape(1, 1))
    st["wihT"] = np.ascontiguousarray(I["L0_ffn_Wih"].T)
    st["whhT"] = np.ascontiguousarray(I["L0_ffn_Whh"].T)
    st["bih_st"] = np.ascontiguousarray(I["L0_ffn_bih"].reshape(24, P).T)
    st["bhh_st"] = np.ascontiguousarray(I["L0_ffn_bhh"].reshape(24, P).T)
    st["wq"] = np.ascontiguousarray(I["L0_ffn_wq"])
    st["bq_st"] = np.ascontiguousarray(I["L0_ffn_bq"].reshape(2, P).T)
    st["wk"] = np.ascontiguousarray(I["L0_ffn_wk"])
    st["bk_st"] = np.ascontiguousarray(I["L0_ffn_bk"].reshape(2, P).T)
    st["wv"] = np.ascontiguousarray(I["L0_ffn_wv"])
    st["bv_row"] = np.ascontiguousarray(I["L0_ffn_bv"].reshape(1, D))
    st["fflng"] = np.ascontiguousarray(I["L0_ffn_ln_g"].reshape(1, D))
    st["fflnb"] = np.ascontiguousarray(I["L0_ffn_ln_b"].reshape(1, D))
    st["wvp"] = np.ascontiguousarray(I["L0_ffn_wvp"])
    st["bvp_st"] = np.ascontiguousarray(I["L0_ffn_bvp"].reshape(8, P).T)
    st["wo"] = np.ascontiguousarray(I["L0_ffn_wo"])
    st["bo_st"] = np.ascontiguousarray(I["L0_ffn_bo"].reshape(8, P).T)
    st["s0T"] = np.ascontiguousarray(I["L0_ffn_slot_init"].T)
    st["xTo"] = np.ascontiguousarray(st["xT"][:, g * TSL:(g + 1) * TSL])
    out = {k: np.asarray(v, dtype=np.float32) for k, v in st.items()}
    out["sown"] = np.array([[2 * g]], dtype=np.int32)
    return out


def kernel(**inputs):
    nc = build()
    in_maps = [_stage_core(inputs, c) for c in range(8)]
    res = run_bass_kernel_spmd(nc, in_maps, core_ids=list(range(8)))
    out = np.empty((B, L, D), np.float32)
    for c in range(8):
        b, g = c // 4, c % 4
        out[b, g * TSL:(g + 1) * TSL, :] = res.results[c]["out"]
    return out


# version 09 (final)
# speedup vs baseline: 1.8764x; 1.6434x over previous
import os
import numpy as np
import concourse.bass as bass
import concourse.mybir as mybir
from concourse.bass_utils import run_bass_kernel_spmd

F32 = mybir.dt.float32
BF16 = mybir.dt.bfloat16
AF = mybir.ActivationFunctionType
ALU = mybir.AluOpType
AX = mybir.AxisListType

P = 128
B, L, D, DI, NST = 2, 8192, 1024, 2048, 16
NCH = 8
Q = L // NCH
DSL = DI // 4
TSL = L // 4
PQ = 256
NPC = TSL // PQ
KD = D // P
RG = [[0, 1, 2, 3], [4, 5, 6, 7]]


class _Op:
    __slots__ = ("eng", "fn", "deps", "sem", "inc", "count_after",
                 "_inst", "_idx", "_eng_i", "_eff_eng", "_eff_dma")


class Sched:
    def __init__(self, nc, semnames):
        self.nc = nc
        self.items = []
        self.counts = {n: 0 for n in semnames}
        self.sems = {n: nc._sched_sems[n] for n in semnames}
        self.npool = sum(1 for n in semnames if n.startswith("dq"))
        self.dma_rr = 0
        self.last_pool_op = {}

    def op(self, eng, fn, inc=None, wait=(), force_wait=False):
        o = _Op()
        o.eng, o.fn = eng, fn
        o.deps = []
        for w in wait:
            if w is None:
                continue
            need = (force_wait or w.eng != eng or
                    (w.sem is not None and
                     (w.sem == "cc" or w.sem.startswith("dq"))))
            o.deps.append((w.sem, w.count_after, need))
        lvl = int(os.environ.get("KSYNC", "0"))
        kr = os.environ.get("KRANGE")
        if kr and eng in ("sync", "gpsimd"):
            lo, hi = map(int, kr.split(":"))
            if lo <= len(self.items) < hi:
                o.deps += [(s, self.counts[s], True) for s in ("vA", "sA")
                           if self.counts[s] > 0]
        if lvl == 2:
            o.deps = [(s, c, True) for s, c in self.counts.items() if c > 0]
        elif lvl == 14:
            if eng in ("sync", "gpsimd"):
                o.deps = [(s, c, True) for s, c in self.counts.items() if c > 0]
            elif eng == "tensor":
                o.deps += [(s, self.counts[s], True) for s in ("vA", "sA")
                           if self.counts[s] > 0]
            else:
                o.deps += [("tA", self.counts["tA"], True)] if self.counts["tA"] > 0 else []
        elif lvl == 1:
            if eng == "tensor":
                o.deps += [(s, self.counts[s], True) for s in ("vA", "sA")
                           if self.counts[s] > 0]
            elif eng in ("vector", "scalar"):
                o.deps += [("tA", self.counts["tA"], True)] if self.counts["tA"] > 0 else []
        o.sem, o.inc = None, 0
        if inc is not None:
            s, n = inc
            if s in ("ld", "st"):
                s = f"dq{self.dma_rr % self.npool}"
                self.dma_rr += 1
            if s == "cc" or s.startswith("dq"):
                prev = self.last_pool_op.get(s)
                if prev is not None:
                    o.deps.append((s, prev.count_after, True))
            self.counts[s] += n
            o.sem, o.inc, o.count_after = s, n, self.counts[s]
            if s == "cc" or s.startswith("dq"):
                self.last_pool_op[s] = o
        self.items.append(o)
        return o

    def emit(self, block):
        waited = {e: {} for e in ("sync", "scalar", "tensor", "vector", "gpsimd")}

        def mk(engname):
            def run(eng):
                w8 = waited[engname]
                for o in self.items:
                    if o.eng != engname:
                        continue
                    for (s, cnt, needed) in o.deps:
                        if needed and w8.get(s, 0) < cnt:
                            eng.wait_ge(self.sems[s], cnt)
                            w8[s] = cnt
                    ins = o.fn(eng)
                    if o.sem is not None:
                        ins.then_inc(self.sems[o.sem], o.inc)
            return run

        block.sync(mk("sync"))
        block.scalar(mk("scalar"))
        block.tensor(mk("tensor"))
        block.vector(mk("vector"))
        block.gpsimd(mk("gpsimd"))


def build():
    nc = bass.Bass("TRN2", target_bir_lowering=False, detect_race_conditions=False)
    E = {}

    def inp(name, shape):
        E[name] = nc.dram_tensor(name, list(shape), F32, kind="ExternalInput")
        return E[name]

    for nm, shp in [
            ("xT", [D, L]), ("wxT", [D, DSL]), ("wzT", [D, DI]),
            ("cw_st", [P, 16]), ("cb_st", [P, 4]), ("xpT_st", [P, 132]),
            ("dtw_st", [P, 4]), ("dtb_st", [P, 4]), ("Alog_st", [P, 64]),
            ("D_st", [P, 4]), ("opT", [DI, D]), ("lng_st", [P, 8]),
            ("lnb_st", [P, 8]), ("w1t", [D, 512]), ("w1b", [D, 512]),
            ("b1_st", [P, 4]), ("w2", [512, 64]), ("b2_st", [64, 1]),
            ("tau", [1, 1]), ("wihT", [D, 3072]), ("whhT", [D, 3072]),
            ("bih_st", [P, 24]), ("bhh_st", [P, 24]), ("wq", [D, 256]),
            ("bq_st", [P, 2]), ("wk", [D, 256]), ("bk_st", [P, 2]),
            ("wv", [D, D]), ("bv_row", [1, D]), ("fflng", [1, D]),
            ("fflnb", [1, D]), ("wvp", [D, D]), ("bvp_st", [P, 8]),
            ("wo", [D, D]), ("bo_st", [P, 8]), ("s0T", [D, 64]),
            ("xTo", [D, TSL])]:
        inp(nm, shp)
    E["sown"] = nc.dram_tensor("sown", [1, 1], mybir.dt.int32, kind="ExternalInput")

    out_h = nc.dram_tensor("out", [TSL, D], F32, kind="ExternalOutput")
    cc_ar_in = nc.dram_tensor("cc_ar_in", [NCH * 33, Q], F32)
    cc_ar_out = nc.dram_tensor("cc_ar_out", [NCH * 33, Q], F32)
    cc_ag_in = nc.dram_tensor("cc_ag_in", [NCH * DSL, Q], F32)
    cc_ag_out = nc.dram_tensor("cc_ag_out", [NCH * DI, Q], F32)
    x2_scr = nc.dram_tensor("x2_scr", [D, TSL], F32)
    cc_sl_in = nc.dram_tensor("cc_sl_in", [64, 1056], F32)
    cc_sl_out = nc.dram_tensor("cc_sl_out", [64, 1056], F32)
    KDBG = bool(os.environ.get("KDBG"))
    if KDBG:
        dbg_p2 = nc.dram_tensor("dbg_p2", [P, 4 * 8 * PQ], F32, kind="ExternalOutput")
        dbg_p3 = nc.dram_tensor("dbg_p3", [P, 3 * 16 * PQ], F32, kind="ExternalOutput")
        dbg_ar = nc.dram_tensor("dbg_ar", [NCH * 33, Q], F32, kind="ExternalOutput")
        dbg_ag = nc.dram_tensor("dbg_ag", [NCH * DI, Q], F32, kind="ExternalOutput")
        dbg_x2 = nc.dram_tensor("dbg_x2", [D, TSL], F32, kind="ExternalOutput")
        dbg_sl = nc.dram_tensor("dbg_sl", [64, 1056], F32, kind="ExternalOutput")

    SB = {}

    def sbp(name, shape, dt=F32):
        SB[name] = nc.alloc_sbuf_tensor(name, list(shape), dt)
        return SB[name]

    sbp("ident", [P, P])
    sbp("identb", [P, P], BF16)
    sbp("onesr", [1, P])
    sbp("onec", [P, 1])
    sbp("sel16", [16, 16 * P])
    sbp("fc", [P, 8])
    FC_EPS, FC_NHALF, FC_SC, FC_TENTH, FC_E8 = 0, 1, 2, 3, 4
    sbp("sdiag", [P, 16 * P], BF16)
    sbp("scw", [P, 16])
    sbp("scb", [P, 4])
    sbp("sxp", [P, 132])
    sbp("sdtw", [P, 4])
    sbp("sdtb", [P, 4])
    sbp("sA", [P, 64])
    sbp("sD", [P, 4])
    sbp("slng", [P, 8])
    sbp("slnb", [P, 8])
    sbp("scarry", [P, 64])
    sbp("salfb", [P, 16 * 64], BF16)
    sbp("swx", [P, 8 * 512], BF16)
    sbp("sw1t", [P, 8 * 512], BF16)
    sbp("sw2b", [P, 4 * 64], BF16)
    sbp("sbt1", [P, 4])
    sbp("sitau", [64, 2])
    sbp("sb2", [64, 1])
    sbp("sbihh", [P, 24])
    sbp("sbhh", [P, 24])
    sbp("sbq", [P, 2])
    sbp("sbk", [P, 2])
    sbp("sbvp", [P, 8])
    sbp("sbo", [P, 8])
    sbp("ssmean", [P, 8])
    sbp("sstg", [P, 3072])
    sbp("ssl", [64, 1056])
    SB["sown"] = nc.alloc_sbuf_tensor("sown_sb", [1, 1], mybir.dt.int32)

    RB = 150 * 1024
    r0, _ = nc.bump_sbuf(RB)

    def mkregion():
        ptr = [r0]

        def f(name, shape, dt=F32):
            nbytes = int(np.prod(shape[1:])) * (2 if dt == BF16 else 4)
            off = (ptr[0] + 31) // 32 * 32
            ptr[0] = off + nbytes
            assert ptr[0] <= r0 + RB, f"{name}: reuse overflow {ptr[0] - r0}"
            SB[name] = nc.alloc_sbuf_tensor_at(name, list(shape), dt, offset=off)
            return SB[name]
        return f

    sb1 = mkregion()
    sb1("sxT", [P, 2 * Q])
    sb1("sxq", [P, 2 * Q])
    sb1("sxnb", [P, 8 * Q], BF16)
    sb1("sxin", [P, 4 * (Q + 4)], BF16)
    sb1("sxc", [P, 4 * Q])
    sb1("sdt", [P, 4 * Q])
    sb1("sys", [P, 4 * Q])
    sb1("sBbc", [P, Q])
    sb1("sCbc", [P, Q])
    sb1("sdA", [P, Q])
    sb1("sdBx", [P, Q])
    sb1("sh", [P, Q])
    sb1("stmp", [P, Q])
    sb1("sx0r", [1, Q])
    sb1("sBr", [16, Q])
    sb1("sCr", [16, Q])
    sb1("sxde", [33, Q])
    sb1("sstat", [1, 2 * Q])
    sb1("svar", [1, Q])
    sb1("ssinv", [1, Q])

    sb2 = mkregion()
    sb2("swzT", [P, 8 * DI], BF16)
    sb2("sopT", [P, 16 * D], BF16)
    sb2("swp", [P, 16 * PQ])
    sb2("ssz", [P, 16 * PQ])
    sb2("sytb", [P, 16 * PQ], BF16)
    sb2("soat", [P, 8 * PQ])
    sb2("sxo", [P, 8 * PQ])
    sb2("sxn8", [P, 8 * PQ])
    sb2("sxnb2", [P, 8 * PQ], BF16)
    sb2("sxn2b", [P, 8 * PQ], BF16)
    sb2("sxn2t", [P, 2 * 1056], BF16)
    sb2("sh1b", [P, 4 * PQ], BF16)
    sb2("srt", [64, PQ])
    sb2("srtt", [P, 2 * 64])
    sb2("st8", [P, 8])
    sb2("stm1", [P, 1])
    sb2("sal", [P, 64])
    sb2("sals", [P, 1])
    sb2("sq2", [P, 2 * PQ])
    sb2("sstat2", [1, 2 * PQ])
    sb2("svar2", [1, PQ])
    sb2("ssinv2", [1, PQ])

    sb3 = mkregion()
    sb3("swkb", [P, 3072], BF16)
    sb3("sw3", [P, 8 * 1024], BF16)
    sb3("ssiT", [P, 8 * 64], BF16)
    sb3("ss0f", [P, 8 * 64])
    sb3("ss0b", [P, 8 * 64], BF16)
    sb3("sgr", [P, 8 * 64])
    sb3("sgz", [P, 8 * 64])
    sb3("sgn", [P, 8 * 64])
    sb3("sgt", [P, 8 * 64])
    sb3("ssnew", [P, 8 * 64])
    sb3("ssnb", [P, 8 * 64], BF16)
    sb3("sqkT", [P, 4 * 64], BF16)
    sb3("satm", [64, 64])
    sb3("satb", [64, 64], BF16)
    sb3("satT", [64, 64], BF16)
    sb3("svtok", [64, D])
    sb3("svtkb", [64, D], BF16)
    sb3("sy3", [64, D])
    sb3("sy3q", [64, D])
    sb3("sy3Tb", [P, 8 * 64], BF16)
    sb3("ssvb", [64, 8 * P], BF16)
    sb3("salfT", [64, 16 * P], BF16)
    sb3("ssc3", [64, 8])
    sb3("sbvb", [64, D])
    sb3("sgbb", [64, 2 * D])
    sb3("sx2p", [P, 8 * PQ])
    sb3("soutb", [P, 8 * PQ], BF16)
    sb3("sfin", [P, 8 * PQ])
    sb3("sfint", [P, D])

    PA = nc.alloc_psum_tensor("PA", [P, 1024], F32)
    PB = nc.alloc_psum_tensor("PB", [P, 1024], F32)
    PC = nc.alloc_psum_tensor("PC", [P, 1024], F32)
    PD = nc.alloc_psum_tensor("PD", [P, 1024], F32)

    NPOOL = 24
    semnames = (["tA", "vA", "sA", "gA", "cc"] +
                [f"dq{i}" for i in range(NPOOL)])
    nc._sched_sems = {}
    nc._sem_guards = []
    for n_ in semnames:
        g = nc.semaphore(n_)
        nc._sched_sems[n_] = g.__enter__()
        nc._sem_guards.append(g)

    S = Sched(nc, semnames)
    T = SB

    def dma(q, dst, src, wait=(), sem="ld"):
        return S.op(q, lambda e, d=dst, s=src: e.dma_start(out=d, in_=s),
                    inc=(sem, 16), wait=wait)

    def vop(fn, wait=(), force=False):
        return S.op("vector", fn, inc=("vA", 1), wait=wait, force_wait=force)

    def sop(fn, wait=(), force=False):
        return S.op("scalar", fn, inc=("sA", 1), wait=wait, force_wait=force)

    def top(fn, wait=(), force=False):
        return S.op("tensor", fn, inc=("tA", 1), wait=wait, force_wait=force)

    def gop(fn, wait=(), force=False):
        return S.op("gpsimd", fn, inc=("gA", 1), wait=wait, force_wait=force)

    SL = {}
    for nm, dst in [("cw_st", "scw"), ("cb_st", "scb"), ("xpT_st", "sxp"),
                    ("dtw_st", "sdtw"), ("dtb_st", "sdtb"), ("Alog_st", "sA"),
                    ("D_st", "sD"), ("lng_st", "slng"), ("lnb_st", "slnb"),
                    ("b2_st", "sb2"), ("bq_st", "sbq"), ("bk_st", "sbk"),
                    ("bvp_st", "sbvp"), ("bo_st", "sbo"), ("bih_st", "sbihh"),
                    ("bhh_st", "sbhh"), ("b1_st", "sbt1")]:
        SL[dst] = dma("sync", T[dst][:], E[nm].ap())
    ld_tau = dma("sync", T["fc"][0:1, 6:7], E["tau"].ap())

    g0 = gop(lambda e: e.memset(T["ident"][:], 0.0))
    g1 = gop(lambda e: e.affine_select(out=T["ident"][:], in_=T["ident"][:],
                                       compare_op=ALU.not_equal, fill=1.0, base=0,
                                       pattern=[[-1, P]], channel_multiplier=1),
             wait=[g0], force=True)
    gop(lambda e: e.memset(T["onesr"][:], 1.0))
    gop(lambda e: e.memset(T["onec"][:], 1.0))
    gs0 = gop(lambda e: e.memset(T["sel16"][:], 0.0))
    gop(lambda e: e.affine_select(
        out=T["sel16"][:].rearrange("c (b j) -> c b j", b=16),
        in_=T["sel16"][:].rearrange("c (b j) -> c b j", b=16),
        compare_op=ALU.not_equal, fill=1.0, base=0,
        pattern=[[-1, 16], [0, P]], channel_multiplier=1),
        wait=[gs0], force=True)
    for col, val in [(FC_EPS, 1e-5), (FC_NHALF, -0.5), (FC_SC, 256.0 ** -0.5),
                     (FC_TENTH, 0.1), (FC_E8, 1e-8)]:
        gop(lambda e, cc=col, v=val: e.memset(T["fc"][:, cc:cc + 1], v))
    g_all = S.op("gpsimd", lambda e: e.engine_nop(), inc=("gA", 1))

    v_idb = vop(lambda e: e.tensor_copy(T["identb"][:], T["ident"][:]), wait=[g1])
    s_negA = sop(lambda e: e.activation(out=T["sA"][:], in_=T["sA"][:], func=AF.Exp),
                 wait=[SL["sA"], g_all])
    v_negA = vop(lambda e: e.tensor_scalar_mul(T["sA"][:], T["sA"][:], -1.0),
                 wait=[s_negA])
    diag_ops = []
    for j in range(16):
        diag_ops.append(vop(lambda e, j=j: e.tensor_scalar_mul(
            T["sdiag"][:, j * P:(j + 1) * P], T["ident"][:], T["scw"][:, j:j + 1]),
            wait=[g1, SL["scw"]]))
    v_bih = vop(lambda e: e.tensor_add(T["sbihh"][:, 0:16], T["sbihh"][:, 0:16],
                                       T["sbhh"][:, 0:16]), wait=[SL["sbihh"], SL["sbhh"]])
    s_ab = sop(lambda e: e.activation(out=T["fc"][0:1, 7:8], in_=T["fc"][0:1, 6:7],
                                      func=AF.Abs), wait=[ld_tau, g_all])
    v_ab = vop(lambda e: e.tensor_add(T["fc"][0:1, 7:8], T["fc"][0:1, 7:8],
                                      T["fc"][0:1, FC_TENTH:FC_TENTH + 1]), wait=[s_ab, g_all])
    v_rc = vop(lambda e: e.reciprocal(T["fc"][0:1, 7:8], T["fc"][0:1, 7:8]),
               wait=[v_ab], force=True)
    t_it = top(lambda e: e.matmul(PA[0:64, 0:1], T["onesr"][0:1, 0:64],
                                  T["fc"][0:1, 7:8], start=True, stop=True),
               wait=[v_rc, g_all])
    v_it = vop(lambda e: e.tensor_copy(T["sitau"][:, 0:1], PA[0:64, 0:1]), wait=[t_it])
    v_it2 = vop(lambda e: e.tensor_mul(T["sitau"][:, 1:2], T["sb2"][:],
                                       T["sitau"][:, 0:1]),
                wait=[v_it, SL["sb2"]], force=True)

    def stream_cast(dram_ap, dst_name, kt, cols, prev=None, dst_coloff=0):
        ops = []
        lastop = prev
        for k in range(kt):
            ldk = dma("sync", T["sstg"][:, 0:cols], dram_ap[k * P:(k + 1) * P, :],
                      wait=[lastop] if lastop else [])
            ck = vop(lambda e, k=k: e.tensor_copy(
                T[dst_name][:, dst_coloff + k * cols:dst_coloff + (k + 1) * cols],
                T["sstg"][:, 0:cols]), wait=[ldk])
            ops.append(ck)
            lastop = ck
        return ops

    wx_ops = stream_cast(E["wxT"].ap(), "swx", 8, 512)
    w1_ops = stream_cast(E["w1t"].ap(), "sw1t", 8, 512, prev=wx_ops[-1])
    w2_ops = stream_cast(E["w2"].ap(), "sw2b", 4, 64, prev=w1_ops[-1])

    prevc = w2_ops[-1]
    sm_ops = []
    for k in range(KD):
        ldk = dma("sync", T["sstg"][:, 0:64], E["s0T"].ap()[k * P:(k + 1) * P, :],
                  wait=[prevc])
        vr = vop(lambda e, k=k: e.reduce_sum(T["ssmean"][:, k:k + 1],
                                             T["sstg"][:, 0:64], axis=AX.X), wait=[ldk])
        sm_ops.append(vr)
        prevc = vr
    v_sm = vop(lambda e: e.tensor_scalar_mul(T["ssmean"][:], T["ssmean"][:], 1.0 / 64.0),
               wait=[prevc], force=True)
    c1_last = []
    prevc = v_sm
    for k in range(KD):
        ldk = dma("sync", T["sstg"][:, 0:512], E["w1b"].ap()[k * P:(k + 1) * P, :],
                  wait=[prevc])
        for mt in range(4):
            mm = top(lambda e, k=k, m=mt: e.matmul(
                PB[0:P, m:m + 1], T["sstg"][:, m * 128:(m + 1) * 128],
                T["ssmean"][:, k:k + 1], start=(k == 0), stop=(k == KD - 1)),
                wait=[ldk, v_sm])
            if k == KD - 1:
                c1_last.append(mm)
        prevc = mm
    v_bt1 = vop(lambda e: e.tensor_add(T["sbt1"][:], T["sbt1"][:], PB[0:P, 0:4]),
                wait=c1_last + [SL["sbt1"]])

    prev_xn_last = None
    prev_sq = None
    prev_scan_end = None
    prev_dt_last = None
    prev_wds = []
    ag_ops = []
    for c in range(NCH):
        C0 = c * Q
        mu_mms, sq_mms, tile_lds = [], [], []
        for k in range(KD):
            half = (k % 2) * Q
            wl = [wx_ops[-1], v_bt1]
            if len(mu_mms) >= 2:
                wl += [mu_mms[-2], sq_mms[-2]]
            if k < 2 and prev_xn_last is not None:
                wl.append(prev_xn_last)
            ldk = dma("sync", T["sxT"][:, half:half + Q],
                      E["xT"].ap()[k * P:(k + 1) * P, C0:C0 + Q], wait=wl)
            tile_lds.append(ldk)
            sqk = sop(lambda e, h=half: e.activation(
                out=T["sxq"][:, h:h + Q], in_=T["sxT"][:, h:h + Q], func=AF.Square),
                wait=[ldk] + ([prev_sq] if prev_sq else []), force=True)
            prev_sq = sqk
            for hf in range(2):
                wl2 = [ldk, g_all]
                if c > 0 and k == 0:
                    wl2 += [prev_scan_end, prev_dt_last]
                mu_mms.append(top(lambda e, h=half, f=hf, kk=k: e.matmul(
                    PA[0:1, f * 512:(f + 1) * 512], T["onec"][:],
                    T["sxT"][:, h + f * 512:h + (f + 1) * 512],
                    start=(kk == 0), stop=(kk == KD - 1)), wait=wl2))
                sq_mms.append(top(lambda e, h=half, f=hf, kk=k: e.matmul(
                    PB[0:1, f * 512:(f + 1) * 512], T["onec"][:],
                    T["sxq"][:, h + f * 512:h + (f + 1) * 512],
                    start=(kk == 0), stop=(kk == KD - 1)), wait=[sqk]))
        v_mu = vop(lambda e: e.tensor_scalar_mul(T["sstat"][0:1, 0:Q], PA[0:1, 0:Q],
                                                 1.0 / D), wait=mu_mms[-2:])
        v_e2 = vop(lambda e: e.tensor_scalar_mul(T["sstat"][0:1, Q:2 * Q], PB[0:1, 0:Q],
                                                 1.0 / D), wait=sq_mms[-2:])
        v_va = vop(lambda e: e.tensor_mul(T["svar"][0:1, :], T["sstat"][0:1, 0:Q],
                                          T["sstat"][0:1, 0:Q]), wait=[v_mu])
        v_vb = vop(lambda e: e.tensor_sub(T["svar"][0:1, :], T["sstat"][0:1, Q:2 * Q],
                                          T["svar"][0:1, :]), wait=[v_va, v_e2], force=True)
        s_ln = sop(lambda e: e.activation(out=T["svar"][0:1, :], in_=T["svar"][0:1, :],
                                          func=AF.Ln, bias=T["fc"][0:1, FC_EPS:FC_EPS + 1]),
                   wait=[v_vb])
        s_si = sop(lambda e: e.activation(out=T["ssinv"][0:1, :], in_=T["svar"][0:1, :],
                                          func=AF.Exp,
                                          scale=T["fc"][0:1, FC_NHALF:FC_NHALF + 1]),
                   wait=[s_ln], force=True)
        bc_mu, bc_si = [], []
        for hf in range(2):
            bc_mu.append(top(lambda e, f=hf: e.matmul(
                PC[0:P, f * 512:(f + 1) * 512], T["onesr"][:],
                T["sstat"][0:1, f * 512:(f + 1) * 512], start=True, stop=True),
                wait=[v_mu, v_vb, prev_scan_end]))
            bc_si.append(top(lambda e, f=hf: e.matmul(
                PD[0:P, f * 512:(f + 1) * 512], T["onesr"][:],
                T["ssinv"][0:1, f * 512:(f + 1) * 512], start=True, stop=True),
                wait=[s_si]))
        xn_ops = []
        for k in range(KD):
            half = (k % 2) * Q
            wl = [mu_mms[-1], sq_mms[-1]]
            if len(xn_ops) >= 2:
                wl.append(xn_ops[-2])
            ldk = dma("sync", T["sxT"][:, half:half + Q],
                      E["xT"].ap()[k * P:(k + 1) * P, C0:C0 + Q], wait=wl)
            v1 = vop(lambda e, h=half: e.tensor_sub(
                T["sxT"][:, h:h + Q], T["sxT"][:, h:h + Q], PC[0:P, 0:Q]),
                wait=[ldk] + bc_mu)
            v2 = vop(lambda e, h=half, k=k: e.tensor_tensor(
                out=T["sxnb"][:, k * Q:(k + 1) * Q], in0=T["sxT"][:, h:h + Q],
                in1=PD[0:P, 0:Q], op=ALU.mult), wait=[v1] + bc_si, force=True)
            xn_ops.append(v2)
        prev_xn_last = xn_ops[-1]
        halo_ops = []
        for dt_ in range(4):
            base = dt_ * (Q + 4)
            if c == 0:
                halo_ops.append(vop(lambda e, b=base: e.memset(T["sxin"][:, b:b + 4], 0.0)))
            else:
                halo_ops.append(vop(lambda e, b=base: e.tensor_copy(
                    T["sxin"][:, b:b + 4], T["sxin"][:, b + Q:b + Q + 4])))
        xi_evac = []
        for mt in range(4):
            pm = [PA, PB][mt % 2]
            mml = []
            for hf in range(2):
                for k in range(KD):
                    wl = [xn_ops[k], wx_ops[-1]]
                    if mt < 2:
                        wl.append([v_mu, v_e2][mt])
                    if len(xi_evac) >= 2:
                        wl.append(xi_evac[-2])
                    mm = top(lambda e, m=mt, f=hf, k=k, p=pm: e.matmul(
                        p[0:P, f * 512:(f + 1) * 512],
                        T["swx"][:, k * 512 + m * 128:k * 512 + (m + 1) * 128],
                        T["sxnb"][:, k * Q + f * 512:k * Q + (f + 1) * 512],
                        start=(k == 0), stop=(k == KD - 1)), wait=wl)
                mml.append(mm)
            ev = vop(lambda e, m=mt, p=pm: e.tensor_copy(
                T["sxin"][:, m * (Q + 4) + 4:(m + 1) * (Q + 4)], p[0:P, 0:Q]),
                wait=mml + halo_ops[mt:mt + 1])
            xi_evac.append(ev)
        xc_ops = []
        for dt_ in range(4):
            pm = [PC, PD][dt_ % 2]
            base = dt_ * (Q + 4)
            mml = []
            for hf in range(2):
                for k_ in range(4):
                    wl = [xi_evac[dt_], v_idb]
                    if dt_ < 2:
                        wl.append(xn_ops[-1])
                    if len(xc_ops) >= 2:
                        wl.append(xc_ops[-2])
                    mm = top(lambda e, d=dt_, f=hf, k=k_, p=pm, b=base: e.matmul(
                        p[0:P, f * 512:(f + 1) * 512],
                        T["sdiag"][:, (d * 4 + k) * P:(d * 4 + k + 1) * P],
                        T["sxin"][:, b + 1 + k + f * 512:b + 1 + k + f * 512 + 512],
                        start=(k == 0), stop=(k == 3)), wait=wl)
                mml.append(mm)
            sv = sop(lambda e, d=dt_, p=pm: e.activation(
                out=T["sxc"][:, d * Q:(d + 1) * Q], in_=p[0:P, 0:Q], func=AF.Silu,
                bias=T["scb"][:, d:d + 1]), wait=mml + [SL["scb"]])
            xc_ops.append(sv)
        xp_mm = []
        for hf in range(2):
            for dt_ in range(4):
                mm = top(lambda e, d=dt_, f=hf: e.matmul(
                    PA[0:33, f * 512:(f + 1) * 512],
                    T["sxp"][:, d * 33:(d + 1) * 33],
                    T["sxc"][:, d * Q + f * 512:d * Q + (f + 1) * 512],
                    start=(d == 0), stop=(d == 3)),
                    wait=[xc_ops[dt_], SL["sxp"], xi_evac[0], xi_evac[2], v_mu])
            xp_mm.append(mm)
        v_xd = vop(lambda e: e.tensor_copy(T["sxde"][:], PA[0:33, 0:Q]), wait=xp_mm)
        d_xd = dma("sync", cc_ar_in.ap()[c * 33:(c + 1) * 33, :], T["sxde"][:],
                   wait=[v_xd], sem="st")
        g_ar = S.op("gpsimd", lambda e, cc_=c: e.collective_compute(
            "AllReduce", ALU.add, replica_groups=RG,
            ins=[cc_ar_in.ap()[cc_ * 33:(cc_ + 1) * 33, :]],
            outs=[cc_ar_out.ap()[cc_ * 33:(cc_ + 1) * 33, :]]),
            inc=("cc", 1), wait=[d_xd])
        d_x0 = dma("sync", T["sx0r"][:], cc_ar_out.ap()[c * 33:c * 33 + 1, :],
                   wait=[g_ar, prev_scan_end])
        d_Br = dma("sync", T["sBr"][:], cc_ar_out.ap()[c * 33 + 1:c * 33 + 17, :],
                   wait=[g_ar, prev_scan_end])
        d_Cr = dma("sync", T["sCr"][:], cc_ar_out.ap()[c * 33 + 17:c * 33 + 33, :],
                   wait=[g_ar, prev_scan_end])
        t_x0 = []
        for hf in range(2):
            t_x0.append(top(lambda e, f=hf: e.matmul(
                PB[0:P, f * 512:(f + 1) * 512], T["onesr"][:],
                T["sx0r"][0:1, f * 512:(f + 1) * 512], start=True, stop=True),
                wait=[d_x0, v_e2, xi_evac[3]]))
        dt_ops = []
        for dt_ in range(4):
            s1 = sop(lambda e, d=dt_: e.activation(
                out=T["sdt"][:, d * Q:(d + 1) * Q], in_=PB[0:P, 0:Q], func=AF.Exp,
                scale=T["sdtw"][:, d:d + 1], bias=T["sdtb"][:, d:d + 1]),
                wait=t_x0 + [SL["sdtw"], SL["sdtb"]])
            s2 = sop(lambda e, d=dt_: e.activation(
                out=T["sdt"][:, d * Q:(d + 1) * Q], in_=T["sdt"][:, d * Q:(d + 1) * Q],
                func=AF.Ln, bias=1.0), wait=[s1], force=True)
            dt_ops.append(s2)
        prev_dt_last = dt_ops[-1]
        ys_init, udt_ops = [], []
        for dt_ in range(4):
            wl = [xc_ops[dt_], xp_mm[-1], SL["sD"]]
            if c > 0:
                wl += prev_wds[dt_:dt_ + 1]
            yi = vop(lambda e, d=dt_: e.tensor_scalar_mul(
                T["sys"][:, d * Q:(d + 1) * Q], T["sxc"][:, d * Q:(d + 1) * Q],
                T["sD"][:, d:d + 1]), wait=wl)
            ys_init.append(yi)
            ud = vop(lambda e, d=dt_: e.tensor_tensor(
                out=T["sxc"][:, d * Q:(d + 1) * Q], in0=T["sxc"][:, d * Q:(d + 1) * Q],
                in1=T["sdt"][:, d * Q:(d + 1) * Q], op=ALU.mult),
                wait=[yi, dt_ops[dt_]], force=True)
            udt_ops.append(ud)
        last_accum = [None] * 4
        prev_vb = prev_vc = None
        for n_ in range(16):
            wbc = [d_Br, v_xd, prev_vb]
            if n_ == 0:
                wbc += [v_mu]
            tb1 = top(lambda e, n=n_: e.matmul(
                PA[0:P, 0:512], T["sel16"][:, n * P:(n + 1) * P],
                T["sBr"][:, 0:512], start=True, stop=True), wait=wbc)
            tb2 = top(lambda e, n=n_: e.matmul(
                PA[0:P, 512:1024], T["sel16"][:, n * P:(n + 1) * P],
                T["sBr"][:, 512:1024], start=True, stop=True), wait=[d_Br, prev_vb])
            tc1 = top(lambda e, n=n_: e.matmul(
                PC[0:P, 0:512], T["sel16"][:, n * P:(n + 1) * P],
                T["sCr"][:, 0:512], start=True, stop=True),
                wait=[d_Cr, xc_ops[2], xc_ops[3], prev_vc])
            tc2 = top(lambda e, n=n_: e.matmul(
                PC[0:P, 512:1024], T["sel16"][:, n * P:(n + 1) * P],
                T["sCr"][:, 512:1024], start=True, stop=True), wait=[d_Cr, prev_vc])
            vb = vop(lambda e: e.tensor_copy(T["sBbc"][:], PA[0:P, 0:Q]), wait=[tb1, tb2])
            vc = vop(lambda e: e.tensor_copy(T["sCbc"][:], PC[0:P, 0:Q]), wait=[tc1, tc2])
            prev_vb, prev_vc = vb, vc
            for dt_ in range(4):
                cidx = dt_ * 16 + n_
                sda = sop(lambda e, d=dt_, n=n_: e.activation(
                    out=T["sdA"][:], in_=T["sdt"][:, d * Q:(d + 1) * Q], func=AF.Exp,
                    scale=T["sA"][:, d * 16 + n:d * 16 + n + 1]),
                    wait=[dt_ops[dt_], v_negA])
                vdx = vop(lambda e, d=dt_: e.tensor_tensor(
                    out=T["sdBx"][:], in0=T["sxc"][:, d * Q:(d + 1) * Q],
                    in1=T["sBbc"][:], op=ALU.mult), wait=[udt_ops[dt_], vb], force=True)
                init = 0.0 if c == 0 else T["scarry"][:, cidx:cidx + 1]
                vsc = vop(lambda e, i=init: e.tensor_tensor_scan(
                    out=T["sh"][:], data0=T["sdA"][:], data1=T["sdBx"][:],
                    initial=i, op0=ALU.mult, op1=ALU.add),
                    wait=[sda, vdx], force=True)
                vcp = vop(lambda e, ci=cidx: e.tensor_copy(
                    T["scarry"][:, ci:ci + 1], T["sh"][:, Q - 1:Q]),
                    wait=[vsc], force=True)
                vtm = vop(lambda e: e.tensor_mul(T["stmp"][:], T["sh"][:], T["sCbc"][:]),
                          wait=[vsc, vc], force=True)
                vac = vop(lambda e, d=dt_: e.tensor_add(
                    T["sys"][:, d * Q:(d + 1) * Q], T["sys"][:, d * Q:(d + 1) * Q],
                    T["stmp"][:]), wait=[vtm, ys_init[dt_]], force=True)
                last_accum[dt_] = vac
        prev_scan_end = last_accum[3]
        wds = []
        for dt_ in range(4):
            wds.append(dma("sync",
                           cc_ag_in.ap()[c * DSL + dt_ * P:c * DSL + (dt_ + 1) * P, :],
                           T["sys"][:, dt_ * Q:(dt_ + 1) * Q],
                           wait=[last_accum[dt_]], sem="st"))
        prev_wds = wds
        g_ag = S.op("gpsimd", lambda e, cc_=c: e.collective_compute(
            "AllGather", ALU.bypass, replica_groups=RG,
            ins=[cc_ag_in.ap()[cc_ * DSL:(cc_ + 1) * DSL, :]],
            outs=[cc_ag_out.ap()[cc_ * DI:(cc_ + 1) * DI, :]]),
            inc=("cc", 1), wait=wds)
        ag_ops.append(g_ag)

    _fix_sda_chain(S)

    p1_done = ag_ops[-1]

    p1v = prev_scan_end
    p1w = prev_wds[-1]
    p1s = prev_dt_last

    def stream_cast2(dram_ap, dst_name, kt, cols, prev, coloff=0, guard=()):
        ops = []
        lastop = prev
        for k in range(kt):
            ldk = dma("sync", T["sstg"][:, 0:cols], dram_ap[k * P:(k + 1) * P, :],
                      wait=[lastop])
            ck = vop(lambda e, k=k, c=cols, co=coloff: e.tensor_copy(
                T[dst_name][:, co + k * c:co + (k + 1) * c], T["sstg"][:, 0:c]),
                wait=[ldk] + (list(guard) if k == 0 else []))
            ops.append(ck)
            lastop = ck
        return ops

    guard2 = [p1v, p1w, p1s]
    wz_ops = stream_cast2(E["wzT"].ap(), "swzT", 8, DI, prev=prev_wds[-1], guard=guard2)
    op_ops = stream_cast2(E["opT"].ap(), "sopT", 16, D, prev=wz_ops[-1], guard=guard2)
    v_slz = vop(lambda e: e.memset(T["ssl"][:], 0.0), wait=[p1v])

    d_own = S.op("gpsimd", lambda e: e.dma_start(out=T["sown"][:], in_=E["sown"].ap()),
                 inc=("ld", 16))
    v_own = S.op("gpsimd",
                 lambda e: setattr(nc, "_own_val", e.value_load(T["sown"][0:1, 0:1])),
                 inc=None, wait=[d_own])

    slot_mm_all = []
    prev_piece = {}
    for p in range(NPC):
        pc = p * PQ
        wl = [ag_ops[-1], op_ops[-1]]
        if p > 0:
            wl.append(prev_piece["ytb"])
        tc = pc % Q
        cadd = pc // Q
        if os.environ.get("KWFIX"):
            wl = wl + [type("W", (), {"sem": s, "count_after": c, "eng": "x"})()
                       for s, c in S.counts.items() if c > 0]
        if os.environ.get("KSTATIC"):
            d_w1 = S.op("sync", lambda e, t0=tc, ca=cadd: e.dma_start(
                out=T["swp"][:].rearrange("p (b t) -> p b t", b=16),
                in_=cc_ag_out.ap().rearrange("(c b p) t -> p b c t", p=P, b=16)
                    [:, :, ca:ca + 1, t0:t0 + PQ].squeeze(2)),
                inc=("ld", 16), wait=wl)
        else:
            d_w1 = S.op("gpsimd", lambda e, t0=tc, ca=cadd: e.dma_start(
                out=T["swp"][:].rearrange("p (b t) -> p b t", b=16),
                in_=cc_ag_out.ap().rearrange("(c b p) t -> p b c t", p=P, b=16)
                    [:, :, bass.ds(nc._own_val + ca, 1), t0:t0 + PQ].squeeze(2)),
                inc=("ld", 16), wait=wl)
        d_w = [d_w1] * 16
        d_x = []
        for k in range(KD):
            wl = [wz_ops[-1]]
            if p > 0:
                wl += prev_piece["x2d"]
            d_x.append(dma("sync", T["sxo"][:, k * PQ:(k + 1) * PQ],
                           E["xTo"].ap()[k * P:(k + 1) * P, pc:pc + PQ], wait=wl))

        if KDBG and p == 0:
            for mi in range(16):
                dma("sync", dbg_p3.ap()[:, mi * PQ:(mi + 1) * PQ],
                    T["swp"][:, mi * PQ:(mi + 1) * PQ], wait=[d_w[0]], sem="st")

        def piece_ln(src_name, dst_f32, dst_b16, guard_v, guard_s, tag):
            mus, sqs = [], []
            prev_sq2 = guard_s
            for k in range(KD):
                half = (k % 2) * PQ
                sq = sop(lambda e, k=k, h=half, s=src_name: e.activation(
                    out=T["sq2"][:, h:h + PQ], in_=T[s][:, k * PQ:(k + 1) * PQ],
                    func=AF.Square),
                    wait=guard_v + ([prev_sq2] if prev_sq2 else []) +
                         ([sqs[-2]] if len(sqs) >= 2 else []),
                    force=True)
                prev_sq2 = sq
                mus.append(top(lambda e, k=k, s=src_name: e.matmul(
                    PA[0:1, 0:PQ], T["onec"][:], T[s][:, k * PQ:(k + 1) * PQ],
                    start=(k == 0), stop=(k == KD - 1)), wait=guard_v))
                sqs.append(top(lambda e, h=half, kk=k: e.matmul(
                    PB[0:1, 0:PQ], T["onec"][:], T["sq2"][:, h:h + PQ],
                    start=(kk == 0), stop=(kk == KD - 1)), wait=[sq]))
            vmu = vop(lambda e: e.tensor_scalar_mul(T["sstat2"][0:1, 0:PQ],
                                                    PA[0:1, 0:PQ], 1.0 / D),
                      wait=mus[-1:])
            ve2 = vop(lambda e: e.tensor_scalar_mul(T["sstat2"][0:1, PQ:2 * PQ],
                                                    PB[0:1, 0:PQ], 1.0 / D),
                      wait=sqs[-1:])
            vva = vop(lambda e: e.tensor_mul(T["svar2"][0:1, :], T["sstat2"][0:1, 0:PQ],
                                             T["sstat2"][0:1, 0:PQ]), wait=[vmu])
            vvb = vop(lambda e: e.tensor_sub(T["svar2"][0:1, :],
                                             T["sstat2"][0:1, PQ:2 * PQ],
                                             T["svar2"][0:1, :]),
                      wait=[vva, ve2], force=True)
            sln = sop(lambda e: e.activation(out=T["svar2"][0:1, :], in_=T["svar2"][0:1, :],
                                             func=AF.Ln,
                                             bias=T["fc"][0:1, FC_EPS:FC_EPS + 1]),
                      wait=[vvb])
            ssi = sop(lambda e: e.activation(out=T["ssinv2"][0:1, :], in_=T["svar2"][0:1, :],
                                             func=AF.Exp,
                                             scale=T["fc"][0:1, FC_NHALF:FC_NHALF + 1]),
                      wait=[sln], force=True)
            bmu = top(lambda e: e.matmul(PA[0:P, 0:PQ], T["onesr"][:],
                                         T["sstat2"][0:1, 0:PQ], start=True, stop=True),
                      wait=[vmu, vvb])
            bsi = top(lambda e: e.matmul(PB[0:P, 0:PQ], T["onesr"][:],
                                         T["ssinv2"][0:1, 0:PQ], start=True, stop=True),
                      wait=[ssi, ve2])
            outs = []
            for k in range(KD):
                v1 = vop(lambda e, k=k, s=src_name, d=dst_f32: e.tensor_sub(
                    T[d][:, k * PQ:(k + 1) * PQ], T[s][:, k * PQ:(k + 1) * PQ],
                    PA[0:P, 0:PQ]), wait=[bmu])
                if dst_b16 is None:
                    v2 = vop(lambda e, k=k, d=dst_f32: e.tensor_tensor(
                        out=T[d][:, k * PQ:(k + 1) * PQ],
                        in0=T[d][:, k * PQ:(k + 1) * PQ], in1=PB[0:P, 0:PQ],
                        op=ALU.mult), wait=[v1, bsi], force=True)
                else:
                    v2 = vop(lambda e, k=k, d=dst_f32: e.tensor_tensor(
                        out=T[d][:, k * PQ:(k + 1) * PQ],
                        in0=T[d][:, k * PQ:(k + 1) * PQ], in1=PB[0:P, 0:PQ],
                        op=ALU.mult), wait=[v1, bsi], force=True)
                    v2 = vop(lambda e, k=k, d=dst_f32, b=dst_b16: e.tensor_copy(
                        T[b][:, k * PQ:(k + 1) * PQ], T[d][:, k * PQ:(k + 1) * PQ]),
                        wait=[v2], force=True)
                outs.append(v2)
            return outs, (vmu, ve2, bmu, bsi, sqs[-1], mus[-1])

        guard_v = [dd for dd in d_x]
        if p > 0:
            guard_v += [prev_piece["vrt"], prev_piece["slot"]]
        gs = prev_piece.get("lastsq")
        xn_o, st1 = piece_ln("sxo", "sxn8", "sxnb2", guard_v, gs, "ln1")
        z_ops = []
        for mi in range(16):
            pm = [PC, PD][mi % 2]
            mml = []
            for k in range(KD):
                wl = [xn_o[k], wz_ops[-1]]
                if len(z_ops) >= 2:
                    wl.append(z_ops[-2])
                if mi < 2:
                    wl += [st1[2], st1[3]]
                mm = top(lambda e, m=mi, k=k, p=pm: e.matmul(
                    p[0:P, 0:PQ], T["swzT"][:, k * DI + m * 128:k * DI + (m + 1) * 128],
                    T["sxnb2"][:, k * PQ:(k + 1) * PQ],
                    start=(k == 0), stop=(k == KD - 1)), wait=wl)
                mml.append(mm)
            sz_ = sop(lambda e, m=mi, p=pm: e.activation(
                out=T["ssz"][:, m * PQ:(m + 1) * PQ], in_=p[0:P, 0:PQ], func=AF.Silu),
                wait=mml)
            z_ops.append(sz_)
        if KDBG and p == 0:
            for mi in range(16):
                dma("sync", dbg_p3.ap()[:, 4096 + mi * PQ:4096 + (mi + 1) * PQ],
                    T["ssz"][:, mi * PQ:(mi + 1) * PQ], wait=[z_ops[mi]], sem="st")
        yt_ops = []
        for mi in range(16):
            yt_ops.append(vop(lambda e, m=mi: e.tensor_tensor(
                out=T["sytb"][:, m * PQ:(m + 1) * PQ],
                in0=T["swp"][:, m * PQ:(m + 1) * PQ],
                in1=T["ssz"][:, m * PQ:(m + 1) * PQ], op=ALU.mult),
                wait=[z_ops[mi], d_w[mi]]))
        ytb_done = yt_ops[-1]
        oat_ops = []
        for mi in range(KD):
            pm = [PA, PB][mi % 2]
            mml = []
            for k in range(16):
                wl = [yt_ops[k], op_ops[-1]]
                if mi < 2:
                    wl += [xn_o[-1], st1[0], st1[1]]
                if len(oat_ops) >= 2:
                    wl.append(oat_ops[-2])
                mm = top(lambda e, m=mi, k=k, p=pm: e.matmul(
                    p[0:P, 0:PQ], T["sopT"][:, k * D + m * 128:k * D + (m + 1) * 128],
                    T["sytb"][:, k * PQ:(k + 1) * PQ],
                    start=(k == 0), stop=(k == 15)), wait=wl)
                mml.append(mm)
            ev = vop(lambda e, m=mi, p=pm: e.tensor_copy(
                T["soat"][:, m * PQ:(m + 1) * PQ], p[0:P, 0:PQ]),
                wait=mml + (prev_piece["x2d"] if p > 0 else []))
            oat_ops.append(ev)
        if KDBG and p == 0:
            for k in range(KD):
                dma("sync", dbg_p2.ap()[:, 0 * 2048 + k * PQ:0 * 2048 + (k + 1) * PQ],
                    T["soat"][:, k * PQ:(k + 1) * PQ], wait=[oat_ops[k]], sem="st")
                dma("sync", dbg_p2.ap()[:, 3 * 2048 + k * PQ:3 * 2048 + (k + 1) * PQ],
                    T["sxn8"][:, k * PQ:(k + 1) * PQ], wait=[xn_o[k]], sem="st")
        o_ops = []
        for k in range(KD):
            o_ops.append(vop(lambda e, k=k: e.tensor_add(
                T["soat"][:, k * PQ:(k + 1) * PQ], T["soat"][:, k * PQ:(k + 1) * PQ],
                T["sxn8"][:, k * PQ:(k + 1) * PQ]), wait=[oat_ops[k], xn_o[k]],
                force=True))
        if KDBG and p == 0:
            for k in range(KD):
                dma("sync", dbg_p2.ap()[:, 1 * 2048 + k * PQ:1 * 2048 + (k + 1) * PQ],
                    T["soat"][:, k * PQ:(k + 1) * PQ], wait=[o_ops[k]], sem="st")
        at_o, st2 = piece_ln("soat", "soat", None, o_ops, st1[4], "lno")
        gb_ops = []
        for k in range(KD):
            gb_ops.append(vop(lambda e, k=k: e.tensor_scalar(
                out=T["soat"][:, k * PQ:(k + 1) * PQ],
                in0=T["soat"][:, k * PQ:(k + 1) * PQ],
                scalar1=T["slng"][:, k:k + 1], scalar2=T["slnb"][:, k:k + 1],
                op0=ALU.mult, op1=ALU.add), wait=[at_o[k], SL["slng"], SL["slnb"]],
                force=True))
        if KDBG and p == 0:
            for k in range(KD):
                dma("sync", dbg_p2.ap()[:, 2 * 2048 + k * PQ:2 * 2048 + (k + 1) * PQ],
                    T["soat"][:, k * PQ:(k + 1) * PQ], wait=[gb_ops[k]], sem="st")
        x2_ops = []
        for k in range(KD):
            x2_ops.append(vop(lambda e, k=k: e.tensor_add(
                T["soat"][:, k * PQ:(k + 1) * PQ], T["soat"][:, k * PQ:(k + 1) * PQ],
                T["sxo"][:, k * PQ:(k + 1) * PQ]), wait=[gb_ops[k]], force=True))
        d_x2 = []
        for k in range(KD):
            d_x2.append(dma("sync", x2_scr.ap()[k * P:(k + 1) * P, pc:pc + PQ],
                            T["soat"][:, k * PQ:(k + 1) * PQ], wait=[x2_ops[k]],
                            sem="st"))
        xn2_o, st3 = piece_ln("soat", "sxn8", "sxn2b", x2_ops, st2[4], "ln2")
        tt_tr = {0: [], 1: []}
        for tt in range(2):
            pm = [PC, PD][tt]
            trl = []
            for k in range(KD):
                wl = [xn2_o[k], v_idb]
                if k == 0 and tt == 0:
                    wl += z_ops[-2:]
                tr = top(lambda e, k=k, t=tt, p=pm: e.transpose(
                    p[0:P, k * 64:(k + 1) * 64].bitcast(BF16),
                    T["sxn2b"][:, k * PQ + t * 128:k * PQ + t * 128 + 128],
                    T["identb"][:]), wait=wl)
                trl.append(tr)
            ev = vop(lambda e, t=tt, p=pm: e.tensor_copy(
                T["sxn2t"][:, t * 1056:t * 1056 + 1024],
                p[0:P, 0:512].bitcast(BF16)), wait=trl)
            vo1 = vop(lambda e, t=tt: e.memset(T["sxn2t"][:, t * 1056 + 1024:t * 1056 + 1025], 1.0),
                      wait=[ev], force=True)
            vo2 = vop(lambda e, t=tt: e.memset(T["sxn2t"][:, t * 1056 + 1025:t * 1056 + 1056], 0.0),
                      wait=[vo1], force=True)
            tt_tr[tt] = [ev, vo1, vo2]
        h1_ops = []
        for mt in range(4):
            pm = [PC, PD][mt % 2]
            mml = []
            for k in range(KD):
                wl = [xn2_o[k], w1_ops[-1], tt_tr[0][0], tt_tr[1][0]]
                if len(h1_ops) >= 2:
                    wl.append(h1_ops[-2])
                mm = top(lambda e, m=mt, k=k, p=pm: e.matmul(
                    p[0:P, 0:PQ], T["sw1t"][:, k * 512 + m * 128:k * 512 + (m + 1) * 128],
                    T["sxn2b"][:, k * PQ:(k + 1) * PQ],
                    start=(k == 0), stop=(k == KD - 1)), wait=wl)
                mml.append(mm)
            sg = sop(lambda e, m=mt, p=pm: e.activation(
                out=T["sh1b"][:, m * PQ:(m + 1) * PQ], in_=p[0:P, 0:PQ], func=AF.Gelu,
                bias=T["sbt1"][:, m:m + 1]), wait=mml + [v_bt1])
            h1_ops.append(sg)
        rt_mm = []
        for k4 in range(4):
            rt_mm.append(top(lambda e, k=k4: e.matmul(
                PA[0:64, 0:PQ], T["sw2b"][:, k * 64:(k + 1) * 64],
                T["sh1b"][:, k * PQ:(k + 1) * PQ],
                start=(k == 0), stop=(k == 3)),
                wait=[h1_ops[k4], w2_ops[-1], st3[0], st3[1]]))
        v_rt = vop(lambda e: e.tensor_scalar(
            out=T["srt"][:], in0=PA[0:64, 0:PQ], scalar1=T["sitau"][:, 0:1],
            scalar2=T["sitau"][:, 1:2], op0=ALU.mult, op1=ALU.add),
            wait=rt_mm + [v_it2])
        rtt_tr = []
        for tt in range(2):
            tr = top(lambda e, t=tt: e.transpose(
                PB[0:P, t * 64:(t + 1) * 64], T["srt"][0:64, t * 128:(t + 1) * 128],
                T["ident"][0:64, 0:64]), wait=[v_rt, st3[2], st3[3]])
            rtt_tr.append(tr)
        v_rtt = vop(lambda e: e.tensor_copy(T["srtt"][:], PB[0:P, 0:128]), wait=rtt_tr)
        for tt in range(2):
            gt = p * 2 + tt
            vm8 = vop(lambda e, t=tt: e.max(out=T["st8"][:],
                                            in_=T["srtt"][:, t * 64:(t + 1) * 64]),
                      wait=[v_rtt])
            vms = vop(lambda e, t=tt: e.tensor_scalar(
                out=T["sal"][:], in0=T["srtt"][:, t * 64:(t + 1) * 64],
                scalar1=T["st8"][:, 7:8], scalar2=None, op0=ALU.is_ge),
                wait=[vm8], force=True)
            vmn = vop(lambda e: e.tensor_scalar_mul(T["stm1"][:], T["st8"][:, 0:1], -1.0),
                      wait=[vm8], force=True)
            sex = sop(lambda e, t=tt: e.activation(
                out=T["sq2"][:, 0:64], in_=T["srtt"][:, t * 64:(t + 1) * 64],
                func=AF.Exp, bias=T["stm1"][:]), wait=[vmn, vms])
            vem = vop(lambda e: e.tensor_mul(T["sal"][:], T["sal"][:], T["sq2"][:, 0:64]),
                      wait=[sex], force=True)
            vsm = vop(lambda e: e.reduce_sum(T["sals"][:], T["sal"][:], axis=AX.X),
                      wait=[vem], force=True)
            vrc = vop(lambda e: e.reciprocal(T["sals"][:], T["sals"][:]),
                      wait=[vsm], force=True)
            val = vop(lambda e, g=gt: e.tensor_scalar(
                out=T["salfb"][:, g * 64:(g + 1) * 64], in0=T["sal"][:],
                scalar1=T["sals"][:], scalar2=None, op0=ALU.mult),
                wait=[vrc], force=True)
            sm1a = top(lambda e, g=gt, t=tt: e.matmul(
                PC[0:64, 0:512], T["salfb"][:, g * 64:(g + 1) * 64],
                T["sxn2t"][:, t * 1056:t * 1056 + 512], start=True, stop=True),
                wait=[val, tt_tr[tt][2], rtt_tr[0], rtt_tr[1], v_rt])
            sm1 = top(lambda e, g=gt, t=tt: e.matmul(
                PC[0:64, 512:1024], T["salfb"][:, g * 64:(g + 1) * 64],
                T["sxn2t"][:, t * 1056 + 512:t * 1056 + 1024], start=True, stop=True),
                wait=[sm1a])
            sm2 = top(lambda e, g=gt, t=tt: e.matmul(
                PD[0:64, 0:32], T["salfb"][:, g * 64:(g + 1) * 64],
                T["sxn2t"][:, t * 1056 + 1024:t * 1056 + 1056], start=True, stop=True),
                wait=[val])
            va1 = vop(lambda e: e.tensor_add(T["ssl"][:, 0:1024], T["ssl"][:, 0:1024],
                                             PC[0:64, 0:1024]), wait=[sm1, sm1a, v_slz],
                      force=True)
            va2 = vop(lambda e: e.tensor_add(T["ssl"][:, 1024:1056], T["ssl"][:, 1024:1056],
                                             PD[0:64, 0:32]), wait=[sm2, va1], force=True)
            slot_mm_all.append(va2)
        prev_piece = {"ytb": ytb_done, "x2d": d_x2, "lastsq": st3[4],
                      "vrt": v_rt, "slot": slot_mm_all[-1]}

    d_sli = dma("sync", cc_sl_in.ap()[:, 0:1056], T["ssl"][:], wait=[slot_mm_all[-1]],
                sem="st")
    g_sl = S.op("gpsimd", lambda e: e.collective_compute(
        "AllReduce", ALU.add, replica_groups=RG,
        ins=[cc_sl_in.ap()[:, 0:1056]], outs=[cc_sl_out.ap()[:, 0:1056]]),
        inc=("cc", 1), wait=[d_sli])

    p2end = [slot_mm_all[-1]] + prev_piece["x2d"]
    d_sl = dma("sync", T["ssl"][:], cc_sl_out.ap()[:, 0:1056], wait=[g_sl])
    s0_ops = []
    for k in range(KD):
        wl = [g_sl] + (p2end if k == 0 else [])
        s0_ops.append(dma("sync", T["ss0f"][:, k * 64:(k + 1) * 64],
                          E["s0T"].ap()[k * P:(k + 1) * P, :], wait=wl))
    v_s0b = vop(lambda e: e.tensor_copy(T["ss0b"][:], T["ss0f"][:]),
                wait=s0_ops + p2end)
    d_bv = dma("sync", T["sstg"][0:1, 0:D], E["bv_row"].ap(), wait=[s0_ops[-1]])
    d_gg = dma("sync", T["sstg"][0:1, D:2 * D], E["fflng"].ap(), wait=[d_bv])
    d_bb = dma("sync", T["sstg"][0:1, 2 * D:3 * D], E["fflnb"].ap(), wait=[d_gg])
    t_bv = top(lambda e: e.matmul(PA[0:64, 0:512], T["onesr"][0:1, 0:64],
                                  T["sstg"][0:1, 0:512], start=True, stop=True),
               wait=[d_bv] + p2end)
    t_bv2 = top(lambda e: e.matmul(PA[0:64, 512:1024], T["onesr"][0:1, 0:64],
                                   T["sstg"][0:1, 512:1024], start=True, stop=True),
                wait=[d_bv])
    v_bv = vop(lambda e: e.tensor_copy(T["sbvb"][:], PA[0:64, 0:1024]),
               wait=[t_bv, t_bv2] + p2end)
    t_gg = top(lambda e: e.matmul(PB[0:64, 0:512], T["onesr"][0:1, 0:64],
                                  T["sstg"][0:1, D:D + 512], start=True, stop=True),
               wait=[d_gg])
    t_gg2 = top(lambda e: e.matmul(PB[0:64, 512:1024], T["onesr"][0:1, 0:64],
                                   T["sstg"][0:1, D + 512:2 * D], start=True, stop=True),
                wait=[d_gg])
    v_gg = vop(lambda e: e.tensor_copy(T["sgbb"][:, 0:1024], PB[0:64, 0:1024]),
               wait=[t_gg, t_gg2], force=True)
    t_bb = top(lambda e: e.matmul(PC[0:64, 0:512], T["onesr"][0:1, 0:64],
                                  T["sstg"][0:1, 2 * D:2 * D + 512], start=True, stop=True),
               wait=[d_bb])
    t_bb2 = top(lambda e: e.matmul(PC[0:64, 512:1024], T["onesr"][0:1, 0:64],
                                   T["sstg"][0:1, 2 * D + 512:3 * D], start=True, stop=True),
                wait=[d_bb])
    v_bb = vop(lambda e: e.tensor_copy(T["sgbb"][:, 1024:2048], PC[0:64, 0:1024]),
               wait=[t_bb, t_bb2], force=True)
    v_dn = vop(lambda e: e.tensor_scalar(
        out=T["ssc3"][:, 0:1], in0=T["ssl"][:, 1024:1025],
        scalar1=T["fc"][0:64, FC_E8:FC_E8 + 1], scalar2=None, op0=ALU.add),
        wait=[d_sl] + p2end)
    v_dr = vop(lambda e: e.reciprocal(T["ssc3"][:, 0:1], T["ssc3"][:, 0:1]),
               wait=[v_dn], force=True)
    v_si = vop(lambda e: e.tensor_scalar(
        out=T["ssl"][:, 0:1024], in0=T["ssl"][:, 0:1024],
        scalar1=T["ssc3"][:, 0:1], scalar2=None, op0=ALU.mult),
        wait=[v_dr], force=True)
    si_tr, si_ev = [], []
    for k in range(KD):
        tr = top(lambda e, k=k: e.transpose(
            PD[0:P, 0:64], T["ssl"][0:64, k * 128:(k + 1) * 128], T["ident"][0:64, 0:64]),
            wait=[v_si, v_bb] + ([si_ev[-1]] if si_ev else []))
        ev = vop(lambda e, k=k: e.tensor_copy(T["ssiT"][:, k * 64:(k + 1) * 64],
                                              PD[0:P, 0:64]), wait=[tr])
        si_tr.append(tr)
        si_ev.append(ev)
    gate_mm = []
    prevw = si_ev[-1]
    for src, rhs_name, part in [("wihT", "ssiT", 0), ("whhT", "ss0b", 1)]:
        for k in range(KD):
            ldk = dma("sync", T["sstg"][:, 0:3072], E[src].ap()[k * P:(k + 1) * P, :],
                      wait=[prevw, v_bv, v_gg, v_bb])
            ck = vop(lambda e: e.tensor_copy(T["swkb"][:], T["sstg"][:, 0:3072]),
                     wait=[ldk, v_gg] + ([gate_mm[-1]] if gate_mm else []))
            prevw = ck
            for mc in range(24):
                if mc < 16:
                    dst = lambda e, m=mc: PA[0:P, m * 64:(m + 1) * 64]
                    first = (part == 0 and k == 0)
                    last = (part == 1 and k == KD - 1)
                else:
                    dst = lambda e, m=mc, pt=part: PB[0:P, (pt * 8 + m - 16) * 64:(pt * 8 + m - 15) * 64]
                    first = (k == 0)
                    last = (k == KD - 1)
                mm = top(lambda e, m=mc, k=k, r=rhs_name, d=dst, f=first, l=last: e.matmul(
                    d(e), T["swkb"][:, m * 128:(m + 1) * 128],
                    T[r][:, k * 64:(k + 1) * 64], start=f, stop=l),
                    wait=[ck, v_s0b, si_ev[k] if part == 0 else v_s0b, v_bv])
                gate_mm.append(mm)
    gate_r = []
    for mc in range(8):
        gate_r.append(sop(lambda e, m=mc: e.activation(
            out=T["sgr"][:, m * 64:(m + 1) * 64], in_=PA[0:P, m * 64:(m + 1) * 64],
            func=AF.Sigmoid, bias=T["sbihh"][:, m:m + 1]), wait=gate_mm[-24:],
            force=True))
    gate_z = []
    for mc in range(8):
        gate_z.append(sop(lambda e, m=mc: e.activation(
            out=T["sgz"][:, m * 64:(m + 1) * 64], in_=PA[0:P, (8 + m) * 64:(9 + m) * 64],
            func=AF.Sigmoid, bias=T["sbihh"][:, 8 + m:9 + m]), wait=gate_mm[-24:],
            force=True))
    v_hn = []
    for mc in range(8):
        a1 = vop(lambda e, m=mc: e.tensor_scalar(
            out=T["sgt"][:, m * 64:(m + 1) * 64], in0=PB[0:P, (8 + m) * 64:(9 + m) * 64],
            scalar1=T["sbhh"][:, 16 + m:17 + m], scalar2=None, op0=ALU.add),
            wait=gate_mm[-24:] + [SL["sbhh"]])
        a2 = vop(lambda e, m=mc: e.tensor_mul(
            T["sgt"][:, m * 64:(m + 1) * 64], T["sgt"][:, m * 64:(m + 1) * 64],
            T["sgr"][:, m * 64:(m + 1) * 64]), wait=[a1, gate_r[mc]], force=True)
        a3 = vop(lambda e, m=mc: e.tensor_scalar(
            out=T["sgn"][:, m * 64:(m + 1) * 64], in0=PB[0:P, m * 64:(m + 1) * 64],
            scalar1=T["sbihh"][:, 16 + m:17 + m], scalar2=None, op0=ALU.add),
            wait=gate_mm[-24:] + [SL["sbihh"]])
        a4 = vop(lambda e, m=mc: e.tensor_add(
            T["sgn"][:, m * 64:(m + 1) * 64], T["sgn"][:, m * 64:(m + 1) * 64],
            T["sgt"][:, m * 64:(m + 1) * 64]), wait=[a2, a3], force=True)
        v_hn.append(a4)
    s_n = sop(lambda e: e.activation(out=T["sgn"][:], in_=T["sgn"][:], func=AF.Tanh),
              wait=v_hn)
    v_sn1 = vop(lambda e: e.tensor_sub(T["sgt"][:], T["ss0f"][:], T["sgn"][:]),
                wait=[s_n, v_s0b])
    v_sn2 = vop(lambda e: e.tensor_mul(T["sgt"][:], T["sgt"][:], T["sgz"][:]),
                wait=[v_sn1] + gate_z, force=True)
    v_sn3 = vop(lambda e: e.tensor_add(T["ssnew"][:], T["sgn"][:], T["sgt"][:]),
                wait=[v_sn2], force=True)
    v_snb = vop(lambda e: e.tensor_copy(T["ssnb"][:], T["ssnew"][:]), wait=[v_sn3],
                force=True)
    wq_ops = stream_cast2(E["wq"].ap(), "sw3", 8, 256, prev=prevw, coloff=0,
                          guard=p2end)
    wk_ops = stream_cast2(E["wk"].ap(), "sw3", 8, 256, prev=wq_ops[-1], coloff=2048)
    qk_ev = []
    for which, co, bias, outoff in [(0, 0, "sbq", 0), (1, 2048, "sbk", 128)]:
        for m2 in range(2):
            mml = []
            for k in range(KD):
                mm = top(lambda e, k=k, m=m2, c=co: e.matmul(
                    PC[0:P, 0:64], T["sw3"][:, c + k * 256 + m * 128:c + k * 256 + (m + 1) * 128],
                    T["ssnb"][:, k * 64:(k + 1) * 64], start=(k == 0), stop=(k == KD - 1)),
                    wait=[v_snb, (wq_ops if which == 0 else wk_ops)[-1],
                          gate_r[-1], gate_z[-1]] + ([qk_ev[-1]] if qk_ev else []))
                mml.append(mm)
            ev = vop(lambda e, m=m2, b=bias, oo=outoff: e.tensor_scalar(
                out=T["sqkT"][:, (oo // 64 + m) * 64:(oo // 64 + m + 1) * 64],
                in0=PC[0:P, 0:64], scalar1=T[b][:, m:m + 1], scalar2=None,
                op0=ALU.add), wait=mml + [SL[bias]])
            qk_ev.append(ev)
    sc_mm = []
    for k2 in range(2):
        sc_mm.append(top(lambda e, k=k2: e.matmul(
            PD[0:64, 0:64], T["sqkT"][:, k * 64:(k + 1) * 64],
            T["sqkT"][:, (2 + k) * 64:(3 + k) * 64], start=(k == 0), stop=(k == 1)),
            wait=[qk_ev[1], qk_ev[3], si_ev[-1]]))
    v_mx = vop(lambda e: e.reduce_max(T["ssc3"][:, 1:2], PD[0:64, 0:64], axis=AX.X),
               wait=sc_mm)
    v_mxs = vop(lambda e: e.tensor_scalar_mul(T["ssc3"][:, 2:3], T["ssc3"][:, 1:2],
                                              -(256.0 ** -0.5)), wait=[v_mx], force=True)
    s_se = sop(lambda e: e.activation(out=T["satm"][:], in_=PD[0:64, 0:64], func=AF.Exp,
                                      scale=T["fc"][0:64, FC_SC:FC_SC + 1],
                                      bias=T["ssc3"][:, 2:3]), wait=[v_mxs])
    v_ss = vop(lambda e: e.reduce_sum(T["ssc3"][:, 3:4], T["satm"][:], axis=AX.X),
               wait=[s_se])
    v_sr = vop(lambda e: e.reciprocal(T["ssc3"][:, 3:4], T["ssc3"][:, 3:4]),
               wait=[v_ss], force=True)
    v_at = vop(lambda e: e.tensor_scalar(
        out=T["satb"][:], in0=T["satm"][:], scalar1=T["ssc3"][:, 3:4], scalar2=None,
        op0=ALU.mult), wait=[v_sr], force=True)
    t_atT = top(lambda e: e.transpose(PD[0:64, 64:96].bitcast(BF16), T["satb"][:],
                                      T["identb"][0:64, 0:64]), wait=[v_at, v_mx, s_se])
    v_atT = vop(lambda e: e.tensor_copy(T["satT"][:], PD[0:64, 64:96].bitcast(BF16)),
                wait=[t_atT])
    wv_ops = stream_cast2(E["wv"].ap(), "sw3", 8, 1024, prev=wk_ops[-1],
                          guard=[qk_ev[-1]])
    v_mm = []
    for hf in range(2):
        for k in range(KD):
            mm = top(lambda e, k=k, f=hf: e.matmul(
                PA[0:64, f * 512:(f + 1) * 512],
                T["ssnb"][:, k * 64:(k + 1) * 64],
                T["sw3"][:, k * 1024 + f * 512:k * 1024 + (f + 1) * 512],
                start=(k == 0), stop=(k == KD - 1)),
                wait=[wv_ops[k], v_snb, v_bv, gate_r[-1], gate_z[-1]])
        v_mm.append(mm)
    v_vt = vop(lambda e: e.tensor_add(T["svtok"][:], PA[0:64, 0:1024], T["sbvb"][:]),
               wait=v_mm + [v_bv])
    v_vtb = vop(lambda e: e.tensor_copy(T["svtkb"][:], T["svtok"][:]), wait=[v_vt],
                force=True)
    t_scma = top(lambda e: e.matmul(PB[0:64, 0:512], T["satT"][:], T["svtkb"][:, 0:512],
                                    start=True, stop=True),
                 wait=[v_atT, v_vtb, v_gg])
    t_scm = top(lambda e: e.matmul(PB[0:64, 512:1024], T["satT"][:],
                                   T["svtkb"][:, 512:1024], start=True, stop=True),
                wait=[t_scma])
    snT_tr = []
    for k in range(KD):
        tr = top(lambda e, k=k: e.transpose(
            PC[0:64, k * 128:(k + 1) * 128], T["ssnew"][:, k * 64:(k + 1) * 64],
            T["ident"][:]), wait=[v_sn3, v_bb, s_se])
        snT_tr.append(tr)
    v_y3a = vop(lambda e: e.tensor_copy(T["sy3"][:], PB[0:64, 0:1024]),
                wait=snT_tr + [t_scm])
    v_y3 = vop(lambda e: e.tensor_add(T["sy3"][:], T["sy3"][:], PC[0:64, 0:1024]),
               wait=[v_y3a], force=True)
    v_m3 = vop(lambda e: e.reduce_sum(T["ssc3"][:, 4:5], T["sy3"][:], axis=AX.X),
               wait=[v_y3], force=True)
    v_m3b = vop(lambda e: e.tensor_scalar_mul(T["ssc3"][:, 4:5], T["ssc3"][:, 4:5],
                                              1.0 / D), wait=[v_m3], force=True)
    s_q3 = sop(lambda e: e.activation(out=T["sy3q"][:], in_=T["sy3"][:], func=AF.Square),
               wait=[v_y3])
    v_e3 = vop(lambda e: e.reduce_sum(T["ssc3"][:, 5:6], T["sy3q"][:], axis=AX.X),
               wait=[s_q3])
    v_e3b = vop(lambda e: e.tensor_scalar_mul(T["ssc3"][:, 5:6], T["ssc3"][:, 5:6],
                                              1.0 / D), wait=[v_e3], force=True)
    v_v3 = vop(lambda e: e.tensor_mul(T["ssc3"][:, 6:7], T["ssc3"][:, 4:5],
                                      T["ssc3"][:, 4:5]), wait=[v_m3b], force=True)
    v_v3b = vop(lambda e: e.tensor_sub(T["ssc3"][:, 6:7], T["ssc3"][:, 5:6],
                                       T["ssc3"][:, 6:7]), wait=[v_v3, v_e3b], force=True)
    s_l3 = sop(lambda e: e.activation(out=T["ssc3"][:, 6:7], in_=T["ssc3"][:, 6:7],
                                      func=AF.Ln, bias=T["fc"][0:64, FC_EPS:FC_EPS + 1]),
               wait=[v_v3b])
    s_i3 = sop(lambda e: e.activation(out=T["ssc3"][:, 7:8], in_=T["ssc3"][:, 6:7],
                                      func=AF.Exp,
                                      scale=T["fc"][0:64, FC_NHALF:FC_NHALF + 1]),
               wait=[s_l3], force=True)
    v_yn = vop(lambda e: e.tensor_scalar(
        out=T["sy3"][:], in0=T["sy3"][:], scalar1=T["ssc3"][:, 4:5],
        scalar2=T["ssc3"][:, 7:8], op0=ALU.subtract, op1=ALU.mult),
        wait=[s_i3], force=True)
    v_yg = vop(lambda e: e.tensor_mul(T["sy3"][:], T["sy3"][:], T["sgbb"][:, 0:1024]),
               wait=[v_yn, v_gg], force=True)
    v_yb = vop(lambda e: e.tensor_add(T["sy3"][:], T["sy3"][:], T["sgbb"][:, 1024:2048]),
               wait=[v_yg, v_bb], force=True)
    y3T_tr, y3T_ev = [], []
    for k in range(KD):
        tr = top(lambda e, k=k: e.transpose(
            PD[0:P, 0:64], T["sy3"][0:64, k * 128:(k + 1) * 128], T["ident"][0:64, 0:64]),
            wait=[v_yb, v_atT] + ([y3T_ev[-1]] if y3T_ev else []))
        ev = vop(lambda e, k=k: e.tensor_copy(T["sy3Tb"][:, k * 64:(k + 1) * 64],
                                              PD[0:P, 0:64]), wait=[tr])
        y3T_tr.append(tr)
        y3T_ev.append(ev)
    wvp_ops = stream_cast2(E["wvp"].ap(), "sw3", 8, 1024, prev=wv_ops[-1],
                           guard=[v_mm[-1]])
    sv_tok = []
    for mt in range(KD):
        mml = []
        for k in range(KD):
            mm = top(lambda e, k=k, m=mt: e.matmul(
                PA[0:P, 0:64], T["sw3"][:, k * 1024 + m * 128:k * 1024 + (m + 1) * 128],
                T["sy3Tb"][:, k * 64:(k + 1) * 64], start=(k == 0), stop=(k == KD - 1)),
                wait=[wvp_ops[k], y3T_ev[k], v_vt] + ([sv_tok[-1]] if sv_tok else []))
            mml.append(mm)
        ev = vop(lambda e, m=mt: e.tensor_scalar(
            out=T["sgt"][:, 0:64], in0=PA[0:P, 0:64], scalar1=T["sbvp"][:, m:m + 1],
            scalar2=None, op0=ALU.add), wait=mml + [SL["sbvp"]])
        tr = top(lambda e, m=mt: e.transpose(PB[0:64, 0:128], T["sgt"][0:P, 0:64],
                                             T["ident"][:]), wait=[ev, v_y3])
        ev2 = vop(lambda e, m=mt: e.tensor_copy(T["ssvb"][:, m * 128:(m + 1) * 128],
                                                PB[0:64, 0:128]), wait=[tr])
        sv_tok.append(ev2)
    al_tr = []
    for gt in range(16):
        tr = top(lambda e, g=gt: e.transpose(
            PC[0:64, 0:64].bitcast(BF16), T["salfb"][:, g * 64:(g + 1) * 64],
            T["identb"][:]),
            wait=[snT_tr[-1], sv_tok[-1] if gt == 0 else al_tr[-1], v_yb])
        ev = vop(lambda e, g=gt: e.tensor_copy(T["salfT"][:, g * 128:(g + 1) * 128],
                                               PC[0:64, 0:64].bitcast(BF16)), wait=[tr])
        al_tr.append(ev)
    wo_ops = stream_cast2(E["wo"].ap(), "sw3", 8, 1024, prev=wvp_ops[-1],
                          guard=[sv_tok[-1]])
    prev_fin = None
    prev_outd = None
    prev_outev = None
    for p in range(NPC):
        pc = p * PQ
        d_x2p = []
        for k in range(KD):
            wl = [wo_ops[-1]]
            if prev_fin is not None:
                wl.append(prev_fin)
            d_x2p.append(dma("sync", T["sx2p"][:, k * PQ:(k + 1) * PQ],
                             x2_scr.ap()[k * P:(k + 1) * P, pc:pc + PQ], wait=wl))
        sc_ev = []
        for dc in range(KD):
            pm = [PA, PB][dc % 2]
            mml = []
            for tt in range(2):
                gt = p * 2 + tt
                mm = top(lambda e, d=dc, g=gt, t=tt, pmm=pm: e.matmul(
                    pmm[0:P, t * 128:(t + 1) * 128],
                    T["ssvb"][0:64, d * 128:(d + 1) * 128],
                    T["salfT"][0:64, g * 128:(g + 1) * 128], start=True, stop=True),
                    wait=[sv_tok[dc], al_tr[gt], v_yb] +
                         ([sc_ev[-2]] if len(sc_ev) >= 2 else []) +
                         ([prev_fin, prev_outev] if (p > 0 and dc == 0) else []))
                mml.append(mm)
            ev = vop(lambda e, d=dc, pmm=pm: e.tensor_copy(
                T["soutb"][:, d * PQ:(d + 1) * PQ], pmm[0:P, 0:PQ]), wait=mml)
            sc_ev.append(ev)
        fin_ops = []
        for mt in range(KD):
            pm = [PC, PD][mt % 2]
            mml = []
            for k in range(KD):
                mm = top(lambda e, k=k, m=mt, pmm=pm: e.matmul(
                    pmm[0:P, 0:PQ], T["sw3"][:, k * 1024 + m * 128:k * 1024 + (m + 1) * 128],
                    T["soutb"][:, k * PQ:(k + 1) * PQ], start=(k == 0), stop=(k == KD - 1)),
                    wait=[wo_ops[k], sc_ev[k]] +
                         ([fin_ops[-2]] if len(fin_ops) >= 2 else []) +
                         ([al_tr[-1]] if (p == 0 and mt == 0) else []))
                mml.append(mm)
            f1 = vop(lambda e, m=mt, pmm=pm: e.tensor_add(
                T["sfin"][:, m * PQ:(m + 1) * PQ], pmm[0:P, 0:PQ],
                T["sx2p"][:, m * PQ:(m + 1) * PQ]), wait=mml + [d_x2p[mt]])
            f2 = vop(lambda e, m=mt: e.tensor_scalar(
                out=T["sfin"][:, m * PQ:(m + 1) * PQ],
                in0=T["sfin"][:, m * PQ:(m + 1) * PQ],
                scalar1=T["sbo"][:, m:m + 1], scalar2=None, op0=ALU.add),
                wait=[f1, SL["sbo"]], force=True)
            fin_ops.append(f2)
        outd = []
        for tt in range(2):
            pm = [PA, PB][tt]
            trl = []
            for k in range(KD):
                tr = top(lambda e, k=k, t=tt, pmm=pm: e.transpose(
                    pmm[0:P, k * 128:(k + 1) * 128],
                    T["sfin"][:, k * PQ + t * 128:k * PQ + t * 128 + 128],
                    T["ident"][:]), wait=[fin_ops[k]] +
                    ([sc_ev[(k + 1) % KD]] if True else []))
                trl.append(tr)
            ev = vop(lambda e, pmm=pm: e.tensor_copy(T["sfint"][:], pmm[0:P, 0:1024]),
                     wait=trl + ([prev_outd] if prev_outd else []), force=True)
            od = dma("sync", out_h.ap()[pc + tt * 128:pc + (tt + 1) * 128, :],
                     T["sfint"][:], wait=[ev], sem="st")
            outd.append(od)
            prev_outd = od
            prev_outev = ev
        prev_fin = fin_ops[-1]

    if KDBG:
        dd1 = dma("sync", dbg_ar.ap(), cc_ar_out.ap(), wait=[prev_outd], sem="st")
        dd2 = dma("sync", dbg_ag.ap(), cc_ag_out.ap(), wait=[dd1], sem="st")
        dd3 = dma("sync", dbg_x2.ap(), x2_scr.ap(), wait=[dd2], sem="st")
        dd4 = dma("sync", dbg_sl.ap(), cc_sl_out.ap(), wait=[dd3], sem="st")
        prev_outd = dd4
    S.op("gpsimd", lambda e: e.engine_nop(), inc=("gA", 1),
         wait=[prev_outd] + list(S.last_pool_op.values()))



    with nc.Block() as block:
        S.emit(block)
    for g in nc._sem_guards:
        g.__exit__(None, None, None)
    return nc


def _fix_sda_chain(S):
    va = 0
    for o in S.items:
        if o.eng == "scalar":
            o.deps.append(("vA", va, True))
        if o.sem == "vA":
            va = o.count_after


def _stage_core(I, core):
    b, g = core // 4, core % 4
    st = {}
    x = np.ascontiguousarray(I["L0_attn_x"][b].T)
    st["xT"] = x.astype(np.float32)
    WT = np.ascontiguousarray(I["L0_attn_in_proj_w"].T)
    st["wxT"] = np.ascontiguousarray(WT[:, g * DSL:(g + 1) * DSL])
    st["wzT"] = np.ascontiguousarray(WT[:, DI:2 * DI])
    dsl = slice(g * DSL, (g + 1) * DSL)
    cw = I["L0_attn_conv_w"][dsl, 0, :]
    st["cw_st"] = np.ascontiguousarray(cw.reshape(4, P, 4).transpose(1, 0, 2).reshape(P, 16))
    st["cb_st"] = np.ascontiguousarray(I["L0_attn_conv_b"][dsl].reshape(4, P).T)
    xp = I["L0_attn_x_proj_w"][:, dsl]
    st["xpT_st"] = np.ascontiguousarray(
        xp.T.reshape(4, P, 33).transpose(1, 0, 2).reshape(P, 132))
    st["dtw_st"] = np.ascontiguousarray(I["L0_attn_dt_proj_w"][dsl, 0].reshape(4, P).T)
    st["dtb_st"] = np.ascontiguousarray(I["L0_attn_dt_proj_b"][dsl].reshape(4, P).T)
    al = I["L0_attn_A_log"][dsl]
    st["Alog_st"] = np.ascontiguousarray(
        al.reshape(4, P, 16).transpose(1, 0, 2).reshape(P, 64))
    st["D_st"] = np.ascontiguousarray(I["L0_attn_D"][dsl].reshape(4, P).T)
    st["opT"] = np.ascontiguousarray(I["L0_attn_out_proj_w"].T)
    st["lng_st"] = np.ascontiguousarray(I["L0_attn_ln_gamma"].reshape(KD, P).T)
    st["lnb_st"] = np.ascontiguousarray(I["L0_attn_ln_beta"].reshape(KD, P).T)
    st["w1t"] = np.ascontiguousarray(I["L0_ffn_w1"][0:D])
    st["w1b"] = np.ascontiguousarray(I["L0_ffn_w1"][D:2 * D])
    st["b1_st"] = np.ascontiguousarray(I["L0_ffn_b1"].reshape(4, P).T)
    st["w2"] = np.ascontiguousarray(I["L0_ffn_w2"])
    st["b2_st"] = np.ascontiguousarray(I["L0_ffn_b2"].reshape(64, 1))
    st["tau"] = np.ascontiguousarray(I["L0_ffn_tau"].reshape(1, 1))
    st["wihT"] = np.ascontiguousarray(I["L0_ffn_Wih"].T)
    st["whhT"] = np.ascontiguousarray(I["L0_ffn_Whh"].T)
    st["bih_st"] = np.ascontiguousarray(I["L0_ffn_bih"].reshape(24, P).T)
    st["bhh_st"] = np.ascontiguousarray(I["L0_ffn_bhh"].reshape(24, P).T)
    st["wq"] = np.ascontiguousarray(I["L0_ffn_wq"])
    st["bq_st"] = np.ascontiguousarray(I["L0_ffn_bq"].reshape(2, P).T)
    st["wk"] = np.ascontiguousarray(I["L0_ffn_wk"])
    st["bk_st"] = np.ascontiguousarray(I["L0_ffn_bk"].reshape(2, P).T)
    st["wv"] = np.ascontiguousarray(I["L0_ffn_wv"])
    st["bv_row"] = np.ascontiguousarray(I["L0_ffn_bv"].reshape(1, D))
    st["fflng"] = np.ascontiguousarray(I["L0_ffn_ln_g"].reshape(1, D))
    st["fflnb"] = np.ascontiguousarray(I["L0_ffn_ln_b"].reshape(1, D))
    st["wvp"] = np.ascontiguousarray(I["L0_ffn_wvp"])
    st["bvp_st"] = np.ascontiguousarray(I["L0_ffn_bvp"].reshape(8, P).T)
    st["wo"] = np.ascontiguousarray(I["L0_ffn_wo"])
    st["bo_st"] = np.ascontiguousarray(I["L0_ffn_bo"].reshape(8, P).T)
    st["s0T"] = np.ascontiguousarray(I["L0_ffn_slot_init"].T)
    st["xTo"] = np.ascontiguousarray(st["xT"][:, g * TSL:(g + 1) * TSL])
    out = {k: np.asarray(v, dtype=np.float32) for k, v in st.items()}
    out["sown"] = np.array([[2 * g]], dtype=np.int32)
    return out


def kernel(**inputs):
    nc = build()
    in_maps = [_stage_core(inputs, c) for c in range(8)]
    res = run_bass_kernel_spmd(nc, in_maps, core_ids=list(range(8)))
    out = np.empty((B, L, D), np.float32)
    for c in range(8):
        b, g = c // 4, c % 4
        out[b, g * TSL:(g + 1) * TSL, :] = res.results[c]["out"]
    return out
